# version 17
# baseline (speedup 1.0000x reference)
import math
from contextlib import ExitStack

import numpy as np
import ml_dtypes

import concourse.bass as bass
import concourse.mybir as mybir
from concourse.bass_utils import run_bass_kernel_spmd

F32 = mybir.dt.float32
BF16 = mybir.dt.bfloat16
I32 = mybir.dt.int32
U8 = mybir.dt.uint8
AF = mybir.ActivationFunctionType
ALU = mybir.AluOpType
AX = mybir.AxisListType

P = 128
TOK = 2048
D = 2048
KC = D // P
H = 16
CH = TOK // P
HID = 5632
HOC = HID // P
EPS_RMS = float(np.finfo(np.float32).eps)
EPS_R1 = 1e-6
EPS_LN = 1e-5
DINV = 1.0 / math.sqrt(P)

ENGINES = ("sync", "scalar", "vector", "tensor", "gpsimd")


class Sem:
    def __init__(self, nc, es, name):
        self.h = es.enter_context(nc.semaphore(name))
        self.n = 0

    def bump(self, k=1):
        self.n += k
        return self.n


class Prog:
    def __init__(self):
        self.streams = {e: [] for e in ENGINES}

    def on(self, eng, fn):
        self.streams[eng].append(fn)

    def wait(self, eng, sem, n=None):
        n = sem.n if n is None else n
        if n <= 0:
            return
        self.on(eng, lambda E, h=sem.h, n=n: E.wait_ge(h, n))


def build_program(debug=False, phases=3, p1_heads=16, p1_relay=2):
    nc = bass.Bass("TRN2", target_bir_lowering=False, debug=False)

    def din(name, shape, dtype):
        return nc.dram_tensor(name, shape, dtype, kind="ExternalInput").ap()

    xT = din("xT", [D, TOK], F32)
    wqT = din("wqT", [D, 3 * D], F32)
    wgpre = din("wgpre", [HID // 8, D], I32)
    wupre = din("wupre", [HID // 8, D], I32)
    wdpre = din("wdpre", [D // 8, HID], I32)
    wq_scan = din("wq_scan", [D, 768], F32)
    wp_scan = din("wp_scan", [D, 256], F32)
    wqn_pc = din("wqn_pc", [P, KC], F32)
    w1_pc = din("w1_pc", [P, KC], F32)
    w2_pc = din("w2_pc", [P, KC], F32)
    sdown_pc = din("sdown_pc", [P, KC], F32)
    sgate_pc = din("sgate_pc", [P, HOC], F32)
    sup_pc = din("sup_pc", [P, HOC], F32)
    s_tiled = din("s_tiled", [P, H], F32)
    ident = din("ident", [P, P], BF16)
    diffT = din("diffT", [P, P], BF16)
    causal = din("causal", [P, P], BF16)
    iota_row = din("iota_row", [1, P], BF16)
    irev_col = din("irev_col", [P, 1], F32)
    GIc = din("GIc", [P, H], F32)
    J128 = din("J128", [P, H], BF16)
    ones_row_f = din("ones_row_f", [1, P], F32)
    neg128_row_f = din("neg128_row_f", [1, P], F32)
    ones_col_bf = din("ones_col_bf", [P, 1], BF16)
    ones_col_f = din("ones_col_f", [P, 1], F32)
    corr_gate = din("corr_gate", [P, 1], F32)
    onehot = din("onehot", [P, 8], F32)
    cvals = din("cvals", [P, 8], F32)

    yout = nc.dram_tensor("y", [D, TOK], F32, kind="ExternalOutput").ap()

    x1h2 = nc.dram_tensor("x1h2", [P, KC * 1024], BF16, kind="Internal").ap()
    wpbf_loc = nc.dram_tensor("wpbf_loc", [2 * P, D], BF16, kind="Internal").ap()
    wpbf = nc.dram_tensor("wpbf", [D, D], BF16, kind="Internal",
                          addr_space="Shared").ap()
    wg8l = nc.dram_tensor("wg8l", [HID // 8, D], U8, kind="Internal").ap()
    wu8l = nc.dram_tensor("wu8l", [HID // 8, D], U8, kind="Internal").ap()
    wd8l = nc.dram_tensor("wd8l", [D // 8, HID], U8, kind="Internal").ap()
    wg8 = nc.dram_tensor("wg8", [HID, D], U8, kind="Internal",
                         addr_space="Shared").ap()
    wu8 = nc.dram_tensor("wu8", [HID, D], U8, kind="Internal",
                         addr_space="Shared").ap()
    wd8 = nc.dram_tensor("wd8", [D, HID], U8, kind="Internal",
                         addr_space="Shared").ap()
    cc_in_a = nc.dram_tensor("cc_in_a", [1, 2], F32, kind="Internal")
    cc_out_a = nc.dram_tensor("cc_out_a", [1, 2], F32, kind="Internal",
                              addr_space="Shared")
    cc_in_h = [nc.dram_tensor(f"cc_in_h{h}", [P, P], BF16, kind="Internal")
               for h in range(H)]
    cc_out_h = [nc.dram_tensor(f"cc_out_h{h}", [8 * P, P], BF16, kind="Internal",
                               addr_space="Shared") for h in range(H)]
    dbg = {}
    if debug:
        dbg["attn"] = nc.dram_tensor("dbg_attn", [P, KC * TOK], BF16,
                                     kind="ExternalOutput").ap()
        dbg["x1a"] = nc.dram_tensor("dbg_x1a", [P, 20480], BF16,
                                    kind="ExternalOutput").ap()
        dbg["x1b"] = nc.dram_tensor("dbg_x1b", [P, 12288], BF16,
                                    kind="ExternalOutput").ap()
        dbg["h"] = nc.dram_tensor("dbg_h", [P, KC * TOK], BF16,
                                  kind="ExternalOutput").ap()
        dbg["misc"] = nc.dram_tensor("dbg_misc", [P, 12], F32,
                                     kind="ExternalOutput").ap()
        dbg["rho"] = nc.dram_tensor("dbg_rho", [P, 2048], F32,
                                    kind="ExternalOutput").ap()
        dbg["wp"] = nc.dram_tensor("dbg_wp", [P, 512], BF16,
                                   kind="ExternalOutput").ap()
        dbg["scan"] = nc.dram_tensor("dbg_scan", [P, 32], F32,
                                     kind="ExternalOutput").ap()

    es = ExitStack()
    with es:
        E = es.enter_context

        R1 = E(nc.sbuf_tensor("R1", [P, 44032], BF16))
        R2 = E(nc.sbuf_tensor("R2", [P, 32768], BF16))
        R3 = E(nc.sbuf_tensor("R3", [P, 22528], BF16))

        c_ident = E(nc.sbuf_tensor("c_ident", [P, P], BF16))
        c_diffT = E(nc.sbuf_tensor("c_diffT", [P, P], BF16))
        c_causal = E(nc.sbuf_tensor("c_causal", [P, P], BF16))
        c_iota_row = E(nc.sbuf_tensor("c_iota_row", [1, P], BF16))
        c_irev = E(nc.sbuf_tensor("c_irev", [P, 1], F32))
        c_GI = E(nc.sbuf_tensor("c_GI", [P, H], F32))
        c_J128 = E(nc.sbuf_tensor("c_J128", [P, H], BF16))
        c_ones_row = E(nc.sbuf_tensor("c_ones_row", [1, P], F32))
        c_neg128_row = E(nc.sbuf_tensor("c_neg128_row", [1, P], F32))
        c_ones_col_bf = E(nc.sbuf_tensor("c_ones_col_bf", [P, 1], BF16))
        c_ones_col_f = E(nc.sbuf_tensor("c_ones_col_f", [P, 1], F32))
        c_gate = E(nc.sbuf_tensor("c_gate", [P, 1], F32))
        c_onehot = E(nc.sbuf_tensor("c_onehot", [P, 8], F32))
        c_cv = E(nc.sbuf_tensor("c_cv", [P, 8], F32))
        c_s = E(nc.sbuf_tensor("c_s", [P, H], F32))
        c_negs = E(nc.sbuf_tensor("c_negs", [P, H], F32))
        c_lamT = E(nc.sbuf_tensor("c_lamT", [P, H], F32))
        v_wqn = E(nc.sbuf_tensor("v_wqn", [P, KC], F32))
        v_w1 = E(nc.sbuf_tensor("v_w1", [P, KC], F32))
        v_w2 = E(nc.sbuf_tensor("v_w2", [P, KC], F32))
        v_w12 = E(nc.sbuf_tensor("v_w12", [P, KC], F32))
        v_sdown = E(nc.sbuf_tensor("v_sdown", [P, KC], F32))
        v_sgate = E(nc.sbuf_tensor("v_sgate", [P, HOC], F32))
        v_sup = E(nc.sbuf_tensor("v_sup", [P, HOC], F32))
        v_stat = E(nc.sbuf_tensor("v_stat", [P, KC], F32))
        v_tmp = E(nc.sbuf_tensor("v_tmp", [P, KC], F32))
        v_shl = E(nc.sbuf_tensor("v_shl", [P, 2 * KC], BF16))
        alpha = E(nc.sbuf_tensor("alpha", [P, 8], F32))
        rowA = E(nc.sbuf_tensor("rowA", [1, 1024], F32))
        rowB = E(nc.sbuf_tensor("rowB", [1, 1024], F32))
        arow = E(nc.sbuf_tensor("arow", [1, 32], F32))
        acc2 = E(nc.sbuf_tensor("acc2", [P, 2], F32))

        pb = [E(nc.psum_tensor(f"pb{i}", [P, 512], F32)) for i in range(6)]
        ptA = E(nc.psum_tensor("ptA", [P, 1024], BF16))
        ptB = E(nc.psum_tensor("ptB", [P, 1024], BF16))

        prog = Prog()
        sems = {}

        def S(name):
            if name not in sems:
                sems[name] = Sem(nc, es, name)
            return sems[name]

        def r1f(lo, n):
            return R1[:, lo:lo + 2 * n].bitcast(F32)

        def r2f(lo, n):
            return R2[:, lo:lo + 2 * n].bitcast(F32)

        def r3f(lo, n):
            return R3[:, lo:lo + 2 * n].bitcast(F32)

        def r3i(lo, n):
            return R3[:, lo:lo + 2 * n].bitcast(I32)

        h_fm = lambda kc: R1[:, kc * TOK:(kc + 1) * TOK]
        xstg = lambda p: r1f(32768 + p * 4096, 2048)
        q_fm = lambda p: R1[:, 32768 + p * 2048: 32768 + (p + 1) * 2048]
        qt_fm = R1[:, 36864:38912]
        k_fm = R1[:, 38912:40960]
        sbf = lambda p: R1[:, 40960 + p * 128: 40960 + (p + 1) * 128]
        s_stage = R1[:, 41216:41344]
        scT = lambda p: R1[:, 41344 + p * 128: 41344 + (p + 1) * 128]
        mask_t = lambda p: R1[:, 41600 + p * 128: 41600 + (p + 1) * 128]
        lam_t = lambda p: R1[:, 41856 + p * 128: 41856 + (p + 1) * 128]
        ktm = lambda p: R1[:, 42112 + p * 128: 42112 + (p + 1) * 128]
        ssel_bf = R1[:, 42368:42496]
        lam2 = lambda p: r1f(42496 + p * 32, 16)
        lamTp = lambda p: r1f(42624 + p * 32, 16)
        lrev_col = lambda p: r1f(42752 + p * 8, 1)
        wpsl = lambda p: R1[:, p * 2048:(p + 1) * 2048]
        p2_r1bf = lambda p: R1[:, 4096 + p * 512: 4096 + (p + 1) * 512]
        p2_r1sq = lambda p: R1[:, 5120 + p * 512: 5120 + (p + 1) * 512]
        p2_sq = lambda p: R1[:, 6144 + p * 512: 6144 + (p + 1) * 512]
        p2_mubc = r1f(7168, 512)
        p2_sgbc = r1f(8192, 512)
        p2_xstg = lambda p: r1f(9216 + p * 1024, 512)
        r1_swp = lambda doc: r1f(11264 + doc * 1024, 512)
        rho12_bc = r1f(27648, 2048)
        gu_sl = lambda hc: R1[:, hc * 512:(hc + 1) * 512]
        p3_rows = r1f(22528, 2048)

        xbf = lambda kc: R2[:, kc * TOK:(kc + 1) * TOK]
        attn = lambda ec: R2[:, ec * TOK:(ec + 1) * TOK]
        r2_swp = lambda doc: r2f(doc * 1024, 512)
        wg_sl = lambda p: R2[:, 16384 + p * 2048: 16384 + (p + 1) * 2048]
        wu_sl = lambda p: R2[:, 20480 + p * 2048: 20480 + (p + 1) * 2048]
        out_sl = lambda p: r2f(24576 + p * 1024, 512)
        gtmp = lambda p: R2[:, 26624 + p * 512: 26624 + (p + 1) * 512]
        utmp = lambda p: R2[:, 27648 + p * 512: 27648 + (p + 1) * 512]
        p3_m2bc = r2f(28672, 512)
        p3_s2bc = r2f(29696, 512)
        p3_m1bc = r2f(30720, 512)
        p3_s1bc = r2f(31744, 512)

        p0stg = lambda p: r3f(p * 2048, 1024)
        rho_bc = r3f(4096, 2048)
        xsq = R3[:, 20480:22528]
        wq_sl = lambda p: R3[:, p * 4096: p * 4096 + 2048]
        wk_sl = lambda p: R3[:, p * 4096 + 2048: p * 4096 + 4096]
        wv_sl = R3[:, 8192:12288]
        wstg = lambda p: r3f(12288 + p * 1024, 512)
        wi32 = r3i(14336, 512)
        sall = R3[:, 15360:16384]
        v_tm = R3[:, 16384:20480]

        def x1_sl(fc, half, t0, n):
            if fc < 10:
                base = half * 10240 + fc * 1024 + t0
                return R3[:, base:base + n]
            base = 31744 + half * 6144 + (fc - 10) * 1024 + t0
            return R1[:, base:base + n]

        def x1_cur(fc, t0, n):
            if fc < 10:
                return R3[:, fc * 1024 + t0: fc * 1024 + t0 + n]
            base = 31744 + (fc - 10) * 1024 + t0
            return R1[:, base:base + n]

        wd_sl = lambda p: R3[:, 10240 + p * 5632: 10240 + (p + 1) * 5632]

        promises = {}

        def promise(name, sem):
            promises[name] = (sem, sem.n)

        def pwait(eng, name):
            if name in promises:
                sem, n = promises[name]
                prog.wait(eng, sem, n)

        def te_bank(i):
            pwait("tensor", f"bank{i}")

        def te_pt(i):
            pwait("tensor", f"pt{i}")

        Lc = S("Lc")
        for t, d in [
            (c_ident, ident), (c_diffT, diffT), (c_causal, causal),
            (c_iota_row, iota_row), (c_irev, irev_col), (c_GI, GIc),
            (c_J128, J128), (c_ones_row, ones_row_f), (c_neg128_row, neg128_row_f),
            (c_ones_col_bf, ones_col_bf), (c_ones_col_f, ones_col_f),
            (c_gate, corr_gate), (c_onehot, onehot), (c_cv, cvals), (c_s, s_tiled),
            (v_wqn, wqn_pc), (v_w1, w1_pc), (v_w2, w2_pc), (v_sdown, sdown_pc),
            (v_sgate, sgate_pc), (v_sup, sup_pc),
        ]:
            Lc.bump(16)
            prog.on("scalar", lambda E, t=t, d=d:
                    E.dma_start(out=t[:], in_=d).then_inc(Lc.h, 16))
        LC_ALL = Lc.n

        sPrep = S("sPrep")
        prog.wait("vector", Lc, LC_ALL)

        def _prep(V):
            V.tensor_mul(v_stat[:], v_w1[:], v_w1[:])
            V.tensor_scalar_mul(c_negs[:], c_s[:], -1.0)
            V.tensor_mul(v_w12[:], v_w1[:], v_w2[:])
            V.tensor_scalar(v_stat[:], v_stat[:], 1.0 / 2048.0, EPS_RMS / 2048.0,
                            ALU.mult, ALU.add)
            V.memset(alpha[:, 7:8], 0.0)
            V.memset(alpha[:, 7:8], 0.0)
            V.tensor_copy(v_shl[:, 0:KC], v_stat[:])
            V.memset(alpha[:, 7:8], 0.0)
            V.memset(alpha[:, 7:8], 0.0)
            V.tensor_copy(v_tmp[:], v_shl[:, 0:KC])
            V.memset(alpha[:, 7:8], 0.0)
            V.memset(alpha[:, 7:8], 0.0)
            V.tensor_sub(v_tmp[:], v_stat[:], v_tmp[:])
            V.memset(alpha[:, 7:8], 0.0)
            V.memset(alpha[:, 7:8], 0.0)
            V.tensor_copy(v_shl[:, KC:2 * KC], v_tmp[:])
            V.memset(alpha[:, 7:8], 0.0)
            V.memset(alpha[:, 7:8], 0.0).then_inc(sPrep.h, 1)
        prog.on("vector", _prep)
        sPrep.bump(1)
        prog.wait("scalar", Lc, LC_ALL)
        prog.on("scalar", lambda Sc: Sc.activation(out=c_lamT[:], in_=c_s[:],
                                                   func=AF.Exp, scale=c_cv[:, 0:1])
                .then_inc(sPrep.h, 1))
        sPrep.bump(1)
        PREP_ALL = sPrep.n

        Lsc = S("Lsc")
        sAbs = S("sAbs")
        prog.wait("vector", Lc, LC_ALL)
        prog.on("vector", lambda V: V.memset(acc2[:], 0.0))
        for i in range(2 * KC):
            is_p = i >= KC
            kc = i % KC
            p = i % 2
            col = 1 if is_p else 0
            src = wp_scan if is_p else wq_scan
            w = 256 if is_p else 768
            if i >= 2:
                prog.wait("gpsimd", sAbs, i - 1)
            tgt = Lsc.bump(16)
            prog.on("gpsimd", lambda G, kc=kc, p=p, src=src, w=w:
                    G.dma_start(out=p0stg(p)[:, 0:w], in_=src[kc * P:(kc + 1) * P, :])
                    .then_inc(Lsc.h, 16))
            prog.wait("vector", Lsc, tgt)

            def _red(V, i=i, p=p, w=w):
                V.reduce_sum(r1f(2 * i, 1), p0stg(p)[:, 0:w], axis=AX.X,
                             apply_absolute_value=True).then_inc(sAbs.h, 1)
            prog.on("vector", _red)
            sAbs.bump(1)
            if i >= 1:
                colp = 1 if (i - 1) >= KC else 0

                def _acc(V, j=i - 1, colp=colp):
                    V.memset(alpha[:, 7:8], 0.0)
                    V.tensor_add(acc2[:, colp:colp + 1], acc2[:, colp:colp + 1],
                                 r1f(2 * j, 1))
                prog.on("vector", _acc)
        ilast = 2 * KC - 1

        def _accf(V, j=ilast):
            V.memset(xsq[:, 0:1024], 0.0)
            V.memset(xsq[:, 1024:2048], 0.0)
            V.tensor_add(acc2[:, 1:2], acc2[:, 1:2], r1f(2 * j, 1))
            V.memset(xsq[:, 0:1024], 0.0)
            V.memset(xsq[:, 1024:2048], 0.0).then_inc(sAbs.h, 1)
        prog.on("vector", _accf)
        sAbs.bump(1)
        ABS_ALL = sAbs.n
        if debug:
            sDbg0 = S("sDbg")
            prog.wait("sync", sAbs, ABS_ALL)
            prog.on("sync", lambda E: E.dma_start(out=dbg["scan"], in_=r1f(0, 32))
                    .then_inc(sDbg0.h, 16))
            sDbg0.bump(16)
            prog.wait("vector", sDbg0, sDbg0.n)

        sMMa = S("sMMa")
        sStgA = S("sStgA")
        te_bank(5)
        prog.wait("tensor", sAbs, ABS_ALL)
        prog.wait("tensor", Lc, LC_ALL)

        def _psum_a(T):
            T.matmul(pb[5][0:1, 0:1], acc2[:, 0:1], c_ones_col_f[:],
                     start=True, stop=True)
            T.matmul(pb[5][0:1, 1:2], acc2[:, 1:2], c_ones_col_f[:],
                     start=True, stop=True).then_inc(sMMa.h, 1)
        prog.on("tensor", _psum_a)
        sMMa.bump(1)
        sCpA = S("sCpA")
        prog.wait("vector", sMMa, 1)

        def _cpA(V):
            V.tensor_copy(arow[0:1, 8:10], pb[5][0:1, 0:2])
            V.memset(arow[0:1, 30:31], 0.0)
            V.memset(arow[0:1, 30:31], 0.0).then_inc(sCpA.h, 1)
        prog.on("vector", _cpA)
        sCpA.bump(1)
        promise("bank5", sCpA)
        prog.wait("gpsimd", sCpA, 1)
        prog.wait("gpsimd", S("sXC"), 2)
        prog.on("gpsimd", lambda G: G.dma_start(out=cc_in_a.ap(),
                                                in_=arow[0:1, 8:10])
                .then_inc(sStgA.h, 16))
        sStgA.bump(16)
        ccA = S("ccA")
        prog.wait("gpsimd", sStgA, 16)
        prog.on("gpsimd", lambda G: G.collective_compute(
            "AllReduce", ALU.add, replica_groups=[list(range(8))],
            ins=[cc_in_a.ap().opt()], outs=[cc_out_a.ap().opt()]).then_inc(ccA.h, 1))
        ccA.bump(1)

        Lx = S("Lx")
        sXC = S("sXC")
        sXMM = S("sXMM")
        for kc in range(KC):
            p = kc % 2
            if kc >= 2:
                prog.wait("sync", sXC, (kc - 1) * 2)
                prog.wait("sync", sXMM, kc - 1)
            tgt = Lx.bump(16)
            prog.on("sync", lambda E, kc=kc, p=p:
                    E.dma_start(out=xstg(p), in_=xT[kc * P:(kc + 1) * P, :])
                    .then_inc(Lx.h, 16))
            prog.wait("vector", Lx, tgt)
            if kc >= 1:
                prog.wait("vector", sXMM, sXMM.n)
            prog.on("vector", lambda V, p=p:
                    V.tensor_mul(xsq, xstg(p), xstg(p)).then_inc(sXC.h, 1))
            sXC.bump(1)
            prog.wait("scalar", Lx, tgt)
            if kc == 0:
                prog.wait("scalar", Lc, LC_ALL)
            prog.on("scalar", lambda Sc, kc=kc, p=p:
                    Sc.activation(out=xbf(kc), in_=xstg(p), func=AF.Copy)
                    .then_inc(sXC.h, 1))
            sXC.bump(1)
            if kc == 0:
                for sw in range(4):
                    te_bank(sw)
                prog.wait("tensor", Lc, LC_ALL)
            prog.wait("tensor", sXC, sXC.n - 1)

            def _xmm(T, kc=kc):
                for sw in range(4):
                    mm = T.matmul(pb[sw][0:1, :], c_ones_col_bf[:],
                                  xsq[:, sw * 512:(sw + 1) * 512],
                                  start=(kc == 0), stop=(kc == KC - 1))
                    if sw == 3:
                        mm.then_inc(sXMM.h, 1)
            prog.on("tensor", _xmm)
            sXMM.bump(1)
        XMM_ALL = sXMM.n

        sRho = S("sRho")
        sRbc = S("sRbc")
        sRev = S("sRev")
        prog.wait("scalar", sXMM, XMM_ALL)
        for batch in range(2):
            if batch == 1:
                prog.wait("scalar", sRbc, 2)

            def _rho_rows(Sc, batch=batch):
                for j in range(2):
                    sw = batch * 2 + j
                    ins = Sc.activation(out=rowA[0:1, j * 512:(j + 1) * 512],
                                        in_=pb[sw][0:1, :], func=AF.Sqrt,
                                        scale=c_cv[0:1, 1:2], bias=c_cv[0:1, 2:3])
                    if j == 0:
                        ins.then_inc(sRho.h, 1)
                Sc.copy(arow[0:1, 24:25], c_cv[0:1, 0:1])
                Sc.copy(arow[0:1, 24:25], c_cv[0:1, 0:1]).then_inc(sRho.h, 1)
            prog.on("scalar", _rho_rows)
            sRho.bump(2)
            promise(f"bank{batch * 2}", sRho)
            promise(f"bank{batch * 2 + 1}", sRho)
            prog.wait("vector", sRho, sRho.n)

            def _rho_rec(V):
                V.reciprocal(rowA[0:1, 0:1024], rowA[0:1, 0:1024])
                V.memset(alpha[:, 7:8], 0.0)
                V.memset(alpha[:, 7:8], 0.0).then_inc(sRho.h, 1)
            prog.on("vector", _rho_rec)
            sRho.bump(1)
            for j in range(2):
                sw = batch * 2 + j
                b = 4 + (sw % 2)
                te_bank(b)
                if j == 0:
                    prog.wait("tensor", sRho, sRho.n)
                prog.on("tensor", lambda T, j=j, b=b:
                        T.matmul(pb[b][:, :], c_ones_row[0:1, :],
                                 rowA[0:1, j * 512:(j + 1) * 512],
                                 start=True, stop=True).then_inc(sRbc.h, 1))
                sRbc.bump(1)
                prog.wait("vector", sRbc, sRbc.n)
                prog.on("vector", lambda V, sw=sw, b=b:
                        V.tensor_copy(rho_bc[:, sw * 512:(sw + 1) * 512], pb[b][:, :])
                        .then_inc(sRev.h, 1))
                sRev.bump(1)
                promise(f"bank{b}", sRev)
        sH = S("sH")
        prog.wait("vector", sRev, sRev.n)
        for kc in range(KC):
            prog.on("vector", lambda V, kc=kc:
                    V.scalar_tensor_tensor(h_fm(kc), xbf(kc), v_wqn[:, kc:kc + 1],
                                           rho_bc, ALU.mult, ALU.mult)
                    .then_inc(sH.h, 1))
            sH.bump(1)
        H_ALL = sH.n

        LdA = S("LdA")
        prog.wait("gpsimd", ccA, 1)
        prog.on("gpsimd", lambda G:
                G.dma_start(out=arow[0:1, 0:2], in_=cc_out_a.ap())
                .then_inc(LdA.h, 16))
        LdA.bump(16)
        sAl = S("sAl")
        prog.wait("vector", LdA, 16)

        def _alpha(V):
            V.tensor_scalar_mul(arow[0:1, 16:17], arow[0:1, 0:1],
                                1.0 / (6144.0 * 2048.0))
            V.tensor_scalar_mul(arow[0:1, 17:18], arow[0:1, 1:2],
                                1.0 / (2048.0 * 2048.0))
            V.memset(arow[0:1, 30:31], 0.0)
            V.tensor_scalar_max(arow[0:1, 16:17], arow[0:1, 16:17], 1e-5)
            V.tensor_scalar_max(arow[0:1, 17:18], arow[0:1, 17:18], 1e-5)
            V.memset(arow[0:1, 30:31], 0.0)
            V.reciprocal(arow[0:1, 18:19], arow[0:1, 16:17])
            V.reciprocal(arow[0:1, 19:20], arow[0:1, 17:18])
            V.memset(arow[0:1, 30:31], 0.0)
            V.tensor_scalar_mul(rowB[0:1, 0:1], arow[0:1, 16:17], DINV)
            V.tensor_copy(rowB[0:1, 1:2], arow[0:1, 16:17])
            V.tensor_copy(rowB[0:1, 2:3], arow[0:1, 18:19])
            V.tensor_copy(rowB[0:1, 3:4], arow[0:1, 17:18])
            V.tensor_copy(rowB[0:1, 4:5], arow[0:1, 19:20])
            V.memset(arow[0:1, 30:31], 0.0)
            V.memset(arow[0:1, 30:31], 0.0).then_inc(sAl.h, 1)
        prog.on("vector", _alpha)
        sAl.bump(1)
        promise("bank5", sAl)
        sMMb = S("sMMb")
        aDone = S("aDone")
        te_bank(5)
        prog.wait("tensor", sAl, 1)
        prog.on("tensor", lambda T: T.matmul(pb[5][:, 0:8], c_ones_row[0:1, :],
                                             rowB[0:1, 0:8], start=True, stop=True)
                .then_inc(sMMb.h, 1))
        sMMb.bump(1)
        prog.wait("vector", sMMb, 1)
        prog.on("vector", lambda V: V.tensor_copy(alpha[:, 0:8], pb[5][:, 0:8])
                .then_inc(aDone.h, 1))
        aDone.bump(1)
        promise("bank5", aDone)
        if debug:
            sDbgM = S("sDbgM")
            prog.wait("sync", aDone, 1)
            prog.on("sync", lambda E: E.dma_start(out=dbg["misc"][:, 0:8],
                                                  in_=alpha[:, 0:8])
                    .then_inc(sDbgM.h, 16))
            sDbgM.bump(16)
            prog.on("sync", lambda E: E.dma_start(out=dbg["misc"][:, 8:10],
                                                  in_=acc2[:, 0:2])
                    .then_inc(sDbgM.h, 16))
            sDbgM.bump(16)
            prog.on("sync", lambda E: E.dma_start(out=dbg["misc"][0:1, 10:12],
                                                  in_=arow[0:1, 0:2])
                    .then_inc(sDbgM.h, 16))
            sDbgM.bump(16)

        def finish_and_emit(waits):
            for eng, sem in waits:
                prog.wait(eng, sem, sem.n)
            with nc.Block() as block:
                @block.sync
                def _(E):
                    for fn in prog.streams["sync"]:
                        fn(E)

                @block.scalar
                def _(E):
                    for fn in prog.streams["scalar"]:
                        fn(E)

                @block.vector
                def _(E):
                    for fn in prog.streams["vector"]:
                        fn(E)

                @block.tensor
                def _(E):
                    for fn in prog.streams["tensor"]:
                        fn(E)

                @block.gpsimd
                def _(E):
                    for fn in prog.streams["gpsimd"]:
                        fn(E)
            return nc

        if phases == 0:
            return finish_and_emit([("sync", sH), ("sync", aDone)])

        sWq = [S("sWq0"), S("sWq1")]
        sWv = S("sWv")
        sStg = [S("sStg0"), S("sStg1")]
        sI32 = S("sI32")
        swctr = [0]

        def stream_weight(dst_ap, col0, ncols, slot_sem, src=None):
            if src is None:
                src = wqT
            total = KC * ncols
            nsub = total // 512
            kcps = 512 // ncols
            for sub in range(nsub):
                sp = swctr[0] % 2
                swctr[0] += 1
                kc0 = sub * kcps
                pwait("gpsimd", f"wstg{sp}")
                tgt = sStg[sp].bump(16)
                prog.on("gpsimd", lambda G, kc0=kc0, kcps=kcps, sp=sp, col0=col0,
                        ncols=ncols, src=src:
                        G.dma_start(
                            out=wstg(sp).rearrange("p (a b) -> p a b", b=ncols),
                            in_=src[kc0 * P:(kc0 + kcps) * P, col0:col0 + ncols]
                            .rearrange("(a p) b -> p a b", p=P))
                        .then_inc(sStg[sp].h, 16))
                prog.wait("scalar", sStg[sp], tgt)
                if swctr[0] <= 2:
                    prog.wait("scalar", aDone, 1)
                pwait("scalar", "wi32")
                prog.on("scalar", lambda Sc, sp=sp:
                        Sc.activation(out=wi32[:, 0:512], in_=wstg(sp),
                                      func=AF.Copy, scale=alpha[:, 2:3])
                        .then_inc(sI32.h, 1))
                sI32.bump(1)
                promise(f"wstg{sp}", sI32)
                prog.wait("vector", sI32, sI32.n)
                o0 = sub * 512
                prog.on("vector", lambda V, dst_ap=dst_ap, o0=o0:
                        V.tensor_scalar(dst_ap[:, o0:o0 + 512], wi32[:, 0:512],
                                        1.0, -1.0, ALU.min, ALU.max)
                        .then_inc(slot_sem.h, 1))
                slot_sem.bump(1)
                promise("wi32", slot_sem)

        def issue_head_weights(h, first=False):
            sl = h % 2
            if first:
                prog.wait("gpsimd", sAbs, ABS_ALL)
                if h == 1:
                    prog.wait("gpsimd", sH, H_ALL)
            pwait("gpsimd", f"wqk{sl}")
            stream_weight(wq_sl(sl), h * P, P, sWq[sl])
            stream_weight(wk_sl(sl), 2048 + h * P, P, sWq[sl])

        def issue_pair_v(g, first=False):
            if first:
                prog.wait("gpsimd", sH, H_ALL)
            pwait("gpsimd", "wv")
            stream_weight(wv_sl, 4096 + g * 256, 256, sWv)

        issue_head_weights(0, first=True)
        if p1_heads > 1:
            issue_head_weights(1, first=True)
        issue_pair_v(0, first=True)

        sQKe = S("sQKe")
        sVe = S("sVe")
        sPrepH = S("sPrepH")
        sPrepHs = S("sPrepHs")
        sScT = S("sScT")
        sIOe = S("sIOe")
        sSup = S("sSup")
        sKtm = S("sKtm")
        sCor = S("sCor")
        sStgCp = S("sStgCp")
        sStgH = S("sStgH")
        ccH = S("ccH")
        sGat = S("sGat")
        sSel = S("sSel")
        sMMq = S("sMMq")
        sLamB = S("sLamB")
        sQt = S("sQt")
        sTr = S("sTr")

        def emit_corrections(hp):
            psl = hp % 2
            if p1_relay != 3:
                prog.wait("gpsimd", ccH, hp + 1)
                prog.wait("gpsimd", sSel, sSel.n)
                tgt = sGat.bump(16)
                prog.on("gpsimd", lambda G, hp=hp:
                        G.dma_start(out=sall[:].rearrange("p (r j) -> p r j", j=P),
                                    in_=cc_out_h[hp].ap()
                                    .rearrange("(r p) j -> p r j", p=P))
                        .then_inc(sGat.h, 16))
                prog.wait("vector", sGat, tgt)
            if p1_relay == 4:
                return
            prog.wait("vector", sStgH, sStgH.n)

            def _sel(V):
                V.tensor_scalar(ssel_bf[:], sall[:, 0:128], c_onehot[:, 0:1],
                                None, ALU.mult)
                for r in range(1, 8):
                    ins = V.scalar_tensor_tensor(
                        ssel_bf[:], sall[:, r * 128:(r + 1) * 128],
                        c_onehot[:, r:r + 1], ssel_bf[:], ALU.mult, ALU.add)
                    if r == 7:
                        ins.then_inc(sSel.h, 1)
            if p1_relay != 6:
                prog.on("vector", _sel)
            else:
                prog.on("vector", lambda V: V.tensor_copy(ssel_bf[:], sall[:, 0:128])
                        .then_inc(sSel.h, 1))
            sSel.bump(1)
            if p1_relay == 5:
                return
            for w in range(2):
                te_bank(2)
                te_bank(3)
                if w == 0:
                    prog.wait("tensor", sSel, sSel.n)

                def _cmm(T, w=w, psl=psl):
                    for j in range(8):
                        c = 8 * w + j
                        b = 2 + j // 4
                        col = (j % 4) * 128
                        mm = T.matmul(pb[b][:, col:col + 128], ssel_bf[:],
                                      q_fm(psl)[:, c * 128:(c + 1) * 128],
                                      start=True, stop=True)
                        if j == 7:
                            mm.then_inc(sCor.h, 1)
                prog.on("tensor", _cmm)
                sCor.bump(1)
                prog.wait("vector", sCor, sCor.n)

                def _cev(V, w=w, hp=hp, psl=psl):
                    for j in range(8):
                        c = 8 * w + j
                        b = 2 + j // 4
                        col = (j % 4) * 128
                        V.scalar_tensor_tensor(s_stage[:], pb[b][:, col:col + 128],
                                               lamTp(psl)[:, c:c + 1], lam_t(psl),
                                               ALU.mult, ALU.mult)
                        ins = V.tensor_add(attn(hp)[:, c * 128:(c + 1) * 128],
                                           attn(hp)[:, c * 128:(c + 1) * 128],
                                           s_stage[:])
                        if j == 7:
                            ins.then_inc(sCor.h, 1)
                prog.on("vector", _cev)
                sCor.bump(1)
                promise("bank2", sCor)
                promise("bank3", sCor)

        for h in range(p1_heads):
            sl = h % 2
            g = h // 2

            if h == 0:
                prog.wait("scalar", Lc, LC_ALL)
                prog.wait("scalar", sPrep, PREP_ALL)
                prog.wait("scalar", sXMM, XMM_ALL)
                prog.wait("scalar", sH, H_ALL)
            prog.wait("scalar", sLamB, sLamB.n)

            def _hb_sc(Sc, h=h, sl=sl):
                Sc.activation(out=lam2(sl), in_=c_GI[:], func=AF.Exp,
                              scale=c_negs[:, h:h + 1])
                Sc.activation(out=lamTp(sl), in_=c_J128[:], func=AF.Exp,
                              scale=c_negs[:, h:h + 1])
                Sc.activation(out=lrev_col(sl), in_=c_irev[:], func=AF.Exp,
                              scale=c_negs[:, h:h + 1])
                Sc.activation(out=mask_t(sl), in_=c_diffT[:], func=AF.Exp,
                              scale=c_negs[:, h:h + 1])
                Sc.activation(out=rowB[0:1, 0:128], in_=c_iota_row[:],
                              func=AF.Exp, scale=c_negs[0:1, h:h + 1])
                Sc.copy(arow[0:1, 24:25], c_cv[0:1, 0:1])
                Sc.copy(arow[0:1, 24:25], c_cv[0:1, 0:1]) \
                    .then_inc(sPrepHs.h, 1)
            prog.on("scalar", _hb_sc)
            sPrepHs.bump(1)
            prog.wait("vector", sPrepHs, sPrepHs.n)
            if h == 0:
                prog.wait("vector", sXMM, XMM_ALL)

            def _hb_ve(V, sl=sl):
                V.tensor_scalar(lam2(sl), lam2(sl), c_gate[:, 0:1], None, ALU.mult)
                V.tensor_scalar(lamTp(sl), lamTp(sl), c_gate[:, 0:1], None, ALU.mult)
                V.tensor_mul(mask_t(sl), mask_t(sl), c_causal[:]) \
                    .then_inc(sPrepH.h, 1)
            prog.on("vector", _hb_ve)
            sPrepH.bump(1)
            te_bank(5)
            prog.wait("tensor", sPrepHs, sPrepHs.n)
            prog.on("tensor", lambda T:
                    T.matmul(pb[5][:, 0:128], c_ones_row[0:1, :],
                             rowB[0:1, 0:128], start=True, stop=True)
                    .then_inc(sLamB.h, 1))
            sLamB.bump(1)
            prog.wait("vector", sLamB, sLamB.n)
            prog.on("vector", lambda V, sl=sl:
                    V.tensor_copy(lam_t(sl), pb[5][:, 0:128])
                    .then_inc(sPrepH.h, 1))
            sPrepH.bump(1)
            promise("bank5", sPrepH)

            cor_prev = sCor.n

            if h == 0:
                prog.wait("tensor", sH, H_ALL)
            prog.wait("tensor", sWq[sl], sWq[sl].n)
            if cor_prev:
                prog.wait("tensor", sCor, cor_prev)
            for sw in range(4):
                b = sw % 2
                te_bank(b)

                def _qmm(T, sw=sw, b=b, sl=sl):
                    for kc in range(KC):
                        mm = T.matmul(pb[b][:, :],
                                      wq_sl(sl)[:, kc * 128:(kc + 1) * 128],
                                      h_fm(kc)[:, sw * 512:(sw + 1) * 512],
                                      start=(kc == 0), stop=(kc == KC - 1))
                        if kc == KC - 1:
                            mm.then_inc(sMMq.h, 1)
                prog.on("tensor", _qmm)
                sMMq.bump(1)
                prog.wait("scalar", sMMq, sMMq.n)
                if sw == 0 and cor_prev:
                    prog.wait("scalar", sCor, cor_prev)
                prog.on("scalar", lambda Sc, sw=sw, b=b, sl=sl:
                        Sc.activation(out=q_fm(sl)[:, sw * 512:(sw + 1) * 512],
                                      in_=pb[b][:, :], func=AF.Copy,
                                      scale=alpha[:, 0:1])
                        .then_inc(sQKe.h, 1))
                sQKe.bump(1)
                promise(f"bank{b}", sQKe)
            for sw in range(4):
                b = 2 + sw % 2
                te_bank(b)

                def _kmm(T, sw=sw, b=b, sl=sl):
                    for kc in range(KC):
                        mm = T.matmul(pb[b][:, :],
                                      wk_sl(sl)[:, kc * 128:(kc + 1) * 128],
                                      h_fm(kc)[:, sw * 512:(sw + 1) * 512],
                                      start=(kc == 0), stop=(kc == KC - 1))
                        if kc == KC - 1:
                            mm.then_inc(sMMq.h, 1)
                prog.on("tensor", _kmm)
                sMMq.bump(1)
                prog.wait("scalar", sMMq, sMMq.n)
                prog.on("scalar", lambda Sc, sw=sw, b=b:
                        Sc.activation(out=k_fm[:, sw * 512:(sw + 1) * 512],
                                      in_=pb[b][:, :], func=AF.Copy,
                                      scale=alpha[:, 1:2])
                        .then_inc(sQKe.h, 1))
                sQKe.bump(1)
                promise(f"bank{b}", sQKe)
                if sw == 3:
                    promise(f"wqk{sl}", sQKe)
            if h % 2 == 0:
                prog.wait("tensor", sWv, sWv.n)
                for c in range(CH):
                    b = 4 + (c % 2)
                    te_bank(b)

                    def _vmm(T, c=c, b=b):
                        for kc in range(KC):
                            mm = T.matmul(pb[b][:, 0:256],
                                          h_fm(kc)[:, c * 128:(c + 1) * 128],
                                          wv_sl[:, kc * 256:(kc + 1) * 256],
                                          start=(kc == 0), stop=(kc == KC - 1))
                            if kc == KC - 1:
                                mm.then_inc(sMMq.h, 1)
                    prog.on("tensor", _vmm)
                    sMMq.bump(1)
                    prog.wait("scalar", sMMq, sMMq.n)
                    prog.on("scalar", lambda Sc, c=c, b=b:
                            Sc.activation(out=v_tm[:, c * 256:(c + 1) * 256],
                                          in_=pb[b][:, 0:256], func=AF.Copy,
                                          scale=alpha[:, 1:2])
                            .then_inc(sVe.h, 1))
                    sVe.bump(1)
                    promise(f"bank{b}", sVe)
                    if c == CH - 1:
                        promise("wv", sVe)
                if g + 1 < H // 2:
                    issue_pair_v(g + 1)
            if h + 2 < p1_heads:
                issue_head_weights(h + 2)
            if h >= 1 and p1_relay >= 2:
                emit_corrections(h - 1)

            prog.wait("vector", sQKe, sQKe.n)
            prog.wait("vector", sPrepH, sPrepH.n)

            def _qt(V, sl=sl):
                rep = lam_t(sl).unsqueeze(1).broadcast_to((P, CH, P))
                q3 = q_fm(sl).rearrange("p (c t) -> p c t", t=P)
                o3 = qt_fm[:].rearrange("p (c t) -> p c t", t=P)
                V.tensor_tensor(o3, q3, rep, ALU.mult).then_inc(sQt.h, 1)
            prog.on("vector", _qt)
            sQt.bump(1)

            te_pt(0)
            prog.wait("tensor", sQKe, sQKe.n)
            prog.on("tensor", lambda T:
                    T.transpose(ptA[:, 0:128], k_fm[:, 0:128], c_ident[:])
                    .then_inc(sTr.h, 1))
            sTr.bump(1)
            prog.wait("scalar", sTr, sTr.n)
            prog.on("scalar", lambda Sc, sl=sl:
                    Sc.activation(out=ktm(0), in_=ptA[:, 0:128], func=AF.Copy,
                                  scale=lrev_col(sl))
                    .then_inc(sKtm.h, 1))
            sKtm.bump(1)
            promise("pt0", sKtm)

            vbase = (h % 2) * 128
            for c in range(CH):
                b_s = c % 2
                te_bank(b_s)
                if c == 0:
                    prog.wait("tensor", sQt, sQt.n)
                prog.on("tensor", lambda T, c=c, b=b_s, sl=sl:
                        T.matmul(pb[b][:, 0:128], k_fm[:, c * 128:(c + 1) * 128],
                                 q_fm(sl)[:, c * 128:(c + 1) * 128],
                                 start=True, stop=True).then_inc(sMMq.h, 1))
                sMMq.bump(1)
                prog.wait("vector", sMMq, sMMq.n)
                if c == 0:
                    prog.wait("vector", sPrepH, sPrepH.n)
                prog.on("vector", lambda V, c=c, b=b_s, sl=sl:
                        V.tensor_mul(scT(c % 2), pb[b][:, 0:128], mask_t(sl))
                        .then_inc(sScT.h, 1))
                sScT.bump(1)
                promise(f"bank{b_s}", sScT)
                b_kv = 4 + (c % 2)
                te_bank(b_kv)
                prog.wait("tensor", sKtm, sKtm.n)
                if c == 0:
                    prog.wait("tensor", sVe, sVe.n)
                prog.on("tensor", lambda T, c=c, b=b_kv, vbase=vbase:
                        T.matmul(pb[b][:, 0:128], ktm(c % 2),
                                 v_tm[:, c * 256 + vbase:c * 256 + vbase + 128],
                                 start=True, stop=True).then_inc(sMMq.h, 1))
                sMMq.bump(1)
                if c + 1 < CH:
                    np_ = (c + 1) % 2
                    te_pt(np_)
                    prog.on("tensor", lambda T, c=c, np_=np_:
                            T.transpose((ptA if np_ == 0 else ptB)[:, 0:128],
                                        k_fm[:, (c + 1) * 128:(c + 2) * 128],
                                        c_ident[:]).then_inc(sTr.h, 1))
                    sTr.bump(1)
                    prog.wait("scalar", sTr, sTr.n)
                    prog.on("scalar", lambda Sc, np_=np_, sl=sl:
                            Sc.activation(out=ktm(np_),
                                          in_=(ptA if np_ == 0 else ptB)[:, 0:128],
                                          func=AF.Copy, scale=lrev_col(sl))
                            .then_inc(sKtm.h, 1))
                    sKtm.bump(1)
                    promise(f"pt{np_}", sKtm)
                b_io = 2 + (c % 2)
                te_bank(b_io)
                prog.wait("tensor", sScT, sScT.n)
                if c > 0:
                    prog.wait("tensor", sSup, sSup.n)

                def _iomm(T, c=c, b=b_io, vbase=vbase):
                    mm1 = T.matmul(pb[b][:, 0:128],
                                   v_tm[:, c * 256 + vbase:c * 256 + vbase + 128],
                                   scT(c % 2), start=True, stop=(c == 0))
                    if c > 0:
                        mm2 = T.matmul(pb[b][:, 0:128], sbf((c + 1) % 2),
                                       qt_fm[:, c * 128:(c + 1) * 128],
                                       start=False, stop=True)
                        mm2.then_inc(sMMq.h, 1)
                    else:
                        mm1.then_inc(sMMq.h, 1)
                prog.on("tensor", _iomm)
                sMMq.bump(1)
                prog.wait("vector", sMMq, sMMq.n)
                prog.on("vector", lambda V, c=c, b=b_io, h=h:
                        V.tensor_copy(attn(h)[:, c * 128:(c + 1) * 128],
                                      pb[b][:, 0:128])
                        .then_inc(sIOe.h, 1))
                sIOe.bump(1)
                promise(f"bank{b_io}", sIOe)

                def _sup(V, c=c, b=b_kv, h=h):
                    if c == 0:
                        V.tensor_copy(sbf(0), pb[b][:, 0:128]).then_inc(sSup.h, 1)
                    else:
                        V.scalar_tensor_tensor(sbf(c % 2), sbf((c + 1) % 2),
                                               c_lamT[:, h:h + 1], pb[b][:, 0:128],
                                               ALU.mult, ALU.add) \
                            .then_inc(sSup.h, 1)
                prog.on("vector", _sup)
                sSup.bump(1)
                promise(f"bank{b_kv}", sSup)

            if p1_relay == 0:
                continue
            prog.wait("vector", sStgH, sStgH.n)

            def _stgcp(V):
                V.tensor_copy(s_stage[:], sbf(1))
                V.memset(alpha[:, 7:8], 0.0)
                V.memset(alpha[:, 7:8], 0.0).then_inc(sStgCp.h, 1)
            prog.on("vector", _stgcp)
            sStgCp.bump(1)
            prog.wait("sync", sStgCp, sStgCp.n)
            prog.on("sync", lambda E, h=h:
                    E.dma_start(out=cc_in_h[h].ap(), in_=s_stage[:])
                    .then_inc(sStgH.h, 16))
            sStgH.bump(16)
            prog.wait("gpsimd", sStgH, sStgH.n)
            prog.on("gpsimd", lambda G, h=h: G.collective_compute(
                "AllGather", ALU.bypass, replica_groups=[list(range(8))],
                ins=[cc_in_h[h].ap().opt()], outs=[cc_out_h[h].ap().opt()])
                .then_inc(ccH.h, 1))
            ccH.bump(1)

        if p1_relay >= 2:
            emit_corrections(p1_heads - 1)
        P1_CORR_ALL = sCor.n

        if debug:
            sDbg = S("sDbg")
            prog.wait("sync", sCor, P1_CORR_ALL)
            prog.on("sync", lambda E: E.dma_start(out=dbg["attn"],
                                                  in_=R2[:, 0:KC * TOK])
                    .then_inc(sDbg.h, 16))
            sDbg.bump(16)
            prog.wait("sync", sH, H_ALL)
            prog.on("sync", lambda E: E.dma_start(out=dbg["h"], in_=R1[:, 0:KC * TOK])
                    .then_inc(sDbg.h, 16))
            sDbg.bump(16)

        if phases == 1:
            return finish_and_emit([("sync", sCor), ("sync", sIOe),
                                    ("sync", sStgH)])

        B1 = S("B1")
        for e in ENGINES:
            prog.on(e, lambda E: E.nop().then_inc(B1.h, 1))
        B1.bump(5)
        for e in ENGINES:
            prog.wait(e, B1, 5)

        if debug:
            prog.wait("vector", sems["sDbg"], sems["sDbg"].n)
        sWpT = S("sWpT")
        sWpSt = S("sWpSt")
        ccW = S("ccW")
        wpb_bounce = R3[:, 20480:20736]
        for kc in range(KC):
            sp = kc % 2
            pwait("sync", f"wstg{sp}")
            tgt = sStg[sp].bump(16)
            prog.on("sync", lambda E, kc=kc, sp=sp:
                    E.dma_start(out=wstg(sp)[:, 0:256],
                                in_=wp_scan[kc * P:(kc + 1) * P, :])
                    .then_inc(sStg[sp].h, 16))
            prog.wait("scalar", sStg[sp], tgt)
            pwait("scalar", "wi32")
            prog.on("scalar", lambda Sc, sp=sp:
                    Sc.activation(out=wi32[:, 0:256], in_=wstg(sp)[:, 0:256],
                                  func=AF.Copy, scale=alpha[:, 4:5])
                    .then_inc(sI32.h, 1))
            sI32.bump(1)
            promise(f"wstg{sp}", sI32)
            prog.wait("vector", sI32, sI32.n)
            if kc >= 1:
                prog.wait("vector", sWpSt, sWpSt.n - 32)
            prog.on("vector", lambda V:
                    V.tensor_scalar(wpb_bounce, wi32[:, 0:256],
                                    1.0, -1.0, ALU.min, ALU.max)
                    .then_inc(sWpT.h, 1))
            sWpT.bump(1)
            promise("wi32", sWpT)
            prog.wait("sync", sWpT, sWpT.n)
            for dsub in range(2):
                prog.on("sync", lambda E, kc=kc, dsub=dsub:
                        E.dma_start(out=wpbf_loc[dsub * P:(dsub + 1) * P,
                                                 kc * P:(kc + 1) * P],
                                    in_=wpb_bounce[:, dsub * 128:(dsub + 1) * 128])
                        .then_inc(sWpSt.h, 16))
                sWpSt.bump(16)
        WPST_ALL = sWpSt.n
        prog.wait("gpsimd", sWpSt, WPST_ALL)
        prog.on("gpsimd", lambda G: G.collective_compute(
            "AllGather", ALU.bypass, replica_groups=[list(range(8))],
            ins=[wpbf_loc.opt()], outs=[wpbf.opt()]).then_inc(ccW.h, 1))
        ccW.bump(1)
        sC8 = S("sC8")
        for dst8, srcp in ((wg8l, wgpre), (wu8l, wupre), (wd8l, wdpre)):
            prog.on("gpsimd", lambda G, dst8=dst8, srcp=srcp:
                    G.dma_start(out=dst8, in_=srcp).then_inc(sC8.h, 16))
            sC8.bump(16)
        prog.wait("gpsimd", sC8, sC8.n)
        for locp, allp in ((wg8l, wg8), (wu8l, wu8), (wd8l, wd8)):
            prog.on("gpsimd", lambda G, locp=locp, allp=allp:
                    G.collective_compute(
                        "AllGather", ALU.bypass, replica_groups=[list(range(8))],
                        ins=[locp.opt()], outs=[allp.opt()]).then_inc(ccW.h, 1))
            ccW.bump(1)
        CCW_ALL = ccW.n

        sSq = S("sSq")
        sStM = S("sStM")
        sScR = S("sScR")
        sRbc2 = S("sRbc2")
        sRev2 = S("sRev2")
        for sw in range(4):
            bh, bl = 2, 3
            te_bank(bh)
            te_bank(bl)
            for ec in range(KC):
                p = ec % 2
                prog.wait("vector", sStM, max(0, sStM.n - 1))

                def _sqv(V, ec=ec, sw=sw, p=p):
                    V.tensor_mul(p2_sq(p), attn(ec)[:, sw * 512:(sw + 1) * 512],
                                 attn(ec)[:, sw * 512:(sw + 1) * 512])
                    V.memset(alpha[:, 7:8], 0.0).then_inc(sSq.h, 1)
                prog.on("vector", _sqv)
                sSq.bump(1)
                prog.wait("tensor", sSq, sSq.n)

                def _stmm(T, ec=ec, p=p, bh=bh, bl=bl):
                    T.matmul(pb[bh][0:1, :], v_shl[:, ec:ec + 1], p2_sq(p),
                             start=(ec == 0), stop=(ec == KC - 1))
                    mm = T.matmul(pb[bl][0:1, :], v_shl[:, KC + ec:KC + ec + 1],
                                  p2_sq(p), start=(ec == 0), stop=(ec == KC - 1))
                    mm.then_inc(sStM.h, 1)
                prog.on("tensor", _stmm)
                sStM.bump(1)
            r = sw % 2
            prog.wait("scalar", sStM, sStM.n)
            if sw >= 2:
                prog.wait("scalar", sRbc2, sw - 1)
            def _rowcp(Sc, r=r, bh=bh):
                Sc.copy(rowA[0:1, r * 512:(r + 1) * 512], pb[bh][0:1, :])
                Sc.copy(arow[0:1, 24:25], c_cv[0:1, 0:1])
                Sc.copy(arow[0:1, 24:25], c_cv[0:1, 0:1]).then_inc(sScR.h, 1)
            prog.on("scalar", _rowcp)
            sScR.bump(1)
            promise(f"bank{bh}", sScR)
            prog.wait("vector", sScR, sScR.n)

            def _rowadd(V, r=r, bl=bl):
                V.tensor_add(rowA[0:1, r * 512:(r + 1) * 512],
                             rowA[0:1, r * 512:(r + 1) * 512], pb[bl][0:1, :])
                V.memset(alpha[:, 7:8], 0.0)
                V.memset(alpha[:, 7:8], 0.0).then_inc(sScR.h, 1)
            prog.on("vector", _rowadd)
            sScR.bump(1)
            promise(f"bank{bl}", sScR)
            prog.wait("scalar", sScR, sScR.n)

            def _rhosq(Sc, r=r):
                Sc.activation(out=rowA[0:1, r * 512:(r + 1) * 512],
                              in_=rowA[0:1, r * 512:(r + 1) * 512],
                              func=AF.Sqrt, bias=c_cv[0:1, 3:4])
                Sc.copy(arow[0:1, 24:25], c_cv[0:1, 0:1])
                Sc.copy(arow[0:1, 24:25], c_cv[0:1, 0:1]).then_inc(sScR.h, 1)
            prog.on("scalar", _rhosq)
            sScR.bump(1)
            prog.wait("vector", sScR, sScR.n)

            def _rhorec(V, r=r):
                V.reciprocal(rowA[0:1, r * 512:(r + 1) * 512],
                             rowA[0:1, r * 512:(r + 1) * 512])
                V.memset(alpha[:, 7:8], 0.0)
                V.memset(alpha[:, 7:8], 0.0).then_inc(sScR.h, 1)
            prog.on("vector", _rhorec)
            sScR.bump(1)
            b = 4 + sw % 2
            te_bank(b)
            prog.wait("tensor", sScR, sScR.n)
            prog.on("tensor", lambda T, r=r, b=b:
                    T.matmul(pb[b][:, :], c_ones_row[0:1, :],
                             rowA[0:1, r * 512:(r + 1) * 512],
                             start=True, stop=True).then_inc(sRbc2.h, 1))
            sRbc2.bump(1)
            prog.wait("vector", sRbc2, sRbc2.n)
            prog.on("vector", lambda V, sw=sw, b=b:
                    V.tensor_copy(rho12_bc[:, sw * 512:(sw + 1) * 512], pb[b][:, :])
                    .then_inc(sRev2.h, 1))
            sRev2.bump(1)
            promise(f"bank{b}", sRev2)

        if debug:
            prog.wait("sync", sRev2, sRev2.n)
            prog.on("sync", lambda E: E.dma_start(out=dbg["rho"],
                                                  in_=rho12_bc)
                    .then_inc(sems["sDbg"].h, 16))
            sems["sDbg"].bump(16)
            prog.wait("sync", ccW, 1)
            prog.on("sync", lambda E: E.dma_start(out=dbg["wp"],
                                                  in_=wpbf[0:P, 0:512])
                    .then_inc(sems["sDbg"].h, 16))
            sems["sDbg"].bump(16)

        sZ = S("sZ")
        sPm = S("sPm")
        sR1 = S("sR1")
        sR1b = S("sR1b")
        sLnM = S("sLnM")
        sLnR = S("sLnR")
        sX1 = S("sX1")
        sX2 = S("sX2")
        sMS = S("sMS")
        sB2c = S("sB2c")
        sWpL = [S("sWpLa"), S("sWpLb")]
        for sw in range(4):
            half, tl0 = sw // 2, (sw % 2) * 512
            prog.wait("vector", sRev2, sRev2.n)

            def _z(V, sw=sw, half=half, tl0=tl0):
                for ec in range(KC):
                    ins = V.scalar_tensor_tensor(
                        x1_sl(ec, half, tl0, 512),
                        attn(ec)[:, sw * 512:(sw + 1) * 512],
                        v_w12[:, ec:ec + 1],
                        rho12_bc[:, sw * 512:(sw + 1) * 512],
                        ALU.mult, ALU.mult)
                    if ec == KC - 1:
                        ins.then_inc(sZ.h, 1)
            prog.on("vector", _z)
            sZ.bump(1)
            for doc in range(KC):
                sp = doc % 2
                pwait("scalar", f"wp{sp}")
                if sw == 0 and doc < 2:
                    prog.wait("scalar", ccW, 1)
                tgtw = sWpL[sp].bump(16)
                prog.on("scalar", lambda Sc, doc=doc, sp=sp:
                        Sc.dma_start(out=wpsl(sp),
                                     in_=wpbf[doc * P:(doc + 1) * P, :])
                        .then_inc(sWpL[sp].h, 16))
                pwait("sync", f"x2{sp}")
                tgt = sX2.bump(16)
                prog.on("sync", lambda E, doc=doc, sw=sw, sp=sp:
                        E.dma_start(out=p2_xstg(sp),
                                    in_=xT[doc * P:(doc + 1) * P,
                                           sw * 512:(sw + 1) * 512])
                        .then_inc(sX2.h, 16))
                b = doc % 2
                te_bank(b)
                prog.wait("tensor", sZ, sZ.n)
                prog.wait("tensor", sWpL[sp], tgtw)

                def _pmm(T, doc=doc, half=half, tl0=tl0, b=b, sp=sp):
                    for ec in range(KC):
                        mm = T.matmul(pb[b][:, :],
                                      wpsl(sp)[:, ec * 128:(ec + 1) * 128],
                                      x1_sl(ec, half, tl0, 512),
                                      start=(ec == 0), stop=(ec == KC - 1))
                        if ec == KC - 1:
                            mm.then_inc(sPm.h, 1)
                prog.on("tensor", _pmm)
                sPm.bump(1)
                prog.wait("vector", sPm, sPm.n)
                prog.wait("vector", sX2, tgt)
                def _r1ev(V, doc=doc, b=b, sp=sp):
                    V.scalar_tensor_tensor(r1_swp(doc), pb[b][:, :],
                                           alpha[:, 3:4], p2_xstg(sp),
                                           ALU.mult, ALU.add)
                    V.memset(alpha[:, 7:8], 0.0)
                    V.memset(alpha[:, 7:8], 0.0).then_inc(sR1.h, 1)
                prog.on("vector", _r1ev)
                sR1.bump(1)
                promise(f"bank{b}", sR1)
                promise(f"x2{sp}", sR1)
                promise(f"wp{sp}", sR1)
                prog.wait("scalar", sR1, sR1.n)
                prog.wait("scalar", sLnM, max(0, sLnM.n - 1))

                def _r1bf(Sc, doc=doc):
                    Sc.activation(out=p2_r1bf(doc % 2), in_=r1_swp(doc),
                                  func=AF.Copy)
                    Sc.copy(arow[0:1, 24:25], c_cv[0:1, 0:1]) \
                        .then_inc(sR1b.h, 1)
                prog.on("scalar", _r1bf)
                sR1b.bump(1)
                prog.wait("vector", sR1b, sR1b.n)

                def _r1sq(V, doc=doc):
                    V.tensor_mul(p2_r1sq(doc % 2), p2_r1bf(doc % 2),
                                 p2_r1bf(doc % 2))
                    V.memset(alpha[:, 7:8], 0.0)
                    V.memset(alpha[:, 7:8], 0.0).then_inc(sR1b.h, 1)
                prog.on("vector", _r1sq)
                sR1b.bump(1)
                if doc == 0:
                    te_bank(2)
                    te_bank(3)
                prog.wait("tensor", sR1b, sR1b.n)

                def _lnmm(T, doc=doc):
                    T.matmul(pb[2][0:1, :], c_ones_col_bf[:], p2_r1bf(doc % 2),
                             start=(doc == 0), stop=(doc == KC - 1))
                    mm = T.matmul(pb[3][0:1, :], c_ones_col_bf[:], p2_r1sq(doc % 2),
                                  start=(doc == 0), stop=(doc == KC - 1))
                    mm.then_inc(sLnM.h, 1)
                prog.on("tensor", _lnmm)
                sLnM.bump(1)
            prog.wait("vector", sLnM, sLnM.n)

            def _lnr(V):
                V.tensor_scalar(rowB[0:1, 0:512], pb[2][0:1, :], 1.0 / 2048.0, 0.0,
                                ALU.mult, ALU.add)
                V.tensor_mul(rowB[0:1, 512:1024], rowB[0:1, 0:512], rowB[0:1, 0:512])
                V.scalar_tensor_tensor(rowB[0:1, 512:1024], pb[3][0:1, :],
                                       1.0 / 2048.0, rowB[0:1, 512:1024],
                                       ALU.mult, ALU.subtract)
                V.memset(alpha[:, 7:8], 0.0)
                V.memset(alpha[:, 7:8], 0.0).then_inc(sLnR.h, 1)
            prog.on("vector", _lnr)
            sLnR.bump(1)
            promise("bank2", sLnR)
            promise("bank3", sLnR)
            prog.wait("scalar", sLnR, sLnR.n)

            def _lnsq(Sc):
                Sc.activation(out=rowB[0:1, 512:1024], in_=rowB[0:1, 512:1024],
                              func=AF.Sqrt, bias=c_cv[0:1, 4:5])
                Sc.copy(arow[0:1, 24:25], c_cv[0:1, 0:1])
                Sc.copy(arow[0:1, 24:25], c_cv[0:1, 0:1]).then_inc(sLnR.h, 1)
            prog.on("scalar", _lnsq)
            sLnR.bump(1)
            prog.wait("vector", sLnR, sLnR.n)

            def _lnf(V):
                V.reciprocal(rowB[0:1, 512:1024], rowB[0:1, 512:1024])
                V.tensor_mul(rowB[0:1, 0:512], rowB[0:1, 0:512],
                             rowB[0:1, 512:1024])
                V.memset(alpha[:, 7:8], 0.0)
                V.memset(alpha[:, 7:8], 0.0).then_inc(sLnR.h, 1)
            prog.on("vector", _lnf)
            sLnR.bump(1)
            for i, (src0, dstt) in enumerate([(512, p2_sgbc), (0, p2_mubc)]):
                b = 4 + i
                te_bank(b)
                prog.wait("tensor", sLnR, sLnR.n)
                prog.on("tensor", lambda T, src0=src0, b=b:
                        T.matmul(pb[b][:, :], c_ones_row[0:1, :],
                                 rowB[0:1, src0:src0 + 512], start=True, stop=True)
                        .then_inc(sB2c.h, 1))
                sB2c.bump(1)
                prog.wait("vector", sB2c, sB2c.n)
                prog.on("vector", lambda V, dstt=dstt, b=b:
                        V.tensor_copy(dstt, pb[b][:, :])
                        .then_inc(sB2c.h, 1))
                sB2c.bump(1)
                promise(f"bank{b}", sB2c)
            prog.wait("vector", sB2c, sB2c.n)

            def _x1(V, sw=sw, half=half, tl0=tl0):
                for doc in range(KC):
                    V.tensor_mul(r1_swp(doc), r1_swp(doc), p2_sgbc)
                    V.tensor_sub(x1_sl(doc, half, tl0, 512), r1_swp(doc),
                                 p2_mubc)
                V.memset(alpha[:, 7:8], 0.0)
                V.memset(alpha[:, 7:8], 0.0).then_inc(sX1.h, 1)
            prog.on("vector", _x1)
            sX1.bump(1)
        X1_ALL = sX1.n

        if debug:
            prog.wait("sync", sX1, X1_ALL)
            prog.on("sync", lambda E: E.dma_start(out=dbg["x1a"], in_=R3[:, 0:20480])
                    .then_inc(sems["sDbg"].h, 16))
            sems["sDbg"].bump(16)
            prog.on("sync", lambda E: E.dma_start(out=dbg["x1b"],
                                                  in_=R1[:, 31744:44032])
                    .then_inc(sems["sDbg"].h, 16))
            sems["sDbg"].bump(16)

        sSp = S("sSp")
        prog.wait("sync", sX1, X1_ALL)
        prog.on("sync", lambda E: E.dma_start(out=x1h2[:, 0:10240],
                                              in_=R3[:, 10240:20480])
                .then_inc(sSp.h, 16))
        sSp.bump(16)
        prog.on("sync", lambda E: E.dma_start(out=x1h2[:, 10240:16384],
                                              in_=R1[:, 31744 + 6144:44032])
                .then_inc(sSp.h, 16))
        sSp.bump(16)

        if phases == 2:
            return finish_and_emit([("sync", sX1), ("sync", sSp)])

        B2 = S("B2")
        for e in ENGINES:
            prog.on(e, lambda E: E.nop().then_inc(B2.h, 1))
        B2.bump(5)
        for e in ENGINES:
            prog.wait(e, B2, 5)
        prog.wait("gpsimd", sSp, sSp.n)

        sWg = [S("sWgA"), S("sWgB")]
        sWu = [S("sWuA"), S("sWuB")]
        sWd = [S("sWdA"), S("sWdB")]
        sCs = S("sCs")
        sGuM = S("sGuM")
        sGuE = S("sGuE")
        sGu = S("sGu")
        sGsum = S("sGsum")
        sDnM = S("sDnM")
        sR2v = S("sR2v")
        sLn2M = S("sLn2M")
        sLn2R = S("sLn2R")
        sB3 = S("sB3")
        sOut = S("sOut")
        sMSl = S("sMSl")
        sX1r = S("sX1r")

        for sw in range(4):
            half, swl = sw // 2, sw % 2
            if sw == 2:
                prog.wait("sync", sGu, sGu.n)
                prog.wait("sync", sOut, sOut.n)
                prog.wait("sync", sCs, sCs.n)
                prog.wait("sync", sR2v, sR2v.n)
                tgt = sX1r.bump(32)
                prog.on("sync", lambda E:
                        E.dma_start(out=R3[:, 0:10240], in_=x1h2[:, 0:10240])
                        .then_inc(sX1r.h, 16))
                prog.on("sync", lambda E:
                        E.dma_start(out=R1[:, 31744:31744 + 6144],
                                    in_=x1h2[:, 10240:16384])
                        .then_inc(sX1r.h, 16))
                for e in ("tensor", "vector"):
                    prog.wait(e, sX1r, tgt)
            te_bank(4)
            if sw == 0:
                prog.wait("tensor", sX1, X1_ALL)

            def _csmm(T, swl=swl):
                for kc in range(KC):
                    mm = T.matmul(pb[4][0:1, :], c_ones_col_bf[:],
                                  x1_cur(kc, swl * 512, 512),
                                  start=(kc == 0), stop=(kc == KC - 1))
                    if kc == KC - 1:
                        mm.then_inc(sCs.h, 1)
            prog.on("tensor", _csmm)
            sCs.bump(1)
            prog.wait("scalar", sCs, sCs.n)

            def _csrow(Sc):
                Sc.copy(p3_rows[0:1, 0:512], pb[4][0:1, :])
                Sc.copy(arow[0:1, 24:25], c_cv[0:1, 0:1])
                Sc.copy(arow[0:1, 24:25], c_cv[0:1, 0:1]).then_inc(sCs.h, 1)
            prog.on("scalar", _csrow)
            sCs.bump(1)
            promise("bank4", sCs)
            for hc in range(HOC):
                sp = hc % 2
                pwait("gpsimd", f"wg{sp}")
                if sw == 0 and hc == 0:
                    prog.wait("gpsimd", ccW, CCW_ALL)
                tgtg = sWg[sp].bump(16)
                prog.on("gpsimd", lambda G, hc=hc, sp=sp:
                        G.dma_start(out=wg_sl(sp),
                                    in_=wg8[hc * P:(hc + 1) * P, :])
                        .then_inc(sWg[sp].h, 16))
                pwait("gpsimd", f"wu{sp}")
                tgtu = sWu[sp].bump(16)
                prog.on("gpsimd", lambda G, hc=hc, sp=sp:
                        G.dma_start(out=wu_sl(sp),
                                    in_=wu8[hc * P:(hc + 1) * P, :])
                        .then_inc(sWu[sp].h, 16))
                te_bank(0)
                prog.wait("tensor", sWg[sp], tgtg)
                if hc == 0:
                    prog.wait("tensor", sCs, sCs.n)

                def _gmm(T, sp=sp, swl=swl):
                    for kc in range(KC):
                        T.matmul(pb[0][:, :], wg_sl(sp)[:, kc * 128:(kc + 1) * 128],
                                 x1_cur(kc, swl * 512, 512),
                                 start=(kc == 0), stop=False)
                    mm = T.matmul(pb[0][:, :], c_neg128_row[0:1, :],
                                  p3_rows[0:1, 0:512], start=False, stop=True)
                    mm.then_inc(sGuM.h, 1)
                prog.on("tensor", _gmm)
                sGuM.bump(1)
                te_bank(1)
                prog.wait("tensor", sWu[sp], tgtu)

                def _umm(T, sp=sp, swl=swl):
                    for kc in range(KC):
                        T.matmul(pb[1][:, :], wu_sl(sp)[:, kc * 128:(kc + 1) * 128],
                                 x1_cur(kc, swl * 512, 512),
                                 start=(kc == 0), stop=False)
                    mm = T.matmul(pb[1][:, :], c_neg128_row[0:1, :],
                                  p3_rows[0:1, 0:512], start=False, stop=True)
                    mm.then_inc(sGuM.h, 1)
                prog.on("tensor", _umm)
                sGuM.bump(1)
                prog.wait("scalar", sGuM, sGuM.n)
                prog.wait("scalar", sGu, max(0, sGu.n - 1))

                def _ge(Sc, hc=hc):
                    Sc.activation(out=gtmp(0), in_=pb[0][:, :], func=AF.Silu,
                                  scale=v_sgate[:, hc:hc + 1])
                    Sc.activation(out=utmp(0), in_=pb[1][:, :], func=AF.Copy,
                                  scale=v_sup[:, hc:hc + 1])
                    Sc.copy(arow[0:1, 24:25], c_cv[0:1, 0:1]) \
                        .then_inc(sGuE.h, 1)
                prog.on("scalar", _ge)
                sGuE.bump(1)
                prog.wait("vector", sGuE, sGuE.n)

                def _gu(V, hc=hc):
                    V.tensor_mul(gu_sl(hc), gtmp(0), utmp(0)).then_inc(sGu.h, 1)
                prog.on("vector", _gu)
                sGu.bump(1)
                promise("bank0", sGu)
                promise("bank1", sGu)
                promise(f"wg{sp}", sGu)
                promise(f"wu{sp}", sGu)
            te_bank(4)
            prog.wait("tensor", sGu, sGu.n)

            def _gsmm(T):
                for hc in range(HOC):
                    mm = T.matmul(pb[4][0:1, :], c_ones_col_bf[:], gu_sl(hc),
                                  start=(hc == 0), stop=(hc == HOC - 1))
                    if hc == HOC - 1:
                        mm.then_inc(sGsum.h, 1)
            prog.on("tensor", _gsmm)
            sGsum.bump(1)
            prog.wait("scalar", sGsum, sGsum.n)

            def _gsrow(Sc):
                Sc.copy(p3_rows[0:1, 512:1024], pb[4][0:1, :])
                Sc.copy(arow[0:1, 24:25], c_cv[0:1, 0:1])
                Sc.copy(arow[0:1, 24:25], c_cv[0:1, 0:1]).then_inc(sGsum.h, 1)
            prog.on("scalar", _gsrow)
            sGsum.bump(1)
            promise("bank4", sGsum)
            for doc in range(KC):
                sp = doc % 2
                pwait("gpsimd", f"wd{sp}")
                tgtd = sWd[sp].bump(16)
                prog.on("gpsimd", lambda G, doc=doc, sp=sp:
                        G.dma_start(out=wd_sl(sp),
                                    in_=wd8[doc * P:(doc + 1) * P, :])
                        .then_inc(sWd[sp].h, 16))
                b = 2 + doc % 2
                te_bank(b)
                prog.wait("tensor", sWd[sp], tgtd)
                if doc == 0:
                    prog.wait("tensor", sGsum, sGsum.n)

                def _dmm(T, doc=doc, sp=sp, b=b):
                    for hc in range(HOC):
                        T.matmul(pb[b][:, :], wd_sl(sp)[:, hc * 128:(hc + 1) * 128],
                                 gu_sl(hc), start=(hc == 0), stop=False)
                    mm = T.matmul(pb[b][:, :], c_neg128_row[0:1, :],
                                  p3_rows[0:1, 512:1024], start=False, stop=True)
                    mm.then_inc(sDnM.h, 1)
                prog.on("tensor", _dmm)
                sDnM.bump(1)
                prog.wait("vector", sDnM, sDnM.n)

                def _r2(V, doc=doc, swl=swl, b=b):
                    V.scalar_tensor_tensor(r2_swp(doc), pb[b][:, :],
                                           v_sdown[:, doc:doc + 1],
                                           x1_cur(doc, swl * 512, 512),
                                           ALU.mult, ALU.add)
                    V.memset(alpha[:, 7:8], 0.0)
                    V.memset(alpha[:, 7:8], 0.0).then_inc(sR2v.h, 1)
                prog.on("vector", _r2)
                sR2v.bump(1)
                promise(f"bank{b}", sR2v)
                promise(f"wd{sp}", sR2v)
                prog.wait("scalar", sR2v, sR2v.n)
                prog.wait("scalar", sLn2M, sLn2M.n)

                def _r2bf(Sc, doc=doc):
                    Sc.activation(out=gtmp(1), in_=r2_swp(doc), func=AF.Copy)
                    Sc.copy(arow[0:1, 24:25], c_cv[0:1, 0:1])
                    Sc.copy(arow[0:1, 24:25], c_cv[0:1, 0:1]) \
                        .then_inc(sR2v.h, 1)
                prog.on("scalar", _r2bf)
                sR2v.bump(1)
                prog.wait("vector", sR2v, sR2v.n)

                def _r2sq(V):
                    V.tensor_mul(utmp(1), gtmp(1), gtmp(1))
                    V.memset(alpha[:, 7:8], 0.0)
                    V.memset(alpha[:, 7:8], 0.0).then_inc(sR2v.h, 1)
                prog.on("vector", _r2sq)
                sR2v.bump(1)
                if doc == 0:
                    te_bank(4)
                    te_bank(5)
                prog.wait("tensor", sR2v, sR2v.n)

                def _ln2mm(T, doc=doc):
                    T.matmul(pb[4][0:1, :], c_ones_col_bf[:], gtmp(1),
                             start=(doc == 0), stop=(doc == KC - 1))
                    mm = T.matmul(pb[5][0:1, :], c_ones_col_bf[:], utmp(1),
                                  start=(doc == 0), stop=(doc == KC - 1))
                    mm.then_inc(sLn2M.h, 1)
                prog.on("tensor", _ln2mm)
                sLn2M.bump(1)
            prog.wait("vector", sLn2M, sLn2M.n)

            def _ln2r(V):
                V.tensor_scalar(rowB[0:1, 0:512], pb[4][0:1, :], 1.0 / 2048.0,
                                0.0, ALU.mult, ALU.add)
                V.tensor_mul(rowB[0:1, 512:1024], rowB[0:1, 0:512],
                             rowB[0:1, 0:512])
                V.scalar_tensor_tensor(rowB[0:1, 512:1024], pb[5][0:1, :],
                                       1.0 / 2048.0, rowB[0:1, 512:1024],
                                       ALU.mult, ALU.subtract)
                V.memset(alpha[:, 7:8], 0.0)
                V.memset(alpha[:, 7:8], 0.0).then_inc(sLn2R.h, 1)
            prog.on("vector", _ln2r)
            sLn2R.bump(1)
            promise("bank4", sLn2R)
            promise("bank5", sLn2R)
            prog.wait("scalar", sLn2R, sLn2R.n)

            def _ln2sq(Sc):
                Sc.activation(out=rowB[0:1, 512:1024], in_=rowB[0:1, 512:1024],
                              func=AF.Sqrt, bias=c_cv[0:1, 4:5])
                Sc.copy(arow[0:1, 24:25], c_cv[0:1, 0:1])
                Sc.copy(arow[0:1, 24:25], c_cv[0:1, 0:1]).then_inc(sLn2R.h, 1)
            prog.on("scalar", _ln2sq)
            sLn2R.bump(1)
            prog.wait("vector", sLn2R, sLn2R.n)

            def _ln2f(V):
                V.reciprocal(rowB[0:1, 512:1024], rowB[0:1, 512:1024])
                V.tensor_mul(rowB[0:1, 0:512], rowB[0:1, 0:512],
                             rowB[0:1, 512:1024])
                V.memset(alpha[:, 7:8], 0.0)
                V.memset(alpha[:, 7:8], 0.0).then_inc(sLn2R.h, 1)
            prog.on("vector", _ln2f)
            sLn2R.bump(1)
            for i, (src0, dstt) in enumerate([(512, p3_s2bc), (0, p3_m2bc)]):
                b = 4 + i
                te_bank(b)
                prog.wait("tensor", sLn2R, sLn2R.n)
                prog.on("tensor", lambda T, src0=src0, b=b:
                        T.matmul(pb[b][:, :], c_ones_row[0:1, :],
                                 rowB[0:1, src0:src0 + 512],
                                 start=True, stop=True).then_inc(sB3.h, 1))
                sB3.bump(1)
                prog.wait("vector", sB3, sB3.n)
                prog.on("vector", lambda V, dstt=dstt, b=b:
                        V.tensor_copy(dstt, pb[b][:, :]).then_inc(sB3.h, 1))
                sB3.bump(1)
                promise(f"bank{b}", sB3)
            for doc in range(KC):
                op = doc % 2
                prog.wait("vector", sB3, sB3.n)
                prog.wait("vector", sOut, max(0, sOut.n - 16))

                def _of(V, doc=doc, op=op):
                    V.tensor_sub(out_sl(op), r2_swp(doc), p3_m2bc)
                    V.tensor_mul(out_sl(op), out_sl(op), p3_s2bc)
                    V.memset(alpha[:, 7:8], 0.0)
                    V.memset(alpha[:, 7:8], 0.0).then_inc(sR2v.h, 1)
                prog.on("vector", _of)
                sR2v.bump(1)
                prog.wait("sync", sR2v, sR2v.n)
                sOut.bump(16)
                prog.on("sync", lambda E, doc=doc, sw=sw, op=op:
                        E.dma_start(out=yout[doc * P:(doc + 1) * P,
                                             sw * 512:(sw + 1) * 512],
                                    in_=out_sl(op)).then_inc(sOut.h, 16))

        prog.wait("sync", sOut, sOut.n)
        if debug:
            prog.wait("sync", sems["sDbg"], sems["sDbg"].n)
            prog.wait("sync", sems["sDbgM"], sems["sDbgM"].n)

        with nc.Block() as block:
            @block.sync
            def _(E):
                for fn in prog.streams["sync"]:
                    fn(E)

            @block.scalar
            def _(E):
                for fn in prog.streams["scalar"]:
                    fn(E)

            @block.vector
            def _(E):
                for fn in prog.streams["vector"]:
                    fn(E)

            @block.tensor
            def _(E):
                for fn in prog.streams["tensor"]:
                    fn(E)

            @block.gpsimd
            def _(E):
                for fn in prog.streams["gpsimd"]:
                    fn(E)

    return nc


def _consts():
    bf = ml_dtypes.bfloat16
    t = np.arange(P, dtype=np.float32)
    c = {}
    c["ident"] = np.eye(P, dtype=bf)
    c["diffT"] = (t[None, :] - t[:, None]).astype(bf)
    c["causal"] = (t[None, :] >= t[:, None]).astype(bf)
    c["iota_row"] = t[None, :].astype(bf)
    c["irev_col"] = (P - t)[:, None].copy()
    c["GIc"] = (t[:, None] + 128.0 * np.arange(H, dtype=np.float32)[None, :]).copy()
    c["J128"] = np.ascontiguousarray(
        np.broadcast_to(128.0 * np.arange(H, dtype=np.float32)[None, :],
                        (P, H)).astype(bf))
    c["ones_row_f"] = np.ones((1, P), np.float32)
    c["neg128_row_f"] = np.full((1, P), -128.0, np.float32)
    c["ones_col_bf"] = np.ones((P, 1), bf)
    c["ones_col_f"] = np.ones((P, 1), np.float32)
    cv = np.zeros((P, 8), np.float32)
    cv[:, 0] = -128.0
    cv[:, 1] = 1.0 / 2048.0
    cv[:, 2] = EPS_RMS
    cv[:, 3] = EPS_RMS * EPS_R1
    cv[:, 4] = EPS_LN
    c["cvals"] = cv
    return c


def _pack_pc(v, ncol):
    return np.ascontiguousarray(np.asarray(v, np.float32).reshape(ncol, P).T)


def kernel(L0_attn_x, L0_attn_qkv_w, L0_attn_qkv_norm_w, L0_attn_proj_w,
           L0_attn_proj_norm_w, L0_attn_norm_w, L0_attn_s,
           L0_ffn_wq_gate, L0_ffn_s_gate, L0_ffn_wq_up, L0_ffn_s_up,
           L0_ffn_wq_down, L0_ffn_s_down):
    x = np.asarray(L0_attn_x, dtype=np.float32)
    wq = np.asarray(L0_attn_qkv_w, dtype=np.float32)
    wp = np.asarray(L0_attn_proj_w, dtype=np.float32)
    wg = np.asarray(L0_ffn_wq_gate, dtype=np.int32)
    wu = np.asarray(L0_ffn_wq_up, dtype=np.int32)
    wd = np.asarray(L0_ffn_wq_down, dtype=np.int32)

    wqT = np.ascontiguousarray(wq.T)
    wpT = np.ascontiguousarray(wp.T)
    wgT = np.ascontiguousarray(
        wg.reshape(HOC, P, KC, P).transpose(0, 3, 2, 1).reshape(HID, D))
    wuT = np.ascontiguousarray(
        wu.reshape(HOC, P, KC, P).transpose(0, 3, 2, 1).reshape(HID, D))
    wdT = np.ascontiguousarray(
        wd.reshape(KC, P, HOC, P).transpose(0, 3, 2, 1).reshape(D, HID))

    shared = dict(
        wqT=wqT,
        wqn_pc=_pack_pc(L0_attn_qkv_norm_w, KC),
        w1_pc=_pack_pc(L0_attn_norm_w, KC),
        w2_pc=_pack_pc(L0_attn_proj_norm_w, KC),
        sdown_pc=_pack_pc(L0_ffn_s_down, KC),
        sgate_pc=_pack_pc(L0_ffn_s_gate, HOC),
        sup_pc=_pack_pc(L0_ffn_s_up, HOC),
        s_tiled=np.ascontiguousarray(
            np.broadcast_to(np.asarray(L0_attn_s, np.float32)[None, :], (P, H))),
        **_consts(),
    )

    in_maps = []
    for c in range(8):
        seq, half = c // 2, c % 2
        xs = x[seq, half * TOK:(half + 1) * TOK, :]
        m = dict(shared)
        m["xT"] = np.ascontiguousarray(xs.T)
        m["wq_scan"] = np.ascontiguousarray(wqT[:, c * 768:(c + 1) * 768])
        m["wp_scan"] = np.ascontiguousarray(wpT[:, c * 256:(c + 1) * 256])
        m["wgpre"] = np.ascontiguousarray(wgT[c * (HID // 8):(c + 1) * (HID // 8)])
        m["wupre"] = np.ascontiguousarray(wuT[c * (HID // 8):(c + 1) * (HID // 8)])
        m["wdpre"] = np.ascontiguousarray(wdT[c * (D // 8):(c + 1) * (D // 8)])
        m["corr_gate"] = np.full((P, 1), 1.0 if half == 1 else 0.0, np.float32)
        oh = np.zeros((P, 8), np.float32)
        oh[:, c & ~1] = 1.0
        m["onehot"] = oh
        in_maps.append(m)

    nc = build_program(debug=False)
    res = run_bass_kernel_spmd(nc, in_maps, core_ids=list(range(8)))

    out = np.empty((4, 4096, D), np.float32)
    for c in range(8):
        seq, half = c // 2, c % 2
        yc = res.results[c]["y"]
        out[seq, half * TOK:(half + 1) * TOK, :] = yc.T
    return out


# version 19
# speedup vs baseline: 1.0092x; 1.0092x over previous
import math
from contextlib import ExitStack

import numpy as np
import ml_dtypes

import concourse.bass as bass
import concourse.mybir as mybir
from concourse.bass_utils import run_bass_kernel_spmd

F32 = mybir.dt.float32
BF16 = mybir.dt.bfloat16
I32 = mybir.dt.int32
U8 = mybir.dt.uint8
AF = mybir.ActivationFunctionType
ALU = mybir.AluOpType
AX = mybir.AxisListType

P = 128
TOK = 2048
D = 2048
KC = D // P
H = 16
CH = TOK // P
HID = 5632
HOC = HID // P
EPS_RMS = float(np.finfo(np.float32).eps)
EPS_R1 = 1e-6
EPS_LN = 1e-5
DINV = 1.0 / math.sqrt(P)

ENGINES = ("sync", "scalar", "vector", "tensor", "gpsimd")


class Sem:
    def __init__(self, nc, es, name):
        self.h = es.enter_context(nc.semaphore(name))
        self.n = 0

    def bump(self, k=1):
        self.n += k
        return self.n


class Prog:
    def __init__(self):
        self.streams = {e: [] for e in ENGINES}

    def on(self, eng, fn):
        self.streams[eng].append(fn)

    def wait(self, eng, sem, n=None):
        n = sem.n if n is None else n
        if n <= 0:
            return
        self.on(eng, lambda E, h=sem.h, n=n: E.wait_ge(h, n))


def build_program(debug=False, phases=3, p1_heads=16, p1_relay=2):
    nc = bass.Bass("TRN2", target_bir_lowering=False, debug=False)

    def din(name, shape, dtype):
        return nc.dram_tensor(name, shape, dtype, kind="ExternalInput").ap()

    xT = din("xT", [D, TOK], F32)
    wqT = din("wqT", [D, 3 * D], F32)
    wgpre = din("wgpre", [HID // 8, D], I32)
    wupre = din("wupre", [HID // 8, D], I32)
    wdpre = din("wdpre", [D // 8, HID], I32)
    wq_scan = din("wq_scan", [D, 768], F32)
    wp_scan = din("wp_scan", [D, 256], F32)
    wqn_pc = din("wqn_pc", [P, KC], F32)
    w1_pc = din("w1_pc", [P, KC], F32)
    w2_pc = din("w2_pc", [P, KC], F32)
    sdown_pc = din("sdown_pc", [P, KC], F32)
    sgate_pc = din("sgate_pc", [P, HOC], F32)
    sup_pc = din("sup_pc", [P, HOC], F32)
    s_tiled = din("s_tiled", [P, H], F32)
    ident = din("ident", [P, P], BF16)
    diffT = din("diffT", [P, P], BF16)
    causal = din("causal", [P, P], BF16)
    iota_row = din("iota_row", [1, P], BF16)
    irev_col = din("irev_col", [P, 1], F32)
    GIc = din("GIc", [P, H], F32)
    J128 = din("J128", [P, H], BF16)
    ones_row_f = din("ones_row_f", [1, P], F32)
    neg128_row_f = din("neg128_row_f", [1, P], F32)
    ones_col_bf = din("ones_col_bf", [P, 1], BF16)
    ones_col_f = din("ones_col_f", [P, 1], F32)
    corr_gate = din("corr_gate", [P, 1], F32)
    onehot = din("onehot", [P, 8], F32)
    cvals = din("cvals", [P, 8], F32)

    yout = nc.dram_tensor("y", [D, TOK], F32, kind="ExternalOutput").ap()

    x1h2 = nc.dram_tensor("x1h2", [P, KC * 1024], BF16, kind="Internal").ap()
    wpbf_loc = nc.dram_tensor("wpbf_loc", [2 * P, D], BF16, kind="Internal").ap()
    wpbf = nc.dram_tensor("wpbf", [D, D], BF16, kind="Internal",
                          addr_space="Shared").ap()
    wg8l = nc.dram_tensor("wg8l", [HID // 8, D], U8, kind="Internal").ap()
    wu8l = nc.dram_tensor("wu8l", [HID // 8, D], U8, kind="Internal").ap()
    wd8l = nc.dram_tensor("wd8l", [D // 8, HID], U8, kind="Internal").ap()
    wg8 = nc.dram_tensor("wg8", [HID, D], U8, kind="Internal",
                         addr_space="Shared").ap()
    wu8 = nc.dram_tensor("wu8", [HID, D], U8, kind="Internal",
                         addr_space="Shared").ap()
    wd8 = nc.dram_tensor("wd8", [D, HID], U8, kind="Internal",
                         addr_space="Shared").ap()
    cc_in_a = nc.dram_tensor("cc_in_a", [1, 2], F32, kind="Internal")
    cc_out_a = nc.dram_tensor("cc_out_a", [1, 2], F32, kind="Internal",
                              addr_space="Shared")
    cc_in_h = [nc.dram_tensor(f"cc_in_h{h}", [P, P], BF16, kind="Internal")
               for h in range(H)]
    cc_out_h = [nc.dram_tensor(f"cc_out_h{h}", [8 * P, P], BF16, kind="Internal",
                               addr_space="Shared") for h in range(H)]
    dbg = {}
    if debug:
        dbg["attn"] = nc.dram_tensor("dbg_attn", [P, KC * TOK], BF16,
                                     kind="ExternalOutput").ap()
        dbg["x1a"] = nc.dram_tensor("dbg_x1a", [P, 20480], BF16,
                                    kind="ExternalOutput").ap()
        dbg["x1b"] = nc.dram_tensor("dbg_x1b", [P, 12288], BF16,
                                    kind="ExternalOutput").ap()
        dbg["h"] = nc.dram_tensor("dbg_h", [P, KC * TOK], BF16,
                                  kind="ExternalOutput").ap()
        dbg["misc"] = nc.dram_tensor("dbg_misc", [P, 12], F32,
                                     kind="ExternalOutput").ap()
        dbg["rho"] = nc.dram_tensor("dbg_rho", [P, 2048], F32,
                                    kind="ExternalOutput").ap()
        dbg["wp"] = nc.dram_tensor("dbg_wp", [P, 512], BF16,
                                   kind="ExternalOutput").ap()
        dbg["scan"] = nc.dram_tensor("dbg_scan", [P, 32], F32,
                                     kind="ExternalOutput").ap()

    es = ExitStack()
    with es:
        E = es.enter_context

        R1 = E(nc.sbuf_tensor("R1", [P, 44032], BF16))
        R2 = E(nc.sbuf_tensor("R2", [P, 32768], BF16))
        R3 = E(nc.sbuf_tensor("R3", [P, 22528], BF16))

        c_ident = E(nc.sbuf_tensor("c_ident", [P, P], BF16))
        c_diffT = E(nc.sbuf_tensor("c_diffT", [P, P], BF16))
        c_causal = E(nc.sbuf_tensor("c_causal", [P, P], BF16))
        c_iota_row = E(nc.sbuf_tensor("c_iota_row", [1, P], BF16))
        c_irev = E(nc.sbuf_tensor("c_irev", [P, 1], F32))
        c_GI = E(nc.sbuf_tensor("c_GI", [P, H], F32))
        c_J128 = E(nc.sbuf_tensor("c_J128", [P, H], BF16))
        c_ones_row = E(nc.sbuf_tensor("c_ones_row", [1, P], F32))
        c_neg128_row = E(nc.sbuf_tensor("c_neg128_row", [1, P], F32))
        c_ones_col_bf = E(nc.sbuf_tensor("c_ones_col_bf", [P, 1], BF16))
        c_ones_col_f = E(nc.sbuf_tensor("c_ones_col_f", [P, 1], F32))
        c_gate = E(nc.sbuf_tensor("c_gate", [P, 1], F32))
        c_onehot = E(nc.sbuf_tensor("c_onehot", [P, 8], F32))
        c_cv = E(nc.sbuf_tensor("c_cv", [P, 8], F32))
        c_s = E(nc.sbuf_tensor("c_s", [P, H], F32))
        c_negs = E(nc.sbuf_tensor("c_negs", [P, H], F32))
        c_lamT = E(nc.sbuf_tensor("c_lamT", [P, H], F32))
        v_wqn = E(nc.sbuf_tensor("v_wqn", [P, KC], F32))
        v_w1 = E(nc.sbuf_tensor("v_w1", [P, KC], F32))
        v_w2 = E(nc.sbuf_tensor("v_w2", [P, KC], F32))
        v_w12 = E(nc.sbuf_tensor("v_w12", [P, KC], F32))
        v_sdown = E(nc.sbuf_tensor("v_sdown", [P, KC], F32))
        v_sgate = E(nc.sbuf_tensor("v_sgate", [P, HOC], F32))
        v_sup = E(nc.sbuf_tensor("v_sup", [P, HOC], F32))
        v_stat = E(nc.sbuf_tensor("v_stat", [P, KC], F32))
        v_tmp = E(nc.sbuf_tensor("v_tmp", [P, KC], F32))
        v_shl = E(nc.sbuf_tensor("v_shl", [P, 2 * KC], BF16))
        alpha = E(nc.sbuf_tensor("alpha", [P, 8], F32))
        rowA = E(nc.sbuf_tensor("rowA", [1, 1024], F32))
        rowB = E(nc.sbuf_tensor("rowB", [1, 1024], F32))
        arow = E(nc.sbuf_tensor("arow", [1, 32], F32))
        acc2 = E(nc.sbuf_tensor("acc2", [P, 2], F32))

        pb = [E(nc.psum_tensor(f"pb{i}", [P, 512], F32)) for i in range(6)]
        ptA = E(nc.psum_tensor("ptA", [P, 1024], BF16))
        ptB = E(nc.psum_tensor("ptB", [P, 1024], BF16))

        prog = Prog()
        sems = {}

        def S(name):
            if name not in sems:
                sems[name] = Sem(nc, es, name)
            return sems[name]

        def r1f(lo, n):
            return R1[:, lo:lo + 2 * n].bitcast(F32)

        def r2f(lo, n):
            return R2[:, lo:lo + 2 * n].bitcast(F32)

        def r3f(lo, n):
            return R3[:, lo:lo + 2 * n].bitcast(F32)

        def r3i(lo, n):
            return R3[:, lo:lo + 2 * n].bitcast(I32)

        h_fm = lambda kc: R1[:, kc * TOK:(kc + 1) * TOK]
        xstg = lambda p: r1f(32768 + p * 4096, 2048)
        q_fm = lambda p: R1[:, 32768 + p * 2048: 32768 + (p + 1) * 2048]
        qt_fm = R1[:, 36864:38912]
        k_fm = R1[:, 38912:40960]
        sbf = lambda p: R1[:, 40960 + p * 128: 40960 + (p + 1) * 128]
        s_stage = R1[:, 41216:41344]
        scT = lambda p: R1[:, 41344 + p * 128: 41344 + (p + 1) * 128]
        mask_t = lambda p: R1[:, 41600 + p * 128: 41600 + (p + 1) * 128]
        lam_t = lambda p: R1[:, 41856 + p * 128: 41856 + (p + 1) * 128]
        ktm = lambda p: R1[:, 42112 + p * 128: 42112 + (p + 1) * 128]
        ssel_bf = R1[:, 42368:42496]
        lam2 = lambda p: r1f(42496 + p * 32, 16)
        lamTp = lambda p: r1f(42624 + p * 32, 16)
        lrev_col = lambda p: r1f(42752 + p * 8, 1)
        wpsl = lambda p: R1[:, p * 2048:(p + 1) * 2048]
        p2_r1bf = lambda p: R1[:, 4096 + p * 512: 4096 + (p + 1) * 512]
        p2_r1sq = lambda p: R1[:, 5120 + p * 512: 5120 + (p + 1) * 512]
        p2_sq = lambda p: R1[:, 6144 + p * 512: 6144 + (p + 1) * 512]
        p2_mubc = r1f(7168, 512)
        p2_sgbc = r1f(8192, 512)
        p2_xstg = lambda p: r1f(9216 + p * 1024, 512)
        r1_swp = lambda doc: r1f(11264 + doc * 1024, 512)
        rho12_bc = r1f(27648, 2048)
        gu_sl = lambda hc: R1[:, hc * 512:(hc + 1) * 512]
        p3_rows = r1f(22528, 2048)

        xbf = lambda kc: R2[:, kc * TOK:(kc + 1) * TOK]
        attn = lambda ec: R2[:, ec * TOK:(ec + 1) * TOK]
        r2_swp = lambda doc: r2f(doc * 1024, 512)
        wg_sl = lambda p: R2[:, 16384 + p * 2048: 16384 + (p + 1) * 2048]
        wu_sl = lambda p: R2[:, 20480 + p * 2048: 20480 + (p + 1) * 2048]
        out_sl = lambda p: r2f(24576 + p * 1024, 512)
        gtmp = lambda p: R2[:, 26624 + p * 512: 26624 + (p + 1) * 512]
        utmp = lambda p: R2[:, 27648 + p * 512: 27648 + (p + 1) * 512]
        p3_m2bc = r2f(28672, 512)
        p3_s2bc = r2f(29696, 512)
        p3_m1bc = r2f(30720, 512)
        p3_s1bc = r2f(31744, 512)

        p0stg = lambda p: r3f(p * 2048, 1024)
        rho_bc = r3f(4096, 2048)
        xsq = R3[:, 20480:22528]
        wq_sl = lambda p: R3[:, p * 4096: p * 4096 + 2048]
        wk_sl = lambda p: R3[:, p * 4096 + 2048: p * 4096 + 4096]
        wv_sl = R3[:, 8192:12288]
        wstg = lambda p: r3f(12288 + p * 1024, 512)
        wi32 = r3i(14336, 512)
        sall = R3[:, 15360:16384]
        v_tm = R3[:, 16384:20480]

        def x1_sl(fc, half, t0, n):
            if fc < 10:
                base = half * 10240 + fc * 1024 + t0
                return R3[:, base:base + n]
            base = 31744 + half * 6144 + (fc - 10) * 1024 + t0
            return R1[:, base:base + n]

        def x1_cur(fc, t0, n):
            if fc < 10:
                return R3[:, fc * 1024 + t0: fc * 1024 + t0 + n]
            base = 31744 + (fc - 10) * 1024 + t0
            return R1[:, base:base + n]

        wd_sl = lambda p: R3[:, 10240 + p * 5632: 10240 + (p + 1) * 5632]

        promises = {}

        def promise(name, sem):
            promises[name] = (sem, sem.n)

        def pwait(eng, name):
            if name in promises:
                sem, n = promises[name]
                prog.wait(eng, sem, n)

        def te_bank(i):
            pwait("tensor", f"bank{i}")

        def te_pt(i):
            pwait("tensor", f"pt{i}")

        Lc = S("Lc")
        for t, d in [
            (c_ident, ident), (c_diffT, diffT), (c_causal, causal),
            (c_iota_row, iota_row), (c_irev, irev_col), (c_GI, GIc),
            (c_J128, J128), (c_ones_row, ones_row_f), (c_neg128_row, neg128_row_f),
            (c_ones_col_bf, ones_col_bf), (c_ones_col_f, ones_col_f),
            (c_gate, corr_gate), (c_onehot, onehot), (c_cv, cvals), (c_s, s_tiled),
            (v_wqn, wqn_pc), (v_w1, w1_pc), (v_w2, w2_pc), (v_sdown, sdown_pc),
            (v_sgate, sgate_pc), (v_sup, sup_pc),
        ]:
            Lc.bump(16)
            prog.on("scalar", lambda E, t=t, d=d:
                    E.dma_start(out=t[:], in_=d).then_inc(Lc.h, 16))
        LC_ALL = Lc.n

        sPrep = S("sPrep")
        prog.wait("vector", Lc, LC_ALL)

        def _prep(V):
            V.tensor_mul(v_stat[:], v_w1[:], v_w1[:])
            V.tensor_scalar_mul(c_negs[:], c_s[:], -1.0)
            V.tensor_mul(v_w12[:], v_w1[:], v_w2[:])
            V.tensor_scalar(v_stat[:], v_stat[:], 1.0 / 2048.0, EPS_RMS / 2048.0,
                            ALU.mult, ALU.add)
            V.memset(alpha[:, 7:8], 0.0)
            V.memset(alpha[:, 7:8], 0.0)
            V.tensor_copy(v_shl[:, 0:KC], v_stat[:])
            V.memset(alpha[:, 7:8], 0.0)
            V.memset(alpha[:, 7:8], 0.0)
            V.tensor_copy(v_tmp[:], v_shl[:, 0:KC])
            V.memset(alpha[:, 7:8], 0.0)
            V.memset(alpha[:, 7:8], 0.0)
            V.tensor_sub(v_tmp[:], v_stat[:], v_tmp[:])
            V.memset(alpha[:, 7:8], 0.0)
            V.memset(alpha[:, 7:8], 0.0)
            V.tensor_copy(v_shl[:, KC:2 * KC], v_tmp[:])
            V.memset(alpha[:, 7:8], 0.0)
            V.memset(alpha[:, 7:8], 0.0).then_inc(sPrep.h, 1)
        prog.on("vector", _prep)
        sPrep.bump(1)
        prog.wait("scalar", Lc, LC_ALL)
        prog.on("scalar", lambda Sc: Sc.activation(out=c_lamT[:], in_=c_s[:],
                                                   func=AF.Exp, scale=c_cv[:, 0:1])
                .then_inc(sPrep.h, 1))
        sPrep.bump(1)
        PREP_ALL = sPrep.n

        Lsc = S("Lsc")
        sAbs = S("sAbs")
        prog.wait("vector", Lc, LC_ALL)
        prog.on("vector", lambda V: V.memset(acc2[:], 0.0))
        for i in range(2 * KC):
            is_p = i >= KC
            kc = i % KC
            p = i % 2
            col = 1 if is_p else 0
            src = wp_scan if is_p else wq_scan
            w = 256 if is_p else 768
            if i >= 2:
                prog.wait("gpsimd", sAbs, i - 1)
            tgt = Lsc.bump(16)
            prog.on("gpsimd", lambda G, kc=kc, p=p, src=src, w=w:
                    G.dma_start(out=p0stg(p)[:, 0:w], in_=src[kc * P:(kc + 1) * P, :])
                    .then_inc(Lsc.h, 16))
            prog.wait("vector", Lsc, tgt)

            def _red(V, i=i, p=p, w=w):
                V.reduce_sum(r1f(2 * i, 1), p0stg(p)[:, 0:w], axis=AX.X,
                             apply_absolute_value=True).then_inc(sAbs.h, 1)
            prog.on("vector", _red)
            sAbs.bump(1)
            if i >= 1:
                colp = 1 if (i - 1) >= KC else 0

                def _acc(V, j=i - 1, colp=colp):
                    V.memset(alpha[:, 7:8], 0.0)
                    V.tensor_add(acc2[:, colp:colp + 1], acc2[:, colp:colp + 1],
                                 r1f(2 * j, 1))
                prog.on("vector", _acc)
        ilast = 2 * KC - 1

        def _accf(V, j=ilast):
            V.memset(xsq[:, 0:1024], 0.0)
            V.memset(xsq[:, 1024:2048], 0.0)
            V.tensor_add(acc2[:, 1:2], acc2[:, 1:2], r1f(2 * j, 1))
            V.memset(xsq[:, 0:1024], 0.0)
            V.memset(xsq[:, 1024:2048], 0.0).then_inc(sAbs.h, 1)
        prog.on("vector", _accf)
        sAbs.bump(1)
        ABS_ALL = sAbs.n
        if debug:
            sDbg0 = S("sDbg")
            prog.wait("sync", sAbs, ABS_ALL)
            prog.on("sync", lambda E: E.dma_start(out=dbg["scan"], in_=r1f(0, 32))
                    .then_inc(sDbg0.h, 16))
            sDbg0.bump(16)
            prog.wait("vector", sDbg0, sDbg0.n)

        sMMa = S("sMMa")
        sStgA = S("sStgA")
        te_bank(5)
        prog.wait("tensor", sAbs, ABS_ALL)
        prog.wait("tensor", Lc, LC_ALL)

        def _psum_a(T):
            T.matmul(pb[5][0:1, 0:1], acc2[:, 0:1], c_ones_col_f[:],
                     start=True, stop=True)
            T.matmul(pb[5][0:1, 1:2], acc2[:, 1:2], c_ones_col_f[:],
                     start=True, stop=True).then_inc(sMMa.h, 1)
        prog.on("tensor", _psum_a)
        sMMa.bump(1)
        sCpA = S("sCpA")
        prog.wait("vector", sMMa, 1)

        def _cpA(V):
            V.tensor_copy(arow[0:1, 8:10], pb[5][0:1, 0:2])
            V.memset(arow[0:1, 30:31], 0.0)
            V.memset(arow[0:1, 30:31], 0.0).then_inc(sCpA.h, 1)
        prog.on("vector", _cpA)
        sCpA.bump(1)
        promise("bank5", sCpA)
        prog.wait("gpsimd", sCpA, 1)
        prog.wait("gpsimd", S("sXC"), 2)
        prog.on("gpsimd", lambda G: G.dma_start(out=cc_in_a.ap(),
                                                in_=arow[0:1, 8:10])
                .then_inc(sStgA.h, 16))
        sStgA.bump(16)
        ccA = S("ccA")
        prog.wait("gpsimd", sStgA, 16)
        prog.on("gpsimd", lambda G: G.collective_compute(
            "AllReduce", ALU.add, replica_groups=[list(range(8))],
            ins=[cc_in_a.ap().opt()], outs=[cc_out_a.ap().opt()]).then_inc(ccA.h, 1))
        ccA.bump(1)

        Lx = S("Lx")
        sXC = S("sXC")
        sXMM = S("sXMM")
        for kc in range(KC):
            p = kc % 2
            if kc >= 2:
                prog.wait("sync", sXC, (kc - 1) * 2)
                prog.wait("sync", sXMM, kc - 1)
            tgt = Lx.bump(16)
            prog.on("sync", lambda E, kc=kc, p=p:
                    E.dma_start(out=xstg(p), in_=xT[kc * P:(kc + 1) * P, :])
                    .then_inc(Lx.h, 16))
            prog.wait("vector", Lx, tgt)
            if kc >= 1:
                prog.wait("vector", sXMM, sXMM.n)
            prog.on("vector", lambda V, p=p:
                    V.tensor_mul(xsq, xstg(p), xstg(p)).then_inc(sXC.h, 1))
            sXC.bump(1)
            prog.wait("scalar", Lx, tgt)
            if kc == 0:
                prog.wait("scalar", Lc, LC_ALL)
            prog.on("scalar", lambda Sc, kc=kc, p=p:
                    Sc.activation(out=xbf(kc), in_=xstg(p), func=AF.Copy)
                    .then_inc(sXC.h, 1))
            sXC.bump(1)
            if kc == 0:
                for sw in range(4):
                    te_bank(sw)
                prog.wait("tensor", Lc, LC_ALL)
            prog.wait("tensor", sXC, sXC.n - 1)

            def _xmm(T, kc=kc):
                for sw in range(4):
                    mm = T.matmul(pb[sw][0:1, :], c_ones_col_bf[:],
                                  xsq[:, sw * 512:(sw + 1) * 512],
                                  start=(kc == 0), stop=(kc == KC - 1))
                    if sw == 3:
                        mm.then_inc(sXMM.h, 1)
            prog.on("tensor", _xmm)
            sXMM.bump(1)
        XMM_ALL = sXMM.n

        sRho = S("sRho")
        sRbc = S("sRbc")
        sRev = S("sRev")
        prog.wait("scalar", sXMM, XMM_ALL)
        for batch in range(2):
            if batch == 1:
                prog.wait("scalar", sRbc, 2)

            def _rho_rows(Sc, batch=batch):
                for j in range(2):
                    sw = batch * 2 + j
                    ins = Sc.activation(out=rowA[0:1, j * 512:(j + 1) * 512],
                                        in_=pb[sw][0:1, :], func=AF.Sqrt,
                                        scale=c_cv[0:1, 1:2], bias=c_cv[0:1, 2:3])
                    if j == 0:
                        ins.then_inc(sRho.h, 1)
                Sc.copy(arow[0:1, 24:25], c_cv[0:1, 0:1])
                Sc.copy(arow[0:1, 24:25], c_cv[0:1, 0:1]).then_inc(sRho.h, 1)
            prog.on("scalar", _rho_rows)
            sRho.bump(2)
            promise(f"bank{batch * 2}", sRho)
            promise(f"bank{batch * 2 + 1}", sRho)
            prog.wait("vector", sRho, sRho.n)

            def _rho_rec(V):
                V.reciprocal(rowA[0:1, 0:1024], rowA[0:1, 0:1024])
                V.memset(alpha[:, 7:8], 0.0)
                V.memset(alpha[:, 7:8], 0.0).then_inc(sRho.h, 1)
            prog.on("vector", _rho_rec)
            sRho.bump(1)
            for j in range(2):
                sw = batch * 2 + j
                b = 4 + (sw % 2)
                te_bank(b)
                if j == 0:
                    prog.wait("tensor", sRho, sRho.n)
                prog.on("tensor", lambda T, j=j, b=b:
                        T.matmul(pb[b][:, :], c_ones_row[0:1, :],
                                 rowA[0:1, j * 512:(j + 1) * 512],
                                 start=True, stop=True).then_inc(sRbc.h, 1))
                sRbc.bump(1)
                prog.wait("vector", sRbc, sRbc.n)
                prog.on("vector", lambda V, sw=sw, b=b:
                        V.tensor_copy(rho_bc[:, sw * 512:(sw + 1) * 512], pb[b][:, :])
                        .then_inc(sRev.h, 1))
                sRev.bump(1)
                promise(f"bank{b}", sRev)
        sH = S("sH")
        prog.wait("vector", sRev, sRev.n)
        for kc in range(KC):
            prog.on("vector", lambda V, kc=kc:
                    V.scalar_tensor_tensor(h_fm(kc), xbf(kc), v_wqn[:, kc:kc + 1],
                                           rho_bc, ALU.mult, ALU.mult)
                    .then_inc(sH.h, 1))
            sH.bump(1)
        H_ALL = sH.n

        LdA = S("LdA")
        prog.wait("gpsimd", ccA, 1)
        prog.on("gpsimd", lambda G:
                G.dma_start(out=arow[0:1, 0:2], in_=cc_out_a.ap())
                .then_inc(LdA.h, 16))
        LdA.bump(16)
        sAl = S("sAl")
        prog.wait("vector", LdA, 16)

        def _alpha(V):
            V.tensor_scalar_mul(arow[0:1, 16:17], arow[0:1, 0:1],
                                1.0 / (6144.0 * 2048.0))
            V.tensor_scalar_mul(arow[0:1, 17:18], arow[0:1, 1:2],
                                1.0 / (2048.0 * 2048.0))
            V.memset(arow[0:1, 30:31], 0.0)
            V.tensor_scalar_max(arow[0:1, 16:17], arow[0:1, 16:17], 1e-5)
            V.tensor_scalar_max(arow[0:1, 17:18], arow[0:1, 17:18], 1e-5)
            V.memset(arow[0:1, 30:31], 0.0)
            V.reciprocal(arow[0:1, 18:19], arow[0:1, 16:17])
            V.reciprocal(arow[0:1, 19:20], arow[0:1, 17:18])
            V.memset(arow[0:1, 30:31], 0.0)
            V.tensor_scalar_mul(rowB[0:1, 0:1], arow[0:1, 16:17], DINV)
            V.tensor_copy(rowB[0:1, 1:2], arow[0:1, 16:17])
            V.tensor_copy(rowB[0:1, 2:3], arow[0:1, 18:19])
            V.tensor_copy(rowB[0:1, 3:4], arow[0:1, 17:18])
            V.tensor_copy(rowB[0:1, 4:5], arow[0:1, 19:20])
            V.memset(arow[0:1, 30:31], 0.0)
            V.memset(arow[0:1, 30:31], 0.0).then_inc(sAl.h, 1)
        prog.on("vector", _alpha)
        sAl.bump(1)
        promise("bank5", sAl)
        sMMb = S("sMMb")
        aDone = S("aDone")
        te_bank(5)
        prog.wait("tensor", sAl, 1)
        prog.on("tensor", lambda T: T.matmul(pb[5][:, 0:8], c_ones_row[0:1, :],
                                             rowB[0:1, 0:8], start=True, stop=True)
                .then_inc(sMMb.h, 1))
        sMMb.bump(1)
        prog.wait("vector", sMMb, 1)
        prog.on("vector", lambda V: V.tensor_copy(alpha[:, 0:8], pb[5][:, 0:8])
                .then_inc(aDone.h, 1))
        aDone.bump(1)
        promise("bank5", aDone)
        if debug:
            sDbgM = S("sDbgM")
            prog.wait("sync", aDone, 1)
            prog.on("sync", lambda E: E.dma_start(out=dbg["misc"][:, 0:8],
                                                  in_=alpha[:, 0:8])
                    .then_inc(sDbgM.h, 16))
            sDbgM.bump(16)
            prog.on("sync", lambda E: E.dma_start(out=dbg["misc"][:, 8:10],
                                                  in_=acc2[:, 0:2])
                    .then_inc(sDbgM.h, 16))
            sDbgM.bump(16)
            prog.on("sync", lambda E: E.dma_start(out=dbg["misc"][0:1, 10:12],
                                                  in_=arow[0:1, 0:2])
                    .then_inc(sDbgM.h, 16))
            sDbgM.bump(16)

        def finish_and_emit(waits):
            for eng, sem in waits:
                prog.wait(eng, sem, sem.n)
            with nc.Block() as block:
                @block.sync
                def _(E):
                    for fn in prog.streams["sync"]:
                        fn(E)

                @block.scalar
                def _(E):
                    for fn in prog.streams["scalar"]:
                        fn(E)

                @block.vector
                def _(E):
                    for fn in prog.streams["vector"]:
                        fn(E)

                @block.tensor
                def _(E):
                    for fn in prog.streams["tensor"]:
                        fn(E)

                @block.gpsimd
                def _(E):
                    for fn in prog.streams["gpsimd"]:
                        fn(E)
            return nc

        if phases == 0:
            return finish_and_emit([("sync", sH), ("sync", aDone)])

        sWq = [S("sWq0"), S("sWq1")]
        sWv = S("sWv")
        sStg = [S("sStg0"), S("sStg1")]
        sI32 = S("sI32")
        swctr = [0]

        def stream_weight(dst_ap, col0, ncols, slot_sem, src=None):
            if src is None:
                src = wqT
            total = KC * ncols
            nsub = total // 512
            kcps = 512 // ncols
            for sub in range(nsub):
                sp = swctr[0] % 2
                swctr[0] += 1
                kc0 = sub * kcps
                pwait("gpsimd", f"wstg{sp}")
                tgt = sStg[sp].bump(16)
                prog.on("gpsimd", lambda G, kc0=kc0, kcps=kcps, sp=sp, col0=col0,
                        ncols=ncols, src=src:
                        G.dma_start(
                            out=wstg(sp).rearrange("p (a b) -> p a b", b=ncols),
                            in_=src[kc0 * P:(kc0 + kcps) * P, col0:col0 + ncols]
                            .rearrange("(a p) b -> p a b", p=P))
                        .then_inc(sStg[sp].h, 16))
                prog.wait("scalar", sStg[sp], tgt)
                if swctr[0] <= 2:
                    prog.wait("scalar", aDone, 1)
                pwait("scalar", "wi32")
                prog.on("scalar", lambda Sc, sp=sp:
                        Sc.activation(out=wi32[:, 0:512], in_=wstg(sp),
                                      func=AF.Copy, scale=alpha[:, 2:3])
                        .then_inc(sI32.h, 1))
                sI32.bump(1)
                promise(f"wstg{sp}", sI32)
                prog.wait("vector", sI32, sI32.n)
                o0 = sub * 512
                prog.on("vector", lambda V, dst_ap=dst_ap, o0=o0:
                        V.tensor_scalar(dst_ap[:, o0:o0 + 512], wi32[:, 0:512],
                                        1.0, -1.0, ALU.min, ALU.max)
                        .then_inc(slot_sem.h, 1))
                slot_sem.bump(1)
                promise("wi32", slot_sem)

        def issue_head_weights(h, first=False):
            sl = h % 2
            if first:
                prog.wait("gpsimd", sAbs, ABS_ALL)
                if h == 1:
                    prog.wait("gpsimd", sH, H_ALL)
            pwait("gpsimd", f"wqk{sl}")
            stream_weight(wq_sl(sl), h * P, P, sWq[sl])
            stream_weight(wk_sl(sl), 2048 + h * P, P, sWq[sl])

        def issue_pair_v(g, first=False):
            if first:
                prog.wait("gpsimd", sH, H_ALL)
            pwait("gpsimd", "wv")
            stream_weight(wv_sl, 4096 + g * 256, 256, sWv)

        issue_head_weights(0, first=True)
        if p1_heads > 1:
            issue_head_weights(1, first=True)
        issue_pair_v(0, first=True)

        sQKe = S("sQKe")
        sVe = S("sVe")
        sPrepH = S("sPrepH")
        sPrepHs = S("sPrepHs")
        sScT = S("sScT")
        sIOe = S("sIOe")
        sSup = S("sSup")
        sKtm = S("sKtm")
        sCor = S("sCor")
        sStgCp = S("sStgCp")
        sStgH = S("sStgH")
        ccH = S("ccH")
        sGat = S("sGat")
        sSel = S("sSel")
        sMMq = S("sMMq")
        sLamB = S("sLamB")
        sQt = S("sQt")
        sTr = S("sTr")

        def emit_corrections(hp):
            psl = hp % 2
            if p1_relay != 3:
                prog.wait("gpsimd", ccH, hp + 1)
                prog.wait("gpsimd", sSel, sSel.n)
                tgt = sGat.bump(16)
                prog.on("gpsimd", lambda G, hp=hp:
                        G.dma_start(out=sall[:].rearrange("p (r j) -> p r j", j=P),
                                    in_=cc_out_h[hp].ap()
                                    .rearrange("(r p) j -> p r j", p=P))
                        .then_inc(sGat.h, 16))
                prog.wait("vector", sGat, tgt)
            if p1_relay == 4:
                return
            prog.wait("vector", sStgH, sStgH.n)

            def _sel(V):
                V.tensor_scalar(ssel_bf[:], sall[:, 0:128], c_onehot[:, 0:1],
                                None, ALU.mult)
                for r in range(1, 8):
                    ins = V.scalar_tensor_tensor(
                        ssel_bf[:], sall[:, r * 128:(r + 1) * 128],
                        c_onehot[:, r:r + 1], ssel_bf[:], ALU.mult, ALU.add)
                    if r == 7:
                        ins.then_inc(sSel.h, 1)
            if p1_relay != 6:
                prog.on("vector", _sel)
            else:
                prog.on("vector", lambda V: V.tensor_copy(ssel_bf[:], sall[:, 0:128])
                        .then_inc(sSel.h, 1))
            sSel.bump(1)
            if p1_relay == 5:
                return
            for w in range(2):
                te_bank(2)
                te_bank(3)
                if w == 0:
                    prog.wait("tensor", sSel, sSel.n)

                def _cmm(T, w=w, psl=psl):
                    for j in range(8):
                        c = 8 * w + j
                        b = 2 + j // 4
                        col = (j % 4) * 128
                        mm = T.matmul(pb[b][:, col:col + 128], ssel_bf[:],
                                      q_fm(psl)[:, c * 128:(c + 1) * 128],
                                      start=True, stop=True)
                        if j == 7:
                            mm.then_inc(sCor.h, 1)
                prog.on("tensor", _cmm)
                sCor.bump(1)
                prog.wait("vector", sCor, sCor.n)

                def _cev(V, w=w, hp=hp, psl=psl):
                    for j in range(8):
                        c = 8 * w + j
                        b = 2 + j // 4
                        col = (j % 4) * 128
                        V.scalar_tensor_tensor(s_stage[:], pb[b][:, col:col + 128],
                                               lamTp(psl)[:, c:c + 1], lam_t(psl),
                                               ALU.mult, ALU.mult)
                        ins = V.tensor_add(attn(hp)[:, c * 128:(c + 1) * 128],
                                           attn(hp)[:, c * 128:(c + 1) * 128],
                                           s_stage[:])
                        if j == 7:
                            ins.then_inc(sCor.h, 1)
                prog.on("vector", _cev)
                sCor.bump(1)
                promise("bank2", sCor)
                promise("bank3", sCor)

        for h in range(p1_heads):
            sl = h % 2
            g = h // 2

            if h == 0:
                prog.wait("scalar", Lc, LC_ALL)
                prog.wait("scalar", sPrep, PREP_ALL)
                prog.wait("scalar", sXMM, XMM_ALL)
                prog.wait("scalar", sH, H_ALL)
            prog.wait("scalar", sLamB, sLamB.n)

            def _hb_sc(Sc, h=h, sl=sl):
                Sc.activation(out=lam2(sl), in_=c_GI[:], func=AF.Exp,
                              scale=c_negs[:, h:h + 1])
                Sc.activation(out=lamTp(sl), in_=c_J128[:], func=AF.Exp,
                              scale=c_negs[:, h:h + 1])
                Sc.activation(out=lrev_col(sl), in_=c_irev[:], func=AF.Exp,
                              scale=c_negs[:, h:h + 1])
                Sc.activation(out=mask_t(sl), in_=c_diffT[:], func=AF.Exp,
                              scale=c_negs[:, h:h + 1])
                Sc.activation(out=rowB[0:1, 0:128], in_=c_iota_row[:],
                              func=AF.Exp, scale=c_negs[0:1, h:h + 1])
                Sc.copy(arow[0:1, 24:25], c_cv[0:1, 0:1])
                Sc.copy(arow[0:1, 24:25], c_cv[0:1, 0:1]) \
                    .then_inc(sPrepHs.h, 1)
            prog.on("scalar", _hb_sc)
            sPrepHs.bump(1)
            prog.wait("vector", sPrepHs, sPrepHs.n)
            if h == 0:
                prog.wait("vector", sXMM, XMM_ALL)

            def _hb_ve(V, sl=sl):
                V.tensor_scalar(lam2(sl), lam2(sl), c_gate[:, 0:1], None, ALU.mult)
                V.tensor_scalar(lamTp(sl), lamTp(sl), c_gate[:, 0:1], None, ALU.mult)
                V.tensor_mul(mask_t(sl), mask_t(sl), c_causal[:]) \
                    .then_inc(sPrepH.h, 1)
            prog.on("vector", _hb_ve)
            sPrepH.bump(1)
            te_bank(5)
            prog.wait("tensor", sPrepHs, sPrepHs.n)
            prog.on("tensor", lambda T:
                    T.matmul(pb[5][:, 0:128], c_ones_row[0:1, :],
                             rowB[0:1, 0:128], start=True, stop=True)
                    .then_inc(sLamB.h, 1))
            sLamB.bump(1)
            prog.wait("vector", sLamB, sLamB.n)
            prog.on("vector", lambda V, sl=sl:
                    V.tensor_copy(lam_t(sl), pb[5][:, 0:128])
                    .then_inc(sPrepH.h, 1))
            sPrepH.bump(1)
            promise("bank5", sPrepH)

            cor_prev = sCor.n

            if h == 0:
                prog.wait("tensor", sH, H_ALL)
            prog.wait("tensor", sWq[sl], sWq[sl].n)
            if cor_prev:
                prog.wait("tensor", sCor, cor_prev)
            for sw in range(4):
                b = sw % 2
                te_bank(b)

                def _qmm(T, sw=sw, b=b, sl=sl):
                    for kc in range(KC):
                        mm = T.matmul(pb[b][:, :],
                                      wq_sl(sl)[:, kc * 128:(kc + 1) * 128],
                                      h_fm(kc)[:, sw * 512:(sw + 1) * 512],
                                      start=(kc == 0), stop=(kc == KC - 1))
                        if kc == KC - 1:
                            mm.then_inc(sMMq.h, 1)
                prog.on("tensor", _qmm)
                sMMq.bump(1)
                prog.wait("scalar", sMMq, sMMq.n)
                if sw == 0 and cor_prev:
                    prog.wait("scalar", sCor, cor_prev)
                prog.on("scalar", lambda Sc, sw=sw, b=b, sl=sl:
                        Sc.activation(out=q_fm(sl)[:, sw * 512:(sw + 1) * 512],
                                      in_=pb[b][:, :], func=AF.Copy,
                                      scale=alpha[:, 0:1])
                        .then_inc(sQKe.h, 1))
                sQKe.bump(1)
                promise(f"bank{b}", sQKe)
            for sw in range(4):
                b = 2 + sw % 2
                te_bank(b)

                def _kmm(T, sw=sw, b=b, sl=sl):
                    for kc in range(KC):
                        mm = T.matmul(pb[b][:, :],
                                      wk_sl(sl)[:, kc * 128:(kc + 1) * 128],
                                      h_fm(kc)[:, sw * 512:(sw + 1) * 512],
                                      start=(kc == 0), stop=(kc == KC - 1))
                        if kc == KC - 1:
                            mm.then_inc(sMMq.h, 1)
                prog.on("tensor", _kmm)
                sMMq.bump(1)
                prog.wait("scalar", sMMq, sMMq.n)
                prog.on("scalar", lambda Sc, sw=sw, b=b:
                        Sc.activation(out=k_fm[:, sw * 512:(sw + 1) * 512],
                                      in_=pb[b][:, :], func=AF.Copy,
                                      scale=alpha[:, 1:2])
                        .then_inc(sQKe.h, 1))
                sQKe.bump(1)
                promise(f"bank{b}", sQKe)
                if sw == 3:
                    promise(f"wqk{sl}", sQKe)
            if h % 2 == 0:
                prog.wait("tensor", sWv, sWv.n)
                for c in range(CH):
                    b = 4 + (c % 2)
                    te_bank(b)

                    def _vmm(T, c=c, b=b):
                        for kc in range(KC):
                            mm = T.matmul(pb[b][:, 0:256],
                                          h_fm(kc)[:, c * 128:(c + 1) * 128],
                                          wv_sl[:, kc * 256:(kc + 1) * 256],
                                          start=(kc == 0), stop=(kc == KC - 1))
                            if kc == KC - 1:
                                mm.then_inc(sMMq.h, 1)
                    prog.on("tensor", _vmm)
                    sMMq.bump(1)
                    prog.wait("scalar", sMMq, sMMq.n)
                    prog.on("scalar", lambda Sc, c=c, b=b:
                            Sc.activation(out=v_tm[:, c * 256:(c + 1) * 256],
                                          in_=pb[b][:, 0:256], func=AF.Copy,
                                          scale=alpha[:, 1:2])
                            .then_inc(sVe.h, 1))
                    sVe.bump(1)
                    promise(f"bank{b}", sVe)
                    if c == CH - 1:
                        promise("wv", sVe)
                if g + 1 < H // 2:
                    issue_pair_v(g + 1)
            if h + 2 < p1_heads:
                issue_head_weights(h + 2)
            if h >= 1 and p1_relay >= 2:
                emit_corrections(h - 1)

            prog.wait("vector", sQKe, sQKe.n)
            prog.wait("vector", sPrepH, sPrepH.n)

            def _qt(V, sl=sl):
                rep = lam_t(sl).unsqueeze(1).broadcast_to((P, CH, P))
                q3 = q_fm(sl).rearrange("p (c t) -> p c t", t=P)
                o3 = qt_fm[:].rearrange("p (c t) -> p c t", t=P)
                V.tensor_tensor(o3, q3, rep, ALU.mult).then_inc(sQt.h, 1)
            prog.on("vector", _qt)
            sQt.bump(1)

            te_pt(0)
            prog.wait("tensor", sQKe, sQKe.n)
            prog.on("tensor", lambda T:
                    T.transpose(ptA[:, 0:128], k_fm[:, 0:128], c_ident[:])
                    .then_inc(sTr.h, 1))
            sTr.bump(1)
            prog.wait("scalar", sTr, sTr.n)
            prog.on("scalar", lambda Sc, sl=sl:
                    Sc.activation(out=ktm(0), in_=ptA[:, 0:128], func=AF.Copy,
                                  scale=lrev_col(sl))
                    .then_inc(sKtm.h, 1))
            sKtm.bump(1)
            promise("pt0", sKtm)

            vbase = (h % 2) * 128
            for c in range(CH):
                b_s = c % 2
                te_bank(b_s)
                if c == 0:
                    prog.wait("tensor", sQt, sQt.n)
                prog.on("tensor", lambda T, c=c, b=b_s, sl=sl:
                        T.matmul(pb[b][:, 0:128], k_fm[:, c * 128:(c + 1) * 128],
                                 q_fm(sl)[:, c * 128:(c + 1) * 128],
                                 start=True, stop=True).then_inc(sMMq.h, 1))
                sMMq.bump(1)
                prog.wait("vector", sMMq, sMMq.n)
                if c == 0:
                    prog.wait("vector", sPrepH, sPrepH.n)
                prog.on("vector", lambda V, c=c, b=b_s, sl=sl:
                        V.tensor_mul(scT(c % 2), pb[b][:, 0:128], mask_t(sl))
                        .then_inc(sScT.h, 1))
                sScT.bump(1)
                promise(f"bank{b_s}", sScT)
                b_kv = 4 + (c % 2)
                te_bank(b_kv)
                prog.wait("tensor", sKtm, sKtm.n)
                if c == 0:
                    prog.wait("tensor", sVe, sVe.n)
                prog.on("tensor", lambda T, c=c, b=b_kv, vbase=vbase:
                        T.matmul(pb[b][:, 0:128], ktm(c % 2),
                                 v_tm[:, c * 256 + vbase:c * 256 + vbase + 128],
                                 start=True, stop=True).then_inc(sMMq.h, 1))
                sMMq.bump(1)
                if c + 1 < CH:
                    np_ = (c + 1) % 2
                    te_pt(np_)
                    prog.on("tensor", lambda T, c=c, np_=np_:
                            T.transpose((ptA if np_ == 0 else ptB)[:, 0:128],
                                        k_fm[:, (c + 1) * 128:(c + 2) * 128],
                                        c_ident[:]).then_inc(sTr.h, 1))
                    sTr.bump(1)
                    prog.wait("scalar", sTr, sTr.n)
                    prog.on("scalar", lambda Sc, np_=np_, sl=sl:
                            Sc.activation(out=ktm(np_),
                                          in_=(ptA if np_ == 0 else ptB)[:, 0:128],
                                          func=AF.Copy, scale=lrev_col(sl))
                            .then_inc(sKtm.h, 1))
                    sKtm.bump(1)
                    promise(f"pt{np_}", sKtm)
                b_io = 2 + (c % 2)
                te_bank(b_io)
                prog.wait("tensor", sScT, sScT.n)
                if c > 0:
                    prog.wait("tensor", sSup, sSup.n)

                def _iomm(T, c=c, b=b_io, vbase=vbase):
                    mm1 = T.matmul(pb[b][:, 0:128],
                                   v_tm[:, c * 256 + vbase:c * 256 + vbase + 128],
                                   scT(c % 2), start=True, stop=(c == 0))
                    if c > 0:
                        mm2 = T.matmul(pb[b][:, 0:128], sbf((c + 1) % 2),
                                       qt_fm[:, c * 128:(c + 1) * 128],
                                       start=False, stop=True)
                        mm2.then_inc(sMMq.h, 1)
                    else:
                        mm1.then_inc(sMMq.h, 1)
                prog.on("tensor", _iomm)
                sMMq.bump(1)
                prog.wait("vector", sMMq, sMMq.n)
                prog.on("vector", lambda V, c=c, b=b_io, h=h:
                        V.tensor_copy(attn(h)[:, c * 128:(c + 1) * 128],
                                      pb[b][:, 0:128])
                        .then_inc(sIOe.h, 1))
                sIOe.bump(1)
                promise(f"bank{b_io}", sIOe)

                def _sup(V, c=c, b=b_kv, h=h):
                    if c == 0:
                        V.tensor_copy(sbf(0), pb[b][:, 0:128]).then_inc(sSup.h, 1)
                    else:
                        V.scalar_tensor_tensor(sbf(c % 2), sbf((c + 1) % 2),
                                               c_lamT[:, h:h + 1], pb[b][:, 0:128],
                                               ALU.mult, ALU.add) \
                            .then_inc(sSup.h, 1)
                prog.on("vector", _sup)
                sSup.bump(1)
                promise(f"bank{b_kv}", sSup)

            if p1_relay == 0:
                continue
            prog.wait("vector", sStgH, sStgH.n)

            def _stgcp(V):
                V.tensor_copy(s_stage[:], sbf(1))
                V.memset(alpha[:, 7:8], 0.0)
                V.memset(alpha[:, 7:8], 0.0).then_inc(sStgCp.h, 1)
            prog.on("vector", _stgcp)
            sStgCp.bump(1)
            prog.wait("sync", sStgCp, sStgCp.n)
            prog.on("sync", lambda E, h=h:
                    E.dma_start(out=cc_in_h[h].ap(), in_=s_stage[:])
                    .then_inc(sStgH.h, 16))
            sStgH.bump(16)
            prog.wait("gpsimd", sStgH, sStgH.n)
            prog.on("gpsimd", lambda G, h=h: G.collective_compute(
                "AllGather", ALU.bypass, replica_groups=[list(range(8))],
                ins=[cc_in_h[h].ap().opt()], outs=[cc_out_h[h].ap().opt()])
                .then_inc(ccH.h, 1))
            ccH.bump(1)

        if p1_relay >= 2:
            emit_corrections(p1_heads - 1)
        P1_CORR_ALL = sCor.n

        if debug:
            sDbg = S("sDbg")
            prog.wait("sync", sCor, P1_CORR_ALL)
            prog.on("sync", lambda E: E.dma_start(out=dbg["attn"],
                                                  in_=R2[:, 0:KC * TOK])
                    .then_inc(sDbg.h, 16))
            sDbg.bump(16)
            prog.wait("sync", sH, H_ALL)
            prog.on("sync", lambda E: E.dma_start(out=dbg["h"], in_=R1[:, 0:KC * TOK])
                    .then_inc(sDbg.h, 16))
            sDbg.bump(16)

        if phases == 1:
            return finish_and_emit([("sync", sCor), ("sync", sIOe),
                                    ("sync", sStgH)])

        B1 = S("B1")
        for e in ENGINES:
            prog.on(e, lambda E: E.nop().then_inc(B1.h, 1))
        B1.bump(5)
        for e in ENGINES:
            prog.wait(e, B1, 5)

        if debug:
            prog.wait("vector", sems["sDbg"], sems["sDbg"].n)
        sWpT = S("sWpT")
        sWpSt = S("sWpSt")
        ccW = S("ccW")
        wpb_bounce = R3[:, 20480:20736]
        for kc in range(KC):
            sp = kc % 2
            pwait("sync", f"wstg{sp}")
            tgt = sStg[sp].bump(16)
            prog.on("sync", lambda E, kc=kc, sp=sp:
                    E.dma_start(out=wstg(sp)[:, 0:256],
                                in_=wp_scan[kc * P:(kc + 1) * P, :])
                    .then_inc(sStg[sp].h, 16))
            prog.wait("scalar", sStg[sp], tgt)
            pwait("scalar", "wi32")
            prog.on("scalar", lambda Sc, sp=sp:
                    Sc.activation(out=wi32[:, 0:256], in_=wstg(sp)[:, 0:256],
                                  func=AF.Copy, scale=alpha[:, 4:5])
                    .then_inc(sI32.h, 1))
            sI32.bump(1)
            promise(f"wstg{sp}", sI32)
            prog.wait("vector", sI32, sI32.n)
            if kc >= 1:
                prog.wait("vector", sWpSt, sWpSt.n - 32)
            prog.on("vector", lambda V:
                    V.tensor_scalar(wpb_bounce, wi32[:, 0:256],
                                    1.0, -1.0, ALU.min, ALU.max)
                    .then_inc(sWpT.h, 1))
            sWpT.bump(1)
            promise("wi32", sWpT)
            prog.wait("sync", sWpT, sWpT.n)
            for dsub in range(2):
                prog.on("sync", lambda E, kc=kc, dsub=dsub:
                        E.dma_start(out=wpbf_loc[dsub * P:(dsub + 1) * P,
                                                 kc * P:(kc + 1) * P],
                                    in_=wpb_bounce[:, dsub * 128:(dsub + 1) * 128])
                        .then_inc(sWpSt.h, 16))
                sWpSt.bump(16)
        WPST_ALL = sWpSt.n
        prog.wait("gpsimd", sWpSt, WPST_ALL)
        prog.on("gpsimd", lambda G: G.collective_compute(
            "AllGather", ALU.bypass, replica_groups=[list(range(8))],
            ins=[wpbf_loc.opt()], outs=[wpbf.opt()]).then_inc(ccW.h, 1))
        ccW.bump(1)
        sC8 = S("sC8")
        for dst8, srcp in ((wg8l, wgpre), (wu8l, wupre), (wd8l, wdpre)):
            prog.on("gpsimd", lambda G, dst8=dst8, srcp=srcp:
                    G.dma_start(out=dst8, in_=srcp).then_inc(sC8.h, 16))
            sC8.bump(16)
        prog.wait("gpsimd", sC8, sC8.n)
        for locp, allp in ((wg8l, wg8), (wu8l, wu8), (wd8l, wd8)):
            prog.on("gpsimd", lambda G, locp=locp, allp=allp:
                    G.collective_compute(
                        "AllGather", ALU.bypass, replica_groups=[list(range(8))],
                        ins=[locp.opt()], outs=[allp.opt()]).then_inc(ccW.h, 1))
            ccW.bump(1)
        CCW_ALL = ccW.n

        sSq = S("sSq")
        sStM = S("sStM")
        sScR = S("sScR")
        sRbc2 = S("sRbc2")
        sRev2 = S("sRev2")
        for sw in range(4):
            bh, bl = 2, 3
            te_bank(bh)
            te_bank(bl)
            for ec in range(KC):
                p = ec % 2
                prog.wait("vector", sStM, max(0, sStM.n - 1))

                def _sqv(V, ec=ec, sw=sw, p=p):
                    V.tensor_mul(p2_sq(p), attn(ec)[:, sw * 512:(sw + 1) * 512],
                                 attn(ec)[:, sw * 512:(sw + 1) * 512])
                    V.memset(alpha[:, 7:8], 0.0).then_inc(sSq.h, 1)
                prog.on("vector", _sqv)
                sSq.bump(1)
                prog.wait("tensor", sSq, sSq.n)

                def _stmm(T, ec=ec, p=p, bh=bh, bl=bl):
                    T.matmul(pb[bh][0:1, :], v_shl[:, ec:ec + 1], p2_sq(p),
                             start=(ec == 0), stop=(ec == KC - 1))
                    mm = T.matmul(pb[bl][0:1, :], v_shl[:, KC + ec:KC + ec + 1],
                                  p2_sq(p), start=(ec == 0), stop=(ec == KC - 1))
                    mm.then_inc(sStM.h, 1)
                prog.on("tensor", _stmm)
                sStM.bump(1)
            r = sw % 2
            prog.wait("scalar", sStM, sStM.n)
            if sw >= 2:
                prog.wait("scalar", sRbc2, sw - 1)
            def _rowcp(Sc, r=r, bh=bh):
                Sc.copy(rowA[0:1, r * 512:(r + 1) * 512], pb[bh][0:1, :])
                Sc.copy(arow[0:1, 24:25], c_cv[0:1, 0:1])
                Sc.copy(arow[0:1, 24:25], c_cv[0:1, 0:1]).then_inc(sScR.h, 1)
            prog.on("scalar", _rowcp)
            sScR.bump(1)
            promise(f"bank{bh}", sScR)
            prog.wait("vector", sScR, sScR.n)

            def _rowadd(V, r=r, bl=bl):
                V.tensor_add(rowA[0:1, r * 512:(r + 1) * 512],
                             rowA[0:1, r * 512:(r + 1) * 512], pb[bl][0:1, :])
                V.memset(alpha[:, 7:8], 0.0)
                V.memset(alpha[:, 7:8], 0.0).then_inc(sScR.h, 1)
            prog.on("vector", _rowadd)
            sScR.bump(1)
            promise(f"bank{bl}", sScR)
            prog.wait("scalar", sScR, sScR.n)

            def _rhosq(Sc, r=r):
                Sc.activation(out=rowA[0:1, r * 512:(r + 1) * 512],
                              in_=rowA[0:1, r * 512:(r + 1) * 512],
                              func=AF.Sqrt, bias=c_cv[0:1, 3:4])
                Sc.copy(arow[0:1, 24:25], c_cv[0:1, 0:1])
                Sc.copy(arow[0:1, 24:25], c_cv[0:1, 0:1]).then_inc(sScR.h, 1)
            prog.on("scalar", _rhosq)
            sScR.bump(1)
            prog.wait("vector", sScR, sScR.n)

            def _rhorec(V, r=r):
                V.reciprocal(rowA[0:1, r * 512:(r + 1) * 512],
                             rowA[0:1, r * 512:(r + 1) * 512])
                V.memset(alpha[:, 7:8], 0.0)
                V.memset(alpha[:, 7:8], 0.0).then_inc(sScR.h, 1)
            prog.on("vector", _rhorec)
            sScR.bump(1)
            b = 4 + sw % 2
            te_bank(b)
            prog.wait("tensor", sScR, sScR.n)
            prog.on("tensor", lambda T, r=r, b=b:
                    T.matmul(pb[b][:, :], c_ones_row[0:1, :],
                             rowA[0:1, r * 512:(r + 1) * 512],
                             start=True, stop=True).then_inc(sRbc2.h, 1))
            sRbc2.bump(1)
            prog.wait("vector", sRbc2, sRbc2.n)
            prog.on("vector", lambda V, sw=sw, b=b:
                    V.tensor_copy(rho12_bc[:, sw * 512:(sw + 1) * 512], pb[b][:, :])
                    .then_inc(sRev2.h, 1))
            sRev2.bump(1)
            promise(f"bank{b}", sRev2)

        if debug:
            prog.wait("sync", sRev2, sRev2.n)
            prog.on("sync", lambda E: E.dma_start(out=dbg["rho"],
                                                  in_=rho12_bc)
                    .then_inc(sems["sDbg"].h, 16))
            sems["sDbg"].bump(16)
            prog.wait("sync", ccW, 1)
            prog.on("sync", lambda E: E.dma_start(out=dbg["wp"],
                                                  in_=wpbf[0:P, 0:512])
                    .then_inc(sems["sDbg"].h, 16))
            sems["sDbg"].bump(16)

        sZ = S("sZ")
        sPm = S("sPm")
        sR1 = S("sR1")
        sR1b = S("sR1b")
        sLnM = S("sLnM")
        sLnR = S("sLnR")
        sX1 = S("sX1")
        sX2 = S("sX2")
        sMS = S("sMS")
        sB2c = S("sB2c")
        sWpL = [S("sWpLa"), S("sWpLb")]

        def _mk_lnmm(doc):
            def f(T, doc=doc):
                T.matmul(pb[2][0:1, :], c_ones_col_bf[:], p2_r1bf(doc % 2),
                         start=(doc == 0), stop=(doc == KC - 1))
                mm = T.matmul(pb[3][0:1, :], c_ones_col_bf[:],
                              p2_r1sq(doc % 2),
                              start=(doc == 0), stop=(doc == KC - 1))
                mm.then_inc(sLnM.h, 1)
            return f
        for sw in range(4):
            half, tl0 = sw // 2, (sw % 2) * 512
            r1b_base = sR1b.n
            lnm_base = sLnM.n
            prog.wait("vector", sRev2, sRev2.n)

            def _z(V, sw=sw, half=half, tl0=tl0):
                for ec in range(KC):
                    ins = V.scalar_tensor_tensor(
                        x1_sl(ec, half, tl0, 512),
                        attn(ec)[:, sw * 512:(sw + 1) * 512],
                        v_w12[:, ec:ec + 1],
                        rho12_bc[:, sw * 512:(sw + 1) * 512],
                        ALU.mult, ALU.mult)
                    if ec == KC - 1:
                        ins.then_inc(sZ.h, 1)
            prog.on("vector", _z)
            sZ.bump(1)
            for doc in range(KC):
                sp = doc % 2
                pwait("scalar", f"wp{sp}")
                if sw == 0 and doc < 2:
                    prog.wait("scalar", ccW, 1)
                tgtw = sWpL[sp].bump(16)
                prog.on("scalar", lambda Sc, doc=doc, sp=sp:
                        Sc.dma_start(out=wpsl(sp),
                                     in_=wpbf[doc * P:(doc + 1) * P, :])
                        .then_inc(sWpL[sp].h, 16))
                pwait("sync", f"x2{sp}")
                tgt = sX2.bump(16)
                prog.on("sync", lambda E, doc=doc, sw=sw, sp=sp:
                        E.dma_start(out=p2_xstg(sp),
                                    in_=xT[doc * P:(doc + 1) * P,
                                           sw * 512:(sw + 1) * 512])
                        .then_inc(sX2.h, 16))
                b = doc % 2
                te_bank(b)
                prog.wait("tensor", sZ, sZ.n)
                prog.wait("tensor", sWpL[sp], tgtw)

                def _pmm(T, doc=doc, half=half, tl0=tl0, b=b, sp=sp):
                    for ec in range(KC):
                        mm = T.matmul(pb[b][:, :],
                                      wpsl(sp)[:, ec * 128:(ec + 1) * 128],
                                      x1_sl(ec, half, tl0, 512),
                                      start=(ec == 0), stop=(ec == KC - 1))
                        if ec == KC - 1:
                            mm.then_inc(sPm.h, 1)
                prog.on("tensor", _pmm)
                sPm.bump(1)
                prog.wait("vector", sPm, sPm.n)
                prog.wait("vector", sX2, tgt)
                def _r1ev(V, doc=doc, b=b, sp=sp):
                    V.scalar_tensor_tensor(r1_swp(doc), pb[b][:, :],
                                           alpha[:, 3:4], p2_xstg(sp),
                                           ALU.mult, ALU.add)
                    V.memset(alpha[:, 7:8], 0.0)
                    V.memset(alpha[:, 7:8], 0.0).then_inc(sR1.h, 1)
                prog.on("vector", _r1ev)
                sR1.bump(1)
                promise(f"bank{b}", sR1)
                promise(f"x2{sp}", sR1)
                promise(f"wp{sp}", sR1)
                prog.wait("scalar", sR1, sR1.n)
                prog.wait("scalar", sLnM, max(0, sLnM.n - 1))

                def _r1bf(Sc, doc=doc):
                    Sc.activation(out=p2_r1bf(doc % 2), in_=r1_swp(doc),
                                  func=AF.Copy)
                    Sc.copy(arow[0:1, 24:25], c_cv[0:1, 0:1]) \
                        .then_inc(sR1b.h, 1)
                prog.on("scalar", _r1bf)
                sR1b.bump(1)
                prog.wait("vector", sR1b, sR1b.n)

                def _r1sq(V, doc=doc):
                    V.tensor_mul(p2_r1sq(doc % 2), p2_r1bf(doc % 2),
                                 p2_r1bf(doc % 2))
                    V.memset(alpha[:, 7:8], 0.0)
                    V.memset(alpha[:, 7:8], 0.0).then_inc(sR1b.h, 1)
                prog.on("vector", _r1sq)
                sR1b.bump(1)
                if doc == 0:
                    te_bank(2)
                    te_bank(3)
                prog.wait("tensor", sR1b, sR1b.n)
                prog.on("tensor", _mk_lnmm(doc))
                sLnM.bump(1)
            prog.wait("vector", sLnM, sLnM.n)

            def _lnr(V):
                V.tensor_scalar(rowB[0:1, 0:512], pb[2][0:1, :], 1.0 / 2048.0, 0.0,
                                ALU.mult, ALU.add)
                V.tensor_mul(rowB[0:1, 512:1024], rowB[0:1, 0:512], rowB[0:1, 0:512])
                V.scalar_tensor_tensor(rowB[0:1, 512:1024], pb[3][0:1, :],
                                       1.0 / 2048.0, rowB[0:1, 512:1024],
                                       ALU.mult, ALU.subtract)
                V.memset(alpha[:, 7:8], 0.0)
                V.memset(alpha[:, 7:8], 0.0).then_inc(sLnR.h, 1)
            prog.on("vector", _lnr)
            sLnR.bump(1)
            promise("bank2", sLnR)
            promise("bank3", sLnR)
            prog.wait("scalar", sLnR, sLnR.n)

            def _lnsq(Sc):
                Sc.activation(out=rowB[0:1, 512:1024], in_=rowB[0:1, 512:1024],
                              func=AF.Sqrt, bias=c_cv[0:1, 4:5])
                Sc.copy(arow[0:1, 24:25], c_cv[0:1, 0:1])
                Sc.copy(arow[0:1, 24:25], c_cv[0:1, 0:1]).then_inc(sLnR.h, 1)
            prog.on("scalar", _lnsq)
            sLnR.bump(1)
            prog.wait("vector", sLnR, sLnR.n)

            def _lnf(V):
                V.reciprocal(rowB[0:1, 512:1024], rowB[0:1, 512:1024])
                V.tensor_mul(rowB[0:1, 0:512], rowB[0:1, 0:512],
                             rowB[0:1, 512:1024])
                V.memset(alpha[:, 7:8], 0.0)
                V.memset(alpha[:, 7:8], 0.0).then_inc(sLnR.h, 1)
            prog.on("vector", _lnf)
            sLnR.bump(1)
            for i, (src0, dstt) in enumerate([(512, p2_sgbc), (0, p2_mubc)]):
                b = 4 + i
                te_bank(b)
                prog.wait("tensor", sLnR, sLnR.n)
                prog.on("tensor", lambda T, src0=src0, b=b:
                        T.matmul(pb[b][:, :], c_ones_row[0:1, :],
                                 rowB[0:1, src0:src0 + 512], start=True, stop=True)
                        .then_inc(sB2c.h, 1))
                sB2c.bump(1)
                prog.wait("vector", sB2c, sB2c.n)
                prog.on("vector", lambda V, dstt=dstt, b=b:
                        V.tensor_copy(dstt, pb[b][:, :])
                        .then_inc(sB2c.h, 1))
                sB2c.bump(1)
                promise(f"bank{b}", sB2c)
            prog.wait("vector", sB2c, sB2c.n)

            def _x1(V, sw=sw, half=half, tl0=tl0):
                for doc in range(KC):
                    V.tensor_mul(r1_swp(doc), r1_swp(doc), p2_sgbc)
                    V.tensor_sub(x1_sl(doc, half, tl0, 512), r1_swp(doc),
                                 p2_mubc)
                V.memset(alpha[:, 7:8], 0.0)
                V.memset(alpha[:, 7:8], 0.0).then_inc(sX1.h, 1)
            prog.on("vector", _x1)
            sX1.bump(1)
        X1_ALL = sX1.n

        if debug:
            prog.wait("sync", sX1, X1_ALL)
            prog.on("sync", lambda E: E.dma_start(out=dbg["x1a"], in_=R3[:, 0:20480])
                    .then_inc(sems["sDbg"].h, 16))
            sems["sDbg"].bump(16)
            prog.on("sync", lambda E: E.dma_start(out=dbg["x1b"],
                                                  in_=R1[:, 31744:44032])
                    .then_inc(sems["sDbg"].h, 16))
            sems["sDbg"].bump(16)

        sSp = S("sSp")
        prog.wait("sync", sX1, X1_ALL)
        prog.on("sync", lambda E: E.dma_start(out=x1h2[:, 0:10240],
                                              in_=R3[:, 10240:20480])
                .then_inc(sSp.h, 16))
        sSp.bump(16)
        prog.on("sync", lambda E: E.dma_start(out=x1h2[:, 10240:16384],
                                              in_=R1[:, 31744 + 6144:44032])
                .then_inc(sSp.h, 16))
        sSp.bump(16)

        if phases == 2:
            return finish_and_emit([("sync", sX1), ("sync", sSp)])

        B2 = S("B2")
        for e in ENGINES:
            prog.on(e, lambda E: E.nop().then_inc(B2.h, 1))
        B2.bump(5)
        for e in ENGINES:
            prog.wait(e, B2, 5)
        prog.wait("gpsimd", sSp, sSp.n)

        sWg = [S("sWgA"), S("sWgB")]
        sWu = [S("sWuA"), S("sWuB")]
        sWd = [S("sWdA"), S("sWdB")]
        sCs = S("sCs")
        sGuM = S("sGuM")
        sGuE = S("sGuE")
        sGu = S("sGu")
        sGsum = S("sGsum")
        sDnM = S("sDnM")
        sR2v = S("sR2v")
        sLn2M = S("sLn2M")
        sLn2R = S("sLn2R")
        sB3 = S("sB3")
        sOut = S("sOut")
        sMSl = S("sMSl")
        sX1r = S("sX1r")

        for sw in range(4):
            half, swl = sw // 2, sw % 2
            if sw == 2:
                prog.wait("sync", sGu, sGu.n)
                prog.wait("sync", sOut, sOut.n)
                prog.wait("sync", sCs, sCs.n)
                prog.wait("sync", sR2v, sR2v.n)
                tgt = sX1r.bump(32)
                prog.on("sync", lambda E:
                        E.dma_start(out=R3[:, 0:10240], in_=x1h2[:, 0:10240])
                        .then_inc(sX1r.h, 16))
                prog.on("sync", lambda E:
                        E.dma_start(out=R1[:, 31744:31744 + 6144],
                                    in_=x1h2[:, 10240:16384])
                        .then_inc(sX1r.h, 16))
                for e in ("tensor", "vector"):
                    prog.wait(e, sX1r, tgt)
            te_bank(4)
            if sw == 0:
                prog.wait("tensor", sX1, X1_ALL)

            def _csmm(T, swl=swl):
                for kc in range(KC):
                    mm = T.matmul(pb[4][0:1, :], c_ones_col_bf[:],
                                  x1_cur(kc, swl * 512, 512),
                                  start=(kc == 0), stop=(kc == KC - 1))
                    if kc == KC - 1:
                        mm.then_inc(sCs.h, 1)
            prog.on("tensor", _csmm)
            sCs.bump(1)
            prog.wait("scalar", sCs, sCs.n)

            def _csrow(Sc):
                Sc.copy(p3_rows[0:1, 0:512], pb[4][0:1, :])
                Sc.copy(arow[0:1, 24:25], c_cv[0:1, 0:1])
                Sc.copy(arow[0:1, 24:25], c_cv[0:1, 0:1]).then_inc(sCs.h, 1)
            prog.on("scalar", _csrow)
            sCs.bump(1)
            promise("bank4", sCs)
            for hc in range(HOC):
                sp = hc % 2
                pwait("gpsimd", f"wg{sp}")
                if sw == 0 and hc == 0:
                    prog.wait("gpsimd", ccW, CCW_ALL)
                tgtg = sWg[sp].bump(16)
                prog.on("gpsimd", lambda G, hc=hc, sp=sp:
                        G.dma_start(out=wg_sl(sp),
                                    in_=wg8[hc * P:(hc + 1) * P, :])
                        .then_inc(sWg[sp].h, 16))
                pwait("gpsimd", f"wu{sp}")
                tgtu = sWu[sp].bump(16)
                prog.on("gpsimd", lambda G, hc=hc, sp=sp:
                        G.dma_start(out=wu_sl(sp),
                                    in_=wu8[hc * P:(hc + 1) * P, :])
                        .then_inc(sWu[sp].h, 16))
                bg = 0
                bu = 1
                te_bank(bg)
                prog.wait("tensor", sWg[sp], tgtg)
                if hc == 0:
                    prog.wait("tensor", sCs, sCs.n)

                def _gmm(T, sp=sp, swl=swl, bg=bg):
                    for kc in range(KC):
                        T.matmul(pb[bg][:, :], wg_sl(sp)[:, kc * 128:(kc + 1) * 128],
                                 x1_cur(kc, swl * 512, 512),
                                 start=(kc == 0), stop=False)
                    mm = T.matmul(pb[bg][:, :], c_neg128_row[0:1, :],
                                  p3_rows[0:1, 0:512], start=False, stop=True)
                    mm.then_inc(sGuM.h, 1)
                prog.on("tensor", _gmm)
                sGuM.bump(1)
                te_bank(bu)
                prog.wait("tensor", sWu[sp], tgtu)

                def _umm(T, sp=sp, swl=swl, bu=bu):
                    for kc in range(KC):
                        T.matmul(pb[bu][:, :], wu_sl(sp)[:, kc * 128:(kc + 1) * 128],
                                 x1_cur(kc, swl * 512, 512),
                                 start=(kc == 0), stop=False)
                    mm = T.matmul(pb[bu][:, :], c_neg128_row[0:1, :],
                                  p3_rows[0:1, 0:512], start=False, stop=True)
                    mm.then_inc(sGuM.h, 1)
                prog.on("tensor", _umm)
                sGuM.bump(1)
                prog.wait("scalar", sGuM, sGuM.n)
                prog.wait("scalar", sGu, max(0, sGu.n - 1))
                if hc == 0:
                    prog.wait("scalar", sLn2M, sLn2M.n)

                def _ge(Sc, hc=hc, bg=bg, bu=bu):
                    Sc.activation(out=gtmp(0), in_=pb[bg][:, :], func=AF.Silu,
                                  scale=v_sgate[:, hc:hc + 1])
                    Sc.activation(out=utmp(0), in_=pb[bu][:, :], func=AF.Copy,
                                  scale=v_sup[:, hc:hc + 1])
                    Sc.copy(arow[0:1, 24:25], c_cv[0:1, 0:1]) \
                        .then_inc(sGuE.h, 1)
                prog.on("scalar", _ge)
                sGuE.bump(1)
                prog.wait("vector", sGuE, sGuE.n)

                def _gu(V, hc=hc):
                    V.tensor_mul(gu_sl(hc), gtmp(0), utmp(0)).then_inc(sGu.h, 1)
                prog.on("vector", _gu)
                sGu.bump(1)
                promise(f"bank{bg}", sGu)
                promise(f"bank{bu}", sGu)
                promise(f"wg{sp}", sGu)
                promise(f"wu{sp}", sGu)
            te_bank(4)
            prog.wait("tensor", sGu, sGu.n)

            def _gsmm(T):
                for hc in range(HOC):
                    mm = T.matmul(pb[4][0:1, :], c_ones_col_bf[:], gu_sl(hc),
                                  start=(hc == 0), stop=(hc == HOC - 1))
                    if hc == HOC - 1:
                        mm.then_inc(sGsum.h, 1)
            prog.on("tensor", _gsmm)
            sGsum.bump(1)
            prog.wait("scalar", sGsum, sGsum.n)

            def _gsrow(Sc):
                Sc.copy(p3_rows[0:1, 512:1024], pb[4][0:1, :])
                Sc.copy(arow[0:1, 24:25], c_cv[0:1, 0:1])
                Sc.copy(arow[0:1, 24:25], c_cv[0:1, 0:1]).then_inc(sGsum.h, 1)
            prog.on("scalar", _gsrow)
            sGsum.bump(1)
            promise("bank4", sGsum)
            r2v_base = sR2v.n
            ln2_base = sLn2M.n

            def _mk_ln2mm(doc):
                def f(T, doc=doc):
                    T.matmul(pb[4][0:1, :], c_ones_col_bf[:], gtmp(doc % 2),
                             start=(doc == 0), stop=(doc == KC - 1))
                    mm = T.matmul(pb[5][0:1, :], c_ones_col_bf[:],
                                  utmp(doc % 2),
                                  start=(doc == 0), stop=(doc == KC - 1))
                    mm.then_inc(sLn2M.h, 1)
                return f
            for doc in range(KC):
                sp = doc % 2
                pwait("gpsimd", f"wd{sp}")
                tgtd = sWd[sp].bump(16)
                prog.on("gpsimd", lambda G, doc=doc, sp=sp:
                        G.dma_start(out=wd_sl(sp),
                                    in_=wd8[doc * P:(doc + 1) * P, :])
                        .then_inc(sWd[sp].h, 16))
                b = 2 + doc % 2
                te_bank(b)
                prog.wait("tensor", sWd[sp], tgtd)
                if doc == 0:
                    prog.wait("tensor", sGsum, sGsum.n)

                def _dmm(T, doc=doc, sp=sp, b=b):
                    for hc in range(HOC):
                        T.matmul(pb[b][:, :], wd_sl(sp)[:, hc * 128:(hc + 1) * 128],
                                 gu_sl(hc), start=(hc == 0), stop=False)
                    mm = T.matmul(pb[b][:, :], c_neg128_row[0:1, :],
                                  p3_rows[0:1, 512:1024], start=False, stop=True)
                    mm.then_inc(sDnM.h, 1)
                prog.on("tensor", _dmm)
                sDnM.bump(1)
                if doc == 0:
                    te_bank(4)
                    te_bank(5)
                prog.wait("vector", sDnM, sDnM.n)

                def _r2(V, doc=doc, swl=swl, b=b):
                    V.scalar_tensor_tensor(r2_swp(doc), pb[b][:, :],
                                           v_sdown[:, doc:doc + 1],
                                           x1_cur(doc, swl * 512, 512),
                                           ALU.mult, ALU.add)
                    V.memset(alpha[:, 7:8], 0.0)
                    V.memset(alpha[:, 7:8], 0.0).then_inc(sR2v.h, 1)
                prog.on("vector", _r2)
                sR2v.bump(1)
                promise(f"bank{b}", sR2v)
                promise(f"wd{sp}", sR2v)
                prog.wait("scalar", sR2v, sR2v.n)
                prog.wait("scalar", sLn2M, max(0, ln2_base + doc - 1))

                def _r2bf(Sc, doc=doc):
                    Sc.activation(out=gtmp(doc % 2), in_=r2_swp(doc), func=AF.Copy)
                    Sc.copy(arow[0:1, 24:25], c_cv[0:1, 0:1])
                    Sc.copy(arow[0:1, 24:25], c_cv[0:1, 0:1]) \
                        .then_inc(sR2v.h, 1)
                prog.on("scalar", _r2bf)
                sR2v.bump(1)
                prog.wait("vector", sR2v, sR2v.n)

                def _r2sq(V, doc=doc):
                    V.tensor_mul(utmp(doc % 2), gtmp(doc % 2), gtmp(doc % 2))
                    V.memset(alpha[:, 7:8], 0.0)
                    V.memset(alpha[:, 7:8], 0.0).then_inc(sR2v.h, 1)
                prog.on("vector", _r2sq)
                sR2v.bump(1)
                prog.wait("tensor", sR2v, sR2v.n)
                prog.on("tensor", _mk_ln2mm(doc))
                sLn2M.bump(1)
            prog.wait("vector", sLn2M, sLn2M.n)

            def _ln2r(V):
                V.tensor_scalar(rowB[0:1, 0:512], pb[4][0:1, :], 1.0 / 2048.0,
                                0.0, ALU.mult, ALU.add)
                V.tensor_mul(rowB[0:1, 512:1024], rowB[0:1, 0:512],
                             rowB[0:1, 0:512])
                V.scalar_tensor_tensor(rowB[0:1, 512:1024], pb[5][0:1, :],
                                       1.0 / 2048.0, rowB[0:1, 512:1024],
                                       ALU.mult, ALU.subtract)
                V.memset(alpha[:, 7:8], 0.0)
                V.memset(alpha[:, 7:8], 0.0).then_inc(sLn2R.h, 1)
            prog.on("vector", _ln2r)
            sLn2R.bump(1)
            promise("bank4", sLn2R)
            promise("bank5", sLn2R)
            prog.wait("scalar", sLn2R, sLn2R.n)

            def _ln2sq(Sc):
                Sc.activation(out=rowB[0:1, 512:1024], in_=rowB[0:1, 512:1024],
                              func=AF.Sqrt, bias=c_cv[0:1, 4:5])
                Sc.copy(arow[0:1, 24:25], c_cv[0:1, 0:1])
                Sc.copy(arow[0:1, 24:25], c_cv[0:1, 0:1]).then_inc(sLn2R.h, 1)
            prog.on("scalar", _ln2sq)
            sLn2R.bump(1)
            prog.wait("vector", sLn2R, sLn2R.n)

            def _ln2f(V):
                V.reciprocal(rowB[0:1, 512:1024], rowB[0:1, 512:1024])
                V.tensor_mul(rowB[0:1, 0:512], rowB[0:1, 0:512],
                             rowB[0:1, 512:1024])
                V.memset(alpha[:, 7:8], 0.0)
                V.memset(alpha[:, 7:8], 0.0).then_inc(sLn2R.h, 1)
            prog.on("vector", _ln2f)
            sLn2R.bump(1)
            for i, (src0, dstt) in enumerate([(512, p3_s2bc), (0, p3_m2bc)]):
                b = 4 + i
                te_bank(b)
                prog.wait("tensor", sLn2R, sLn2R.n)
                prog.on("tensor", lambda T, src0=src0, b=b:
                        T.matmul(pb[b][:, :], c_ones_row[0:1, :],
                                 rowB[0:1, src0:src0 + 512],
                                 start=True, stop=True).then_inc(sB3.h, 1))
                sB3.bump(1)
                prog.wait("vector", sB3, sB3.n)
                prog.on("vector", lambda V, dstt=dstt, b=b:
                        V.tensor_copy(dstt, pb[b][:, :]).then_inc(sB3.h, 1))
                sB3.bump(1)
                promise(f"bank{b}", sB3)
            for doc in range(KC):
                op = doc % 2
                prog.wait("vector", sB3, sB3.n)
                prog.wait("vector", sOut, max(0, sOut.n - 16))

                def _of(V, doc=doc, op=op):
                    V.tensor_sub(out_sl(op), r2_swp(doc), p3_m2bc)
                    V.tensor_mul(out_sl(op), out_sl(op), p3_s2bc)
                    V.memset(alpha[:, 7:8], 0.0)
                    V.memset(alpha[:, 7:8], 0.0).then_inc(sR2v.h, 1)
                prog.on("vector", _of)
                sR2v.bump(1)
                prog.wait("sync", sR2v, sR2v.n)
                sOut.bump(16)
                prog.on("sync", lambda E, doc=doc, sw=sw, op=op:
                        E.dma_start(out=yout[doc * P:(doc + 1) * P,
                                             sw * 512:(sw + 1) * 512],
                                    in_=out_sl(op)).then_inc(sOut.h, 16))

        prog.wait("sync", sOut, sOut.n)
        if debug:
            prog.wait("sync", sems["sDbg"], sems["sDbg"].n)
            prog.wait("sync", sems["sDbgM"], sems["sDbgM"].n)

        with nc.Block() as block:
            @block.sync
            def _(E):
                for fn in prog.streams["sync"]:
                    fn(E)

            @block.scalar
            def _(E):
                for fn in prog.streams["scalar"]:
                    fn(E)

            @block.vector
            def _(E):
                for fn in prog.streams["vector"]:
                    fn(E)

            @block.tensor
            def _(E):
                for fn in prog.streams["tensor"]:
                    fn(E)

            @block.gpsimd
            def _(E):
                for fn in prog.streams["gpsimd"]:
                    fn(E)

    return nc


def _consts():
    bf = ml_dtypes.bfloat16
    t = np.arange(P, dtype=np.float32)
    c = {}
    c["ident"] = np.eye(P, dtype=bf)
    c["diffT"] = (t[None, :] - t[:, None]).astype(bf)
    c["causal"] = (t[None, :] >= t[:, None]).astype(bf)
    c["iota_row"] = t[None, :].astype(bf)
    c["irev_col"] = (P - t)[:, None].copy()
    c["GIc"] = (t[:, None] + 128.0 * np.arange(H, dtype=np.float32)[None, :]).copy()
    c["J128"] = np.ascontiguousarray(
        np.broadcast_to(128.0 * np.arange(H, dtype=np.float32)[None, :],
                        (P, H)).astype(bf))
    c["ones_row_f"] = np.ones((1, P), np.float32)
    c["neg128_row_f"] = np.full((1, P), -128.0, np.float32)
    c["ones_col_bf"] = np.ones((P, 1), bf)
    c["ones_col_f"] = np.ones((P, 1), np.float32)
    cv = np.zeros((P, 8), np.float32)
    cv[:, 0] = -128.0
    cv[:, 1] = 1.0 / 2048.0
    cv[:, 2] = EPS_RMS
    cv[:, 3] = EPS_RMS * EPS_R1
    cv[:, 4] = EPS_LN
    c["cvals"] = cv
    return c


def _pack_pc(v, ncol):
    return np.ascontiguousarray(np.asarray(v, np.float32).reshape(ncol, P).T)


def kernel(L0_attn_x, L0_attn_qkv_w, L0_attn_qkv_norm_w, L0_attn_proj_w,
           L0_attn_proj_norm_w, L0_attn_norm_w, L0_attn_s,
           L0_ffn_wq_gate, L0_ffn_s_gate, L0_ffn_wq_up, L0_ffn_s_up,
           L0_ffn_wq_down, L0_ffn_s_down):
    x = np.asarray(L0_attn_x, dtype=np.float32)
    wq = np.asarray(L0_attn_qkv_w, dtype=np.float32)
    wp = np.asarray(L0_attn_proj_w, dtype=np.float32)
    wg = np.asarray(L0_ffn_wq_gate, dtype=np.int32)
    wu = np.asarray(L0_ffn_wq_up, dtype=np.int32)
    wd = np.asarray(L0_ffn_wq_down, dtype=np.int32)

    wqT = np.ascontiguousarray(wq.T)
    wpT = np.ascontiguousarray(wp.T)
    wgT = np.ascontiguousarray(
        wg.reshape(HOC, P, KC, P).transpose(0, 3, 2, 1).reshape(HID, D))
    wuT = np.ascontiguousarray(
        wu.reshape(HOC, P, KC, P).transpose(0, 3, 2, 1).reshape(HID, D))
    wdT = np.ascontiguousarray(
        wd.reshape(KC, P, HOC, P).transpose(0, 3, 2, 1).reshape(D, HID))

    shared = dict(
        wqT=wqT,
        wqn_pc=_pack_pc(L0_attn_qkv_norm_w, KC),
        w1_pc=_pack_pc(L0_attn_norm_w, KC),
        w2_pc=_pack_pc(L0_attn_proj_norm_w, KC),
        sdown_pc=_pack_pc(L0_ffn_s_down, KC),
        sgate_pc=_pack_pc(L0_ffn_s_gate, HOC),
        sup_pc=_pack_pc(L0_ffn_s_up, HOC),
        s_tiled=np.ascontiguousarray(
            np.broadcast_to(np.asarray(L0_attn_s, np.float32)[None, :], (P, H))),
        **_consts(),
    )

    in_maps = []
    for c in range(8):
        seq, half = c // 2, c % 2
        xs = x[seq, half * TOK:(half + 1) * TOK, :]
        m = dict(shared)
        m["xT"] = np.ascontiguousarray(xs.T)
        m["wq_scan"] = np.ascontiguousarray(wqT[:, c * 768:(c + 1) * 768])
        m["wp_scan"] = np.ascontiguousarray(wpT[:, c * 256:(c + 1) * 256])
        m["wgpre"] = np.ascontiguousarray(wgT[c * (HID // 8):(c + 1) * (HID // 8)])
        m["wupre"] = np.ascontiguousarray(wuT[c * (HID // 8):(c + 1) * (HID // 8)])
        m["wdpre"] = np.ascontiguousarray(wdT[c * (D // 8):(c + 1) * (D // 8)])
        m["corr_gate"] = np.full((P, 1), 1.0 if half == 1 else 0.0, np.float32)
        oh = np.zeros((P, 8), np.float32)
        oh[:, c & ~1] = 1.0
        m["onehot"] = oh
        in_maps.append(m)

    nc = build_program(debug=False)
    res = run_bass_kernel_spmd(nc, in_maps, core_ids=list(range(8)))

    out = np.empty((4, 4096, D), np.float32)
    for c in range(8):
        seq, half = c // 2, c % 2
        yc = res.results[c]["y"]
        out[seq, half * TOK:(half + 1) * TOK, :] = yc.T
    return out


# version 20
# speedup vs baseline: 1.0385x; 1.0290x over previous
import math
from contextlib import ExitStack

import numpy as np
import ml_dtypes

import concourse.bass as bass
import concourse.mybir as mybir
from concourse.bass_utils import run_bass_kernel_spmd

F32 = mybir.dt.float32
BF16 = mybir.dt.bfloat16
I32 = mybir.dt.int32
AF = mybir.ActivationFunctionType
ALU = mybir.AluOpType
AX = mybir.AxisListType

P = 128
TOK = 2048
D = 2048
KC = D // P
H = 16
CH = TOK // P
HID = 5632
HOC = HID // P
EPS_RMS = float(np.finfo(np.float32).eps)
EPS_R1 = 1e-6
EPS_LN = 1e-5
DINV = 1.0 / math.sqrt(P)

ENGINES = ("sync", "scalar", "vector", "tensor", "gpsimd")


class Sem:
    def __init__(self, nc, es, name):
        self.h = es.enter_context(nc.semaphore(name))
        self.n = 0

    def bump(self, k=1):
        self.n += k
        return self.n


class Prog:
    def __init__(self):
        self.streams = {e: [] for e in ENGINES}

    def on(self, eng, fn):
        self.streams[eng].append(fn)

    def wait(self, eng, sem, n=None):
        n = sem.n if n is None else n
        if n <= 0:
            return
        self.on(eng, lambda E, h=sem.h, n=n: E.wait_ge(h, n))


def build_program(debug=False, phases=3, p1_heads=16, p1_relay=2):
    nc = bass.Bass("TRN2", target_bir_lowering=False, debug=False)

    def din(name, shape, dtype):
        return nc.dram_tensor(name, shape, dtype, kind="ExternalInput").ap()

    xT = din("xT", [D, TOK], F32)
    wqT = din("wqT", [D, 3 * D], F32)
    wpT = din("wpT", [D, D], F32)
    wgT = din("wgT", [HID, D], I32)
    wuT = din("wuT", [HID, D], I32)
    wdT = din("wdT", [D, HID], I32)
    wq_scan = din("wq_scan", [D, 768], F32)
    wp_scan = din("wp_scan", [D, 256], F32)
    wqn_pc = din("wqn_pc", [P, KC], F32)
    w1_pc = din("w1_pc", [P, KC], F32)
    w2_pc = din("w2_pc", [P, KC], F32)
    sdown_pc = din("sdown_pc", [P, KC], F32)
    sgate_pc = din("sgate_pc", [P, HOC], F32)
    sup_pc = din("sup_pc", [P, HOC], F32)
    s_tiled = din("s_tiled", [P, H], F32)
    ident = din("ident", [P, P], BF16)
    diffT = din("diffT", [P, P], BF16)
    causal = din("causal", [P, P], BF16)
    iota_row = din("iota_row", [1, P], BF16)
    irev_col = din("irev_col", [P, 1], F32)
    GIc = din("GIc", [P, H], F32)
    J128 = din("J128", [P, H], BF16)
    ones_row_f = din("ones_row_f", [1, P], F32)
    neg128_row_f = din("neg128_row_f", [1, P], F32)
    ones_col_bf = din("ones_col_bf", [P, 1], BF16)
    ones_col_f = din("ones_col_f", [P, 1], F32)
    corr_gate = din("corr_gate", [P, 1], F32)
    onehot = din("onehot", [P, 8], F32)
    cvals = din("cvals", [P, 8], F32)

    yout = nc.dram_tensor("y", [D, TOK], F32, kind="ExternalOutput").ap()

    x1h2 = nc.dram_tensor("x1h2", [P, KC * 1024], BF16, kind="Internal").ap()
    wpbf = nc.dram_tensor("wpbf", [D, D], BF16, kind="Internal").ap()
    musig = nc.dram_tensor("musig", [2, TOK], F32, kind="Internal").ap()
    cc_in_a = nc.dram_tensor("cc_in_a", [1, 2], F32, kind="Internal")
    cc_out_a = nc.dram_tensor("cc_out_a", [1, 2], F32, kind="Internal",
                              addr_space="Shared")
    cc_in_h = [nc.dram_tensor(f"cc_in_h{h}", [P, P], BF16, kind="Internal")
               for h in range(H)]
    cc_out_h = [nc.dram_tensor(f"cc_out_h{h}", [8 * P, P], BF16, kind="Internal",
                               addr_space="Shared") for h in range(H)]
    dbg = {}
    if debug:
        dbg["attn"] = nc.dram_tensor("dbg_attn", [P, KC * TOK], BF16,
                                     kind="ExternalOutput").ap()
        dbg["x1a"] = nc.dram_tensor("dbg_x1a", [P, 20480], BF16,
                                    kind="ExternalOutput").ap()
        dbg["x1b"] = nc.dram_tensor("dbg_x1b", [P, 12288], BF16,
                                    kind="ExternalOutput").ap()
        dbg["h"] = nc.dram_tensor("dbg_h", [P, KC * TOK], BF16,
                                  kind="ExternalOutput").ap()
        dbg["misc"] = nc.dram_tensor("dbg_misc", [P, 12], F32,
                                     kind="ExternalOutput").ap()
        dbg["rho"] = nc.dram_tensor("dbg_rho", [P, 2048], F32,
                                    kind="ExternalOutput").ap()
        dbg["wp"] = nc.dram_tensor("dbg_wp", [P, 512], BF16,
                                   kind="ExternalOutput").ap()
        dbg["scan"] = nc.dram_tensor("dbg_scan", [P, 32], F32,
                                     kind="ExternalOutput").ap()

    es = ExitStack()
    with es:
        E = es.enter_context

        R1 = E(nc.sbuf_tensor("R1", [P, 44032], BF16))
        R2 = E(nc.sbuf_tensor("R2", [P, 32768], BF16))
        R3 = E(nc.sbuf_tensor("R3", [P, 22528], BF16))

        c_ident = E(nc.sbuf_tensor("c_ident", [P, P], BF16))
        c_diffT = E(nc.sbuf_tensor("c_diffT", [P, P], BF16))
        c_causal = E(nc.sbuf_tensor("c_causal", [P, P], BF16))
        c_iota_row = E(nc.sbuf_tensor("c_iota_row", [1, P], BF16))
        c_irev = E(nc.sbuf_tensor("c_irev", [P, 1], F32))
        c_GI = E(nc.sbuf_tensor("c_GI", [P, H], F32))
        c_J128 = E(nc.sbuf_tensor("c_J128", [P, H], BF16))
        c_ones_row = E(nc.sbuf_tensor("c_ones_row", [1, P], F32))
        c_neg128_row = E(nc.sbuf_tensor("c_neg128_row", [1, P], F32))
        c_ones_col_bf = E(nc.sbuf_tensor("c_ones_col_bf", [P, 1], BF16))
        c_ones_col_f = E(nc.sbuf_tensor("c_ones_col_f", [P, 1], F32))
        c_gate = E(nc.sbuf_tensor("c_gate", [P, 1], F32))
        c_onehot = E(nc.sbuf_tensor("c_onehot", [P, 8], F32))
        c_cv = E(nc.sbuf_tensor("c_cv", [P, 8], F32))
        c_s = E(nc.sbuf_tensor("c_s", [P, H], F32))
        c_negs = E(nc.sbuf_tensor("c_negs", [P, H], F32))
        c_lamT = E(nc.sbuf_tensor("c_lamT", [P, H], F32))
        v_wqn = E(nc.sbuf_tensor("v_wqn", [P, KC], F32))
        v_w1 = E(nc.sbuf_tensor("v_w1", [P, KC], F32))
        v_w2 = E(nc.sbuf_tensor("v_w2", [P, KC], F32))
        v_w12 = E(nc.sbuf_tensor("v_w12", [P, KC], F32))
        v_sdown = E(nc.sbuf_tensor("v_sdown", [P, KC], F32))
        v_sgate = E(nc.sbuf_tensor("v_sgate", [P, HOC], F32))
        v_sup = E(nc.sbuf_tensor("v_sup", [P, HOC], F32))
        v_stat = E(nc.sbuf_tensor("v_stat", [P, KC], F32))
        v_tmp = E(nc.sbuf_tensor("v_tmp", [P, KC], F32))
        v_shl = E(nc.sbuf_tensor("v_shl", [P, 2 * KC], BF16))
        alpha = E(nc.sbuf_tensor("alpha", [P, 8], F32))
        rowA = E(nc.sbuf_tensor("rowA", [1, 1024], F32))
        rowB = E(nc.sbuf_tensor("rowB", [1, 1024], F32))
        arow = E(nc.sbuf_tensor("arow", [1, 32], F32))
        acc2 = E(nc.sbuf_tensor("acc2", [P, 2], F32))

        pb = [E(nc.psum_tensor(f"pb{i}", [P, 512], F32)) for i in range(6)]
        ptA = E(nc.psum_tensor("ptA", [P, 1024], BF16))
        ptB = E(nc.psum_tensor("ptB", [P, 1024], BF16))

        prog = Prog()
        sems = {}

        def S(name):
            if name not in sems:
                sems[name] = Sem(nc, es, name)
            return sems[name]

        def r1f(lo, n):
            return R1[:, lo:lo + 2 * n].bitcast(F32)

        def r2f(lo, n):
            return R2[:, lo:lo + 2 * n].bitcast(F32)

        def r3f(lo, n):
            return R3[:, lo:lo + 2 * n].bitcast(F32)

        def r3i(lo, n):
            return R3[:, lo:lo + 2 * n].bitcast(I32)

        h_fm = lambda kc: R1[:, kc * TOK:(kc + 1) * TOK]
        xstg = lambda p: r1f(32768 + p * 4096, 2048)
        q_fm = lambda p: R1[:, 32768 + p * 2048: 32768 + (p + 1) * 2048]
        qt_fm = R1[:, 36864:38912]
        k_fm = R1[:, 38912:40960]
        sbf = lambda p: R1[:, 40960 + p * 128: 40960 + (p + 1) * 128]
        s_stage = R1[:, 41216:41344]
        scT = lambda p: R1[:, 41344 + p * 128: 41344 + (p + 1) * 128]
        mask_t = lambda p: R1[:, 41600 + p * 128: 41600 + (p + 1) * 128]
        lam_t = lambda p: R1[:, 41856 + p * 128: 41856 + (p + 1) * 128]
        ktm = lambda p: R1[:, 42112 + p * 128: 42112 + (p + 1) * 128]
        ssel_bf = R1[:, 42368:42496]
        lam2 = lambda p: r1f(42496 + p * 32, 16)
        lamTp = lambda p: r1f(42624 + p * 32, 16)
        lrev_col = lambda p: r1f(42752 + p * 8, 1)
        wpsl = lambda p: R1[:, p * 2048:(p + 1) * 2048]
        p2_r1bf = lambda p: R1[:, 4096 + p * 512: 4096 + (p + 1) * 512]
        p2_r1sq = lambda p: R1[:, 5120 + p * 512: 5120 + (p + 1) * 512]
        p2_sq = lambda p: R1[:, 6144 + p * 512: 6144 + (p + 1) * 512]
        p2_mubc = r1f(7168, 512)
        p2_sgbc = r1f(8192, 512)
        p2_xstg = lambda p: r1f(9216 + p * 1024, 512)
        r1_swp = lambda doc: r1f(11264 + doc * 1024, 512)
        rho12_bc = r1f(27648, 2048)
        gu_sl = lambda hc: R1[:, hc * 512:(hc + 1) * 512]
        p3_rows = r1f(22528, 2048)

        xbf = lambda kc: R2[:, kc * TOK:(kc + 1) * TOK]
        attn = lambda ec: R2[:, ec * TOK:(ec + 1) * TOK]
        r2_swp = lambda doc: r2f(doc * 1024, 512)
        wg_sl = lambda p: R2[:, 16384 + p * 2048: 16384 + (p + 1) * 2048]
        wu_sl = lambda p: R2[:, 20480 + p * 2048: 20480 + (p + 1) * 2048]
        out_sl = lambda p: r2f(24576 + p * 1024, 512)
        gtmp = lambda p: R2[:, 26624 + p * 512: 26624 + (p + 1) * 512]
        utmp = lambda p: R2[:, 27648 + p * 512: 27648 + (p + 1) * 512]
        p3_m2bc = r2f(28672, 512)
        p3_s2bc = r2f(29696, 512)
        p3_m1bc = r2f(30720, 512)
        p3_s1bc = r2f(31744, 512)

        p0stg = lambda p: r3f(p * 2048, 1024)
        rho_bc = r3f(4096, 2048)
        xsq = R3[:, 20480:22528]
        wq_sl = lambda p: R3[:, p * 4096: p * 4096 + 2048]
        wk_sl = lambda p: R3[:, p * 4096 + 2048: p * 4096 + 4096]
        wv_sl = R3[:, 8192:12288]
        wstg = lambda p: r3f(12288 + p * 1024, 512)
        wi32 = r3i(14336, 512)
        sall = R3[:, 15360:16384]
        v_tm = R3[:, 16384:20480]

        def x1_sl(fc, half, t0, n):
            if fc < 10:
                base = half * 10240 + fc * 1024 + t0
                return R3[:, base:base + n]
            base = 31744 + half * 6144 + (fc - 10) * 1024 + t0
            return R1[:, base:base + n]

        def x1_cur(fc, t0, n):
            if fc < 10:
                return R3[:, fc * 1024 + t0: fc * 1024 + t0 + n]
            base = 31744 + (fc - 10) * 1024 + t0
            return R1[:, base:base + n]

        wd_sl = lambda p: R3[:, 10240 + p * 5632: 10240 + (p + 1) * 5632]

        promises = {}

        def promise(name, sem):
            promises[name] = (sem, sem.n)

        def pwait(eng, name):
            if name in promises:
                sem, n = promises[name]
                prog.wait(eng, sem, n)

        def te_bank(i):
            pwait("tensor", f"bank{i}")

        def te_pt(i):
            pwait("tensor", f"pt{i}")

        Lc = S("Lc")
        for t, d in [
            (c_ident, ident), (c_diffT, diffT), (c_causal, causal),
            (c_iota_row, iota_row), (c_irev, irev_col), (c_GI, GIc),
            (c_J128, J128), (c_ones_row, ones_row_f), (c_neg128_row, neg128_row_f),
            (c_ones_col_bf, ones_col_bf), (c_ones_col_f, ones_col_f),
            (c_gate, corr_gate), (c_onehot, onehot), (c_cv, cvals), (c_s, s_tiled),
            (v_wqn, wqn_pc), (v_w1, w1_pc), (v_w2, w2_pc), (v_sdown, sdown_pc),
            (v_sgate, sgate_pc), (v_sup, sup_pc),
        ]:
            Lc.bump(16)
            prog.on("scalar", lambda E, t=t, d=d:
                    E.dma_start(out=t[:], in_=d).then_inc(Lc.h, 16))
        LC_ALL = Lc.n

        sPrep = S("sPrep")
        prog.wait("vector", Lc, LC_ALL)

        def _prep(V):
            V.tensor_mul(v_stat[:], v_w1[:], v_w1[:])
            V.tensor_scalar_mul(c_negs[:], c_s[:], -1.0)
            V.tensor_mul(v_w12[:], v_w1[:], v_w2[:])
            V.tensor_scalar(v_stat[:], v_stat[:], 1.0 / 2048.0, EPS_RMS / 2048.0,
                            ALU.mult, ALU.add)
            V.memset(alpha[:, 7:8], 0.0)
            V.memset(alpha[:, 7:8], 0.0)
            V.tensor_copy(v_shl[:, 0:KC], v_stat[:])
            V.memset(alpha[:, 7:8], 0.0)
            V.memset(alpha[:, 7:8], 0.0)
            V.tensor_copy(v_tmp[:], v_shl[:, 0:KC])
            V.memset(alpha[:, 7:8], 0.0)
            V.memset(alpha[:, 7:8], 0.0)
            V.tensor_sub(v_tmp[:], v_stat[:], v_tmp[:])
            V.memset(alpha[:, 7:8], 0.0)
            V.memset(alpha[:, 7:8], 0.0)
            V.tensor_copy(v_shl[:, KC:2 * KC], v_tmp[:])
            V.memset(alpha[:, 7:8], 0.0)
            V.memset(alpha[:, 7:8], 0.0).then_inc(sPrep.h, 1)
        prog.on("vector", _prep)
        sPrep.bump(1)
        prog.wait("scalar", Lc, LC_ALL)
        prog.on("scalar", lambda Sc: Sc.activation(out=c_lamT[:], in_=c_s[:],
                                                   func=AF.Exp, scale=c_cv[:, 0:1])
                .then_inc(sPrep.h, 1))
        sPrep.bump(1)
        PREP_ALL = sPrep.n

        Lsc = S("Lsc")
        sAbs = S("sAbs")
        prog.wait("vector", Lc, LC_ALL)
        prog.on("vector", lambda V: V.memset(acc2[:], 0.0))
        for i in range(2 * KC):
            is_p = i >= KC
            kc = i % KC
            p = i % 2
            col = 1 if is_p else 0
            src = wp_scan if is_p else wq_scan
            w = 256 if is_p else 768
            if i >= 2:
                prog.wait("gpsimd", sAbs, i - 1)
            tgt = Lsc.bump(16)
            prog.on("gpsimd", lambda G, kc=kc, p=p, src=src, w=w:
                    G.dma_start(out=p0stg(p)[:, 0:w], in_=src[kc * P:(kc + 1) * P, :])
                    .then_inc(Lsc.h, 16))
            prog.wait("vector", Lsc, tgt)

            def _red(V, i=i, p=p, w=w):
                V.reduce_sum(r1f(2 * i, 1), p0stg(p)[:, 0:w], axis=AX.X,
                             apply_absolute_value=True).then_inc(sAbs.h, 1)
            prog.on("vector", _red)
            sAbs.bump(1)
            if i >= 1:
                colp = 1 if (i - 1) >= KC else 0

                def _acc(V, j=i - 1, colp=colp):
                    V.memset(alpha[:, 7:8], 0.0)
                    V.tensor_add(acc2[:, colp:colp + 1], acc2[:, colp:colp + 1],
                                 r1f(2 * j, 1))
                prog.on("vector", _acc)
        ilast = 2 * KC - 1

        def _accf(V, j=ilast):
            V.memset(xsq[:, 0:1024], 0.0)
            V.memset(xsq[:, 1024:2048], 0.0)
            V.tensor_add(acc2[:, 1:2], acc2[:, 1:2], r1f(2 * j, 1))
            V.memset(xsq[:, 0:1024], 0.0)
            V.memset(xsq[:, 1024:2048], 0.0).then_inc(sAbs.h, 1)
        prog.on("vector", _accf)
        sAbs.bump(1)
        ABS_ALL = sAbs.n
        if debug:
            sDbg0 = S("sDbg")
            prog.wait("sync", sAbs, ABS_ALL)
            prog.on("sync", lambda E: E.dma_start(out=dbg["scan"], in_=r1f(0, 32))
                    .then_inc(sDbg0.h, 16))
            sDbg0.bump(16)
            prog.wait("vector", sDbg0, sDbg0.n)

        sMMa = S("sMMa")
        sStgA = S("sStgA")
        te_bank(5)
        prog.wait("tensor", sAbs, ABS_ALL)
        prog.wait("tensor", Lc, LC_ALL)

        def _psum_a(T):
            T.matmul(pb[5][0:1, 0:1], acc2[:, 0:1], c_ones_col_f[:],
                     start=True, stop=True)
            T.matmul(pb[5][0:1, 1:2], acc2[:, 1:2], c_ones_col_f[:],
                     start=True, stop=True).then_inc(sMMa.h, 1)
        prog.on("tensor", _psum_a)
        sMMa.bump(1)
        sCpA = S("sCpA")
        prog.wait("vector", sMMa, 1)

        def _cpA(V):
            V.tensor_copy(arow[0:1, 8:10], pb[5][0:1, 0:2])
            V.memset(arow[0:1, 30:31], 0.0)
            V.memset(arow[0:1, 30:31], 0.0).then_inc(sCpA.h, 1)
        prog.on("vector", _cpA)
        sCpA.bump(1)
        promise("bank5", sCpA)
        prog.wait("gpsimd", sCpA, 1)
        prog.wait("gpsimd", S("sXC"), 2)
        prog.on("gpsimd", lambda G: G.dma_start(out=cc_in_a.ap(),
                                                in_=arow[0:1, 8:10])
                .then_inc(sStgA.h, 16))
        sStgA.bump(16)
        ccA = S("ccA")
        prog.wait("gpsimd", sStgA, 16)
        prog.on("gpsimd", lambda G: G.collective_compute(
            "AllReduce", ALU.add, replica_groups=[list(range(8))],
            ins=[cc_in_a.ap().opt()], outs=[cc_out_a.ap().opt()]).then_inc(ccA.h, 1))
        ccA.bump(1)

        LxE = S("LxE")
        LxO = S("LxO")
        sXC = S("sXC")
        sXMM = S("sXMM")
        for kc in range(KC):
            p = kc % 2
            ldq = "sync" if p == 0 else "scalar"
            Lq = LxE if p == 0 else LxO
            if kc >= 2:
                prog.wait(ldq, sXC, (kc - 1) * 2)
                prog.wait(ldq, sXMM, kc - 1)
            tgt = Lq.bump(16)
            prog.on(ldq, lambda E, kc=kc, p=p, Lq=Lq:
                    E.dma_start(out=xstg(p), in_=xT[kc * P:(kc + 1) * P, :])
                    .then_inc(Lq.h, 16))
            prog.wait("vector", Lq, tgt)
            if kc >= 1:
                prog.wait("vector", sXMM, sXMM.n)
            prog.on("vector", lambda V, p=p:
                    V.tensor_mul(xsq, xstg(p), xstg(p)).then_inc(sXC.h, 1))
            sXC.bump(1)
            prog.wait("scalar", Lq, tgt)
            if kc == 0:
                prog.wait("scalar", Lc, LC_ALL)
            prog.on("scalar", lambda Sc, kc=kc, p=p:
                    Sc.activation(out=xbf(kc), in_=xstg(p), func=AF.Copy)
                    .then_inc(sXC.h, 1))
            sXC.bump(1)
            if kc == 0:
                for sw in range(4):
                    te_bank(sw)
                prog.wait("tensor", Lc, LC_ALL)
            prog.wait("tensor", sXC, sXC.n - 1)

            def _xmm(T, kc=kc):
                for sw in range(4):
                    mm = T.matmul(pb[sw][0:1, :], c_ones_col_bf[:],
                                  xsq[:, sw * 512:(sw + 1) * 512],
                                  start=(kc == 0), stop=(kc == KC - 1))
                    if sw == 3:
                        mm.then_inc(sXMM.h, 1)
            prog.on("tensor", _xmm)
            sXMM.bump(1)
        XMM_ALL = sXMM.n

        sRho = S("sRho")
        sRbc = S("sRbc")
        sRev = S("sRev")
        prog.wait("scalar", sXMM, XMM_ALL)
        for batch in range(2):
            if batch == 1:
                prog.wait("scalar", sRbc, 2)

            def _rho_rows(Sc, batch=batch):
                for j in range(2):
                    sw = batch * 2 + j
                    ins = Sc.activation(out=rowA[0:1, j * 512:(j + 1) * 512],
                                        in_=pb[sw][0:1, :], func=AF.Sqrt,
                                        scale=c_cv[0:1, 1:2], bias=c_cv[0:1, 2:3])
                    if j == 0:
                        ins.then_inc(sRho.h, 1)
                Sc.copy(arow[0:1, 24:25], c_cv[0:1, 0:1])
                Sc.copy(arow[0:1, 24:25], c_cv[0:1, 0:1]).then_inc(sRho.h, 1)
            prog.on("scalar", _rho_rows)
            sRho.bump(2)
            promise(f"bank{batch * 2}", sRho)
            promise(f"bank{batch * 2 + 1}", sRho)
            prog.wait("vector", sRho, sRho.n)

            def _rho_rec(V):
                V.reciprocal(rowA[0:1, 0:1024], rowA[0:1, 0:1024])
                V.memset(alpha[:, 7:8], 0.0)
                V.memset(alpha[:, 7:8], 0.0).then_inc(sRho.h, 1)
            prog.on("vector", _rho_rec)
            sRho.bump(1)
            for j in range(2):
                sw = batch * 2 + j
                b = 4 + (sw % 2)
                te_bank(b)
                if j == 0:
                    prog.wait("tensor", sRho, sRho.n)
                prog.on("tensor", lambda T, j=j, b=b:
                        T.matmul(pb[b][:, :], c_ones_row[0:1, :],
                                 rowA[0:1, j * 512:(j + 1) * 512],
                                 start=True, stop=True).then_inc(sRbc.h, 1))
                sRbc.bump(1)
                prog.wait("vector", sRbc, sRbc.n)
                prog.on("vector", lambda V, sw=sw, b=b:
                        V.tensor_copy(rho_bc[:, sw * 512:(sw + 1) * 512], pb[b][:, :])
                        .then_inc(sRev.h, 1))
                sRev.bump(1)
                promise(f"bank{b}", sRev)
        sH = S("sH")
        prog.wait("vector", sRev, sRev.n)
        for kc in range(KC):
            prog.on("vector", lambda V, kc=kc:
                    V.scalar_tensor_tensor(h_fm(kc), xbf(kc), v_wqn[:, kc:kc + 1],
                                           rho_bc, ALU.mult, ALU.mult)
                    .then_inc(sH.h, 1))
            sH.bump(1)
        H_ALL = sH.n

        LdA = S("LdA")
        prog.wait("gpsimd", ccA, 1)
        prog.on("gpsimd", lambda G:
                G.dma_start(out=arow[0:1, 0:2], in_=cc_out_a.ap())
                .then_inc(LdA.h, 16))
        LdA.bump(16)
        sAl = S("sAl")
        prog.wait("vector", LdA, 16)

        def _alpha(V):
            V.tensor_scalar_mul(arow[0:1, 16:17], arow[0:1, 0:1],
                                1.0 / (6144.0 * 2048.0))
            V.tensor_scalar_mul(arow[0:1, 17:18], arow[0:1, 1:2],
                                1.0 / (2048.0 * 2048.0))
            V.memset(arow[0:1, 30:31], 0.0)
            V.tensor_scalar_max(arow[0:1, 16:17], arow[0:1, 16:17], 1e-5)
            V.tensor_scalar_max(arow[0:1, 17:18], arow[0:1, 17:18], 1e-5)
            V.memset(arow[0:1, 30:31], 0.0)
            V.reciprocal(arow[0:1, 18:19], arow[0:1, 16:17])
            V.reciprocal(arow[0:1, 19:20], arow[0:1, 17:18])
            V.memset(arow[0:1, 30:31], 0.0)
            V.tensor_scalar_mul(rowB[0:1, 0:1], arow[0:1, 16:17], DINV)
            V.tensor_copy(rowB[0:1, 1:2], arow[0:1, 16:17])
            V.tensor_copy(rowB[0:1, 2:3], arow[0:1, 18:19])
            V.tensor_copy(rowB[0:1, 3:4], arow[0:1, 17:18])
            V.tensor_copy(rowB[0:1, 4:5], arow[0:1, 19:20])
            V.memset(arow[0:1, 30:31], 0.0)
            V.memset(arow[0:1, 30:31], 0.0).then_inc(sAl.h, 1)
        prog.on("vector", _alpha)
        sAl.bump(1)
        promise("bank5", sAl)
        sMMb = S("sMMb")
        aDone = S("aDone")
        te_bank(5)
        prog.wait("tensor", sAl, 1)
        prog.on("tensor", lambda T: T.matmul(pb[5][:, 0:8], c_ones_row[0:1, :],
                                             rowB[0:1, 0:8], start=True, stop=True)
                .then_inc(sMMb.h, 1))
        sMMb.bump(1)
        prog.wait("vector", sMMb, 1)
        prog.on("vector", lambda V: V.tensor_copy(alpha[:, 0:8], pb[5][:, 0:8])
                .then_inc(aDone.h, 1))
        aDone.bump(1)
        promise("bank5", aDone)
        if debug:
            sDbgM = S("sDbgM")
            prog.wait("sync", aDone, 1)
            prog.on("sync", lambda E: E.dma_start(out=dbg["misc"][:, 0:8],
                                                  in_=alpha[:, 0:8])
                    .then_inc(sDbgM.h, 16))
            sDbgM.bump(16)
            prog.on("sync", lambda E: E.dma_start(out=dbg["misc"][:, 8:10],
                                                  in_=acc2[:, 0:2])
                    .then_inc(sDbgM.h, 16))
            sDbgM.bump(16)
            prog.on("sync", lambda E: E.dma_start(out=dbg["misc"][0:1, 10:12],
                                                  in_=arow[0:1, 0:2])
                    .then_inc(sDbgM.h, 16))
            sDbgM.bump(16)

        def finish_and_emit(waits):
            for eng, sem in waits:
                prog.wait(eng, sem, sem.n)
            with nc.Block() as block:
                @block.sync
                def _(E):
                    for fn in prog.streams["sync"]:
                        fn(E)

                @block.scalar
                def _(E):
                    for fn in prog.streams["scalar"]:
                        fn(E)

                @block.vector
                def _(E):
                    for fn in prog.streams["vector"]:
                        fn(E)

                @block.tensor
                def _(E):
                    for fn in prog.streams["tensor"]:
                        fn(E)

                @block.gpsimd
                def _(E):
                    for fn in prog.streams["gpsimd"]:
                        fn(E)
            return nc

        if phases == 0:
            return finish_and_emit([("sync", sH), ("sync", aDone)])

        sWq = [S("sWq0"), S("sWq1")]
        sWv = S("sWv")
        sStg = [S("sStg0"), S("sStg1")]
        sI32 = S("sI32")
        swctr = [0]

        def stream_weight(dst_ap, col0, ncols, slot_sem, src=None):
            if src is None:
                src = wqT
            total = KC * ncols
            nsub = total // 512
            kcps = 512 // ncols
            for sub in range(nsub):
                sp = swctr[0] % 2
                swctr[0] += 1
                kc0 = sub * kcps
                pwait("gpsimd", f"wstg{sp}")
                tgt = sStg[sp].bump(16)
                prog.on("gpsimd", lambda G, kc0=kc0, kcps=kcps, sp=sp, col0=col0,
                        ncols=ncols, src=src:
                        G.dma_start(
                            out=wstg(sp).rearrange("p (a b) -> p a b", b=ncols),
                            in_=src[kc0 * P:(kc0 + kcps) * P, col0:col0 + ncols]
                            .rearrange("(a p) b -> p a b", p=P))
                        .then_inc(sStg[sp].h, 16))
                prog.wait("scalar", sStg[sp], tgt)
                if swctr[0] <= 2:
                    prog.wait("scalar", aDone, 1)
                pwait("scalar", "wi32")
                prog.on("scalar", lambda Sc, sp=sp:
                        Sc.activation(out=wi32[:, 0:512], in_=wstg(sp),
                                      func=AF.Copy, scale=alpha[:, 2:3])
                        .then_inc(sI32.h, 1))
                sI32.bump(1)
                promise(f"wstg{sp}", sI32)
                prog.wait("vector", sI32, sI32.n)
                o0 = sub * 512
                prog.on("vector", lambda V, dst_ap=dst_ap, o0=o0:
                        V.tensor_scalar(dst_ap[:, o0:o0 + 512], wi32[:, 0:512],
                                        1.0, -1.0, ALU.min, ALU.max)
                        .then_inc(slot_sem.h, 1))
                slot_sem.bump(1)
                promise("wi32", slot_sem)

        def issue_head_weights(h, first=False):
            sl = h % 2
            if first:
                prog.wait("gpsimd", sAbs, ABS_ALL)
                if h == 1:
                    prog.wait("gpsimd", sH, H_ALL)
            pwait("gpsimd", f"wqk{sl}")
            stream_weight(wq_sl(sl), h * P, P, sWq[sl])
            stream_weight(wk_sl(sl), 2048 + h * P, P, sWq[sl])

        def issue_pair_v(g, first=False):
            if first:
                prog.wait("gpsimd", sH, H_ALL)
            pwait("gpsimd", "wv")
            stream_weight(wv_sl, 4096 + g * 256, 256, sWv)

        issue_head_weights(0, first=True)
        if p1_heads > 1:
            issue_head_weights(1, first=True)
        issue_pair_v(0, first=True)

        sQKe = S("sQKe")
        sVe = S("sVe")
        sPrepH = S("sPrepH")
        sPrepHs = S("sPrepHs")
        sScT = S("sScT")
        sIOe = S("sIOe")
        sSup = S("sSup")
        sKtm = S("sKtm")
        sCor = S("sCor")
        sStgCp = S("sStgCp")
        sStgH = S("sStgH")
        ccH = S("ccH")
        sGat = S("sGat")
        sSel = S("sSel")
        sMMq = S("sMMq")
        sLamB = S("sLamB")
        sQt = S("sQt")
        sTr = S("sTr")

        def emit_corrections(hp):
            psl = hp % 2
            if p1_relay != 3:
                prog.wait("gpsimd", ccH, hp + 1)
                prog.wait("gpsimd", sSel, sSel.n)
                tgt = sGat.bump(16)
                prog.on("gpsimd", lambda G, hp=hp:
                        G.dma_start(out=sall[:].rearrange("p (r j) -> p r j", j=P),
                                    in_=cc_out_h[hp].ap()
                                    .rearrange("(r p) j -> p r j", p=P))
                        .then_inc(sGat.h, 16))
                prog.wait("vector", sGat, tgt)
            if p1_relay == 4:
                return
            prog.wait("vector", sStgH, sStgH.n)

            def _sel(V):
                V.tensor_scalar(ssel_bf[:], sall[:, 0:128], c_onehot[:, 0:1],
                                None, ALU.mult)
                for r in range(1, 8):
                    ins = V.scalar_tensor_tensor(
                        ssel_bf[:], sall[:, r * 128:(r + 1) * 128],
                        c_onehot[:, r:r + 1], ssel_bf[:], ALU.mult, ALU.add)
                    if r == 7:
                        ins.then_inc(sSel.h, 1)
            if p1_relay != 6:
                prog.on("vector", _sel)
            else:
                prog.on("vector", lambda V: V.tensor_copy(ssel_bf[:], sall[:, 0:128])
                        .then_inc(sSel.h, 1))
            sSel.bump(1)
            if p1_relay == 5:
                return
            for w in range(2):
                te_bank(2)
                te_bank(3)
                if w == 0:
                    prog.wait("tensor", sSel, sSel.n)

                def _cmm(T, w=w, psl=psl):
                    for j in range(8):
                        c = 8 * w + j
                        b = 2 + j // 4
                        col = (j % 4) * 128
                        mm = T.matmul(pb[b][:, col:col + 128], ssel_bf[:],
                                      q_fm(psl)[:, c * 128:(c + 1) * 128],
                                      start=True, stop=True)
                        if j == 7:
                            mm.then_inc(sCor.h, 1)
                prog.on("tensor", _cmm)
                sCor.bump(1)
                prog.wait("vector", sCor, sCor.n)

                def _cev(V, w=w, hp=hp, psl=psl):
                    for j in range(8):
                        c = 8 * w + j
                        b = 2 + j // 4
                        col = (j % 4) * 128
                        V.scalar_tensor_tensor(s_stage[:], pb[b][:, col:col + 128],
                                               lamTp(psl)[:, c:c + 1], lam_t(psl),
                                               ALU.mult, ALU.mult)
                        ins = V.tensor_add(attn(hp)[:, c * 128:(c + 1) * 128],
                                           attn(hp)[:, c * 128:(c + 1) * 128],
                                           s_stage[:])
                        if j == 7:
                            ins.then_inc(sCor.h, 1)
                prog.on("vector", _cev)
                sCor.bump(1)
                promise("bank2", sCor)
                promise("bank3", sCor)

        for h in range(p1_heads):
            sl = h % 2
            g = h // 2

            if h == 0:
                prog.wait("scalar", Lc, LC_ALL)
                prog.wait("scalar", sPrep, PREP_ALL)
                prog.wait("scalar", sXMM, XMM_ALL)
                prog.wait("scalar", sH, H_ALL)
            prog.wait("scalar", sLamB, sLamB.n)

            def _hb_sc(Sc, h=h, sl=sl):
                Sc.activation(out=lam2(sl), in_=c_GI[:], func=AF.Exp,
                              scale=c_negs[:, h:h + 1])
                Sc.activation(out=lamTp(sl), in_=c_J128[:], func=AF.Exp,
                              scale=c_negs[:, h:h + 1])
                Sc.activation(out=lrev_col(sl), in_=c_irev[:], func=AF.Exp,
                              scale=c_negs[:, h:h + 1])
                Sc.activation(out=mask_t(sl), in_=c_diffT[:], func=AF.Exp,
                              scale=c_negs[:, h:h + 1])
                Sc.activation(out=rowB[0:1, 0:128], in_=c_iota_row[:],
                              func=AF.Exp, scale=c_negs[0:1, h:h + 1])
                Sc.copy(arow[0:1, 24:25], c_cv[0:1, 0:1])
                Sc.copy(arow[0:1, 24:25], c_cv[0:1, 0:1]) \
                    .then_inc(sPrepHs.h, 1)
            prog.on("scalar", _hb_sc)
            sPrepHs.bump(1)
            prog.wait("vector", sPrepHs, sPrepHs.n)
            if h == 0:
                prog.wait("vector", sXMM, XMM_ALL)

            def _hb_ve(V, sl=sl):
                V.tensor_scalar(lam2(sl), lam2(sl), c_gate[:, 0:1], None, ALU.mult)
                V.tensor_scalar(lamTp(sl), lamTp(sl), c_gate[:, 0:1], None, ALU.mult)
                V.tensor_mul(mask_t(sl), mask_t(sl), c_causal[:]) \
                    .then_inc(sPrepH.h, 1)
            prog.on("vector", _hb_ve)
            sPrepH.bump(1)
            te_bank(5)
            prog.wait("tensor", sPrepHs, sPrepHs.n)
            prog.on("tensor", lambda T:
                    T.matmul(pb[5][:, 0:128], c_ones_row[0:1, :],
                             rowB[0:1, 0:128], start=True, stop=True)
                    .then_inc(sLamB.h, 1))
            sLamB.bump(1)
            prog.wait("vector", sLamB, sLamB.n)
            prog.on("vector", lambda V, sl=sl:
                    V.tensor_copy(lam_t(sl), pb[5][:, 0:128])
                    .then_inc(sPrepH.h, 1))
            sPrepH.bump(1)
            promise("bank5", sPrepH)

            cor_prev = sCor.n

            if h == 0:
                prog.wait("tensor", sH, H_ALL)
            prog.wait("tensor", sWq[sl], sWq[sl].n)
            if cor_prev:
                prog.wait("tensor", sCor, cor_prev)
            for sw in range(4):
                b = sw % 2
                te_bank(b)

                def _qmm(T, sw=sw, b=b, sl=sl):
                    for kc in range(KC):
                        mm = T.matmul(pb[b][:, :],
                                      wq_sl(sl)[:, kc * 128:(kc + 1) * 128],
                                      h_fm(kc)[:, sw * 512:(sw + 1) * 512],
                                      start=(kc == 0), stop=(kc == KC - 1))
                        if kc == KC - 1:
                            mm.then_inc(sMMq.h, 1)
                prog.on("tensor", _qmm)
                sMMq.bump(1)
                prog.wait("scalar", sMMq, sMMq.n)
                if sw == 0 and cor_prev:
                    prog.wait("scalar", sCor, cor_prev)
                prog.on("scalar", lambda Sc, sw=sw, b=b, sl=sl:
                        Sc.activation(out=q_fm(sl)[:, sw * 512:(sw + 1) * 512],
                                      in_=pb[b][:, :], func=AF.Copy,
                                      scale=alpha[:, 0:1])
                        .then_inc(sQKe.h, 1))
                sQKe.bump(1)
                promise(f"bank{b}", sQKe)
            for sw in range(4):
                b = 2 + sw % 2
                te_bank(b)

                def _kmm(T, sw=sw, b=b, sl=sl):
                    for kc in range(KC):
                        mm = T.matmul(pb[b][:, :],
                                      wk_sl(sl)[:, kc * 128:(kc + 1) * 128],
                                      h_fm(kc)[:, sw * 512:(sw + 1) * 512],
                                      start=(kc == 0), stop=(kc == KC - 1))
                        if kc == KC - 1:
                            mm.then_inc(sMMq.h, 1)
                prog.on("tensor", _kmm)
                sMMq.bump(1)
                prog.wait("scalar", sMMq, sMMq.n)
                prog.on("scalar", lambda Sc, sw=sw, b=b:
                        Sc.activation(out=k_fm[:, sw * 512:(sw + 1) * 512],
                                      in_=pb[b][:, :], func=AF.Copy,
                                      scale=alpha[:, 1:2])
                        .then_inc(sQKe.h, 1))
                sQKe.bump(1)
                promise(f"bank{b}", sQKe)
                if sw == 3:
                    promise(f"wqk{sl}", sQKe)
            if h % 2 == 0:
                prog.wait("tensor", sWv, sWv.n)
                for c in range(CH):
                    b = 4 + (c % 2)
                    te_bank(b)

                    def _vmm(T, c=c, b=b):
                        for kc in range(KC):
                            mm = T.matmul(pb[b][:, 0:256],
                                          h_fm(kc)[:, c * 128:(c + 1) * 128],
                                          wv_sl[:, kc * 256:(kc + 1) * 256],
                                          start=(kc == 0), stop=(kc == KC - 1))
                            if kc == KC - 1:
                                mm.then_inc(sMMq.h, 1)
                    prog.on("tensor", _vmm)
                    sMMq.bump(1)
                    prog.wait("scalar", sMMq, sMMq.n)
                    prog.on("scalar", lambda Sc, c=c, b=b:
                            Sc.activation(out=v_tm[:, c * 256:(c + 1) * 256],
                                          in_=pb[b][:, 0:256], func=AF.Copy,
                                          scale=alpha[:, 1:2])
                            .then_inc(sVe.h, 1))
                    sVe.bump(1)
                    promise(f"bank{b}", sVe)
                    if c == CH - 1:
                        promise("wv", sVe)
                if g + 1 < H // 2:
                    issue_pair_v(g + 1)
            if h + 2 < p1_heads:
                issue_head_weights(h + 2)
            if h >= 1 and p1_relay >= 2:
                emit_corrections(h - 1)

            prog.wait("vector", sQKe, sQKe.n)
            prog.wait("vector", sPrepH, sPrepH.n)

            def _qt(V, sl=sl):
                rep = lam_t(sl).unsqueeze(1).broadcast_to((P, CH, P))
                q3 = q_fm(sl).rearrange("p (c t) -> p c t", t=P)
                o3 = qt_fm[:].rearrange("p (c t) -> p c t", t=P)
                V.tensor_tensor(o3, q3, rep, ALU.mult).then_inc(sQt.h, 1)
            prog.on("vector", _qt)
            sQt.bump(1)

            te_pt(0)
            prog.wait("tensor", sQKe, sQKe.n)
            prog.on("tensor", lambda T:
                    T.transpose(ptA[:, 0:128], k_fm[:, 0:128], c_ident[:])
                    .then_inc(sTr.h, 1))
            sTr.bump(1)
            prog.wait("scalar", sTr, sTr.n)
            prog.on("scalar", lambda Sc, sl=sl:
                    Sc.activation(out=ktm(0), in_=ptA[:, 0:128], func=AF.Copy,
                                  scale=lrev_col(sl))
                    .then_inc(sKtm.h, 1))
            sKtm.bump(1)
            promise("pt0", sKtm)

            vbase = (h % 2) * 128
            for c in range(CH):
                b_s = c % 2
                te_bank(b_s)
                if c == 0:
                    prog.wait("tensor", sQt, sQt.n)
                prog.on("tensor", lambda T, c=c, b=b_s, sl=sl:
                        T.matmul(pb[b][:, 0:128], k_fm[:, c * 128:(c + 1) * 128],
                                 q_fm(sl)[:, c * 128:(c + 1) * 128],
                                 start=True, stop=True).then_inc(sMMq.h, 1))
                sMMq.bump(1)
                prog.wait("vector", sMMq, sMMq.n)
                if c == 0:
                    prog.wait("vector", sPrepH, sPrepH.n)
                prog.on("vector", lambda V, c=c, b=b_s, sl=sl:
                        V.tensor_mul(scT(c % 2), pb[b][:, 0:128], mask_t(sl))
                        .then_inc(sScT.h, 1))
                sScT.bump(1)
                promise(f"bank{b_s}", sScT)
                b_kv = 4 + (c % 2)
                te_bank(b_kv)
                prog.wait("tensor", sKtm, sKtm.n)
                if c == 0:
                    prog.wait("tensor", sVe, sVe.n)
                prog.on("tensor", lambda T, c=c, b=b_kv, vbase=vbase:
                        T.matmul(pb[b][:, 0:128], ktm(c % 2),
                                 v_tm[:, c * 256 + vbase:c * 256 + vbase + 128],
                                 start=True, stop=True).then_inc(sMMq.h, 1))
                sMMq.bump(1)
                if c + 1 < CH:
                    np_ = (c + 1) % 2
                    te_pt(np_)
                    prog.on("tensor", lambda T, c=c, np_=np_:
                            T.transpose((ptA if np_ == 0 else ptB)[:, 0:128],
                                        k_fm[:, (c + 1) * 128:(c + 2) * 128],
                                        c_ident[:]).then_inc(sTr.h, 1))
                    sTr.bump(1)
                    prog.wait("scalar", sTr, sTr.n)
                    prog.on("scalar", lambda Sc, np_=np_, sl=sl:
                            Sc.activation(out=ktm(np_),
                                          in_=(ptA if np_ == 0 else ptB)[:, 0:128],
                                          func=AF.Copy, scale=lrev_col(sl))
                            .then_inc(sKtm.h, 1))
                    sKtm.bump(1)
                    promise(f"pt{np_}", sKtm)
                b_io = 2 + (c % 2)
                te_bank(b_io)
                prog.wait("tensor", sScT, sScT.n)
                if c > 0:
                    prog.wait("tensor", sSup, sSup.n)

                def _iomm(T, c=c, b=b_io, vbase=vbase):
                    mm1 = T.matmul(pb[b][:, 0:128],
                                   v_tm[:, c * 256 + vbase:c * 256 + vbase + 128],
                                   scT(c % 2), start=True, stop=(c == 0))
                    if c > 0:
                        mm2 = T.matmul(pb[b][:, 0:128], sbf((c + 1) % 2),
                                       qt_fm[:, c * 128:(c + 1) * 128],
                                       start=False, stop=True)
                        mm2.then_inc(sMMq.h, 1)
                    else:
                        mm1.then_inc(sMMq.h, 1)
                prog.on("tensor", _iomm)
                sMMq.bump(1)
                prog.wait("vector", sMMq, sMMq.n)
                prog.on("vector", lambda V, c=c, b=b_io, h=h:
                        V.tensor_copy(attn(h)[:, c * 128:(c + 1) * 128],
                                      pb[b][:, 0:128])
                        .then_inc(sIOe.h, 1))
                sIOe.bump(1)
                promise(f"bank{b_io}", sIOe)

                def _sup(V, c=c, b=b_kv, h=h):
                    if c == 0:
                        V.tensor_copy(sbf(0), pb[b][:, 0:128]).then_inc(sSup.h, 1)
                    else:
                        V.scalar_tensor_tensor(sbf(c % 2), sbf((c + 1) % 2),
                                               c_lamT[:, h:h + 1], pb[b][:, 0:128],
                                               ALU.mult, ALU.add) \
                            .then_inc(sSup.h, 1)
                prog.on("vector", _sup)
                sSup.bump(1)
                promise(f"bank{b_kv}", sSup)

            if p1_relay == 0:
                continue
            prog.wait("vector", sStgH, sStgH.n)

            def _stgcp(V):
                V.tensor_copy(s_stage[:], sbf(1))
                V.memset(alpha[:, 7:8], 0.0)
                V.memset(alpha[:, 7:8], 0.0).then_inc(sStgCp.h, 1)
            prog.on("vector", _stgcp)
            sStgCp.bump(1)
            prog.wait("sync", sStgCp, sStgCp.n)
            prog.on("sync", lambda E, h=h:
                    E.dma_start(out=cc_in_h[h].ap(), in_=s_stage[:])
                    .then_inc(sStgH.h, 16))
            sStgH.bump(16)
            prog.wait("gpsimd", sStgH, sStgH.n)
            prog.on("gpsimd", lambda G, h=h: G.collective_compute(
                "AllGather", ALU.bypass, replica_groups=[list(range(8))],
                ins=[cc_in_h[h].ap().opt()], outs=[cc_out_h[h].ap().opt()])
                .then_inc(ccH.h, 1))
            ccH.bump(1)

        if p1_relay >= 2:
            emit_corrections(p1_heads - 1)
        P1_CORR_ALL = sCor.n

        if debug:
            sDbg = S("sDbg")
            prog.wait("sync", sCor, P1_CORR_ALL)
            prog.on("sync", lambda E: E.dma_start(out=dbg["attn"],
                                                  in_=R2[:, 0:KC * TOK])
                    .then_inc(sDbg.h, 16))
            sDbg.bump(16)
            prog.wait("sync", sH, H_ALL)
            prog.on("sync", lambda E: E.dma_start(out=dbg["h"], in_=R1[:, 0:KC * TOK])
                    .then_inc(sDbg.h, 16))
            sDbg.bump(16)

        if phases == 1:
            return finish_and_emit([("sync", sCor), ("sync", sIOe),
                                    ("sync", sStgH)])

        B1 = S("B1")
        for e in ENGINES:
            prog.on(e, lambda E: E.nop().then_inc(B1.h, 1))
        B1.bump(5)
        for e in ENGINES:
            prog.wait(e, B1, 5)

        if debug:
            prog.wait("vector", sems["sDbg"], sems["sDbg"].n)
        sWpT = S("sWpT")
        sWpSt = S("sWpSt")
        wpb_bounce = lambda p: (R3[:, 20480:22528] if p == 0
                                else R1[:, 0:2048])
        for kc in range(KC):
            for quad in range(4):
                it = kc * 4 + quad
                sp = it % 2
                pwait("gpsimd", f"wstg{sp}")
                tgt = sStg[sp].bump(16)
                prog.on("gpsimd", lambda G, kc=kc, quad=quad, sp=sp:
                        G.dma_start(out=wstg(sp),
                                    in_=wpT[kc * P:(kc + 1) * P,
                                            quad * 512:(quad + 1) * 512])
                        .then_inc(sStg[sp].h, 16))
                prog.wait("scalar", sStg[sp], tgt)
                pwait("scalar", "wi32")
                prog.on("scalar", lambda Sc, sp=sp:
                        Sc.activation(out=wi32[:, 0:512], in_=wstg(sp),
                                      func=AF.Copy, scale=alpha[:, 4:5])
                        .then_inc(sI32.h, 1))
                sI32.bump(1)
                promise(f"wstg{sp}", sI32)
                prog.wait("vector", sI32, sI32.n)
                bp = it % 2
                if it >= 2:
                    prog.wait("vector", sWpSt, sWpSt.n - 64)
                prog.on("vector", lambda V, bp=bp:
                        V.tensor_scalar(wpb_bounce(bp)[:, 0:512], wi32[:, 0:512],
                                        1.0, -1.0, ALU.min, ALU.max)
                        .then_inc(sWpT.h, 1))
                sWpT.bump(1)
                promise("wi32", sWpT)
                prog.wait("sync", sWpT, sWpT.n)
                for dsub in range(4):
                    prog.on("sync", lambda E, kc=kc, quad=quad, bp=bp, dsub=dsub:
                            E.dma_start(out=wpbf[(quad * 4 + dsub) * P:
                                                 (quad * 4 + dsub + 1) * P,
                                                 kc * P:(kc + 1) * P],
                                        in_=wpb_bounce(bp)[:, dsub * 128:
                                                           (dsub + 1) * 128])
                            .then_inc(sWpSt.h, 16))
                    sWpSt.bump(16)
        WPT_ALL = sWpT.n
        WPST_ALL = sWpSt.n

        sSq = S("sSq")
        sStM = S("sStM")
        sScR = S("sScR")
        sRbc2 = S("sRbc2")
        sRev2 = S("sRev2")
        for sw in range(4):
            bh, bl = 2, 3
            te_bank(bh)
            te_bank(bl)
            for ec in range(KC):
                p = ec % 2
                prog.wait("vector", sStM, max(0, sStM.n - 1))

                def _sqv(V, ec=ec, sw=sw, p=p):
                    V.tensor_mul(p2_sq(p), attn(ec)[:, sw * 512:(sw + 1) * 512],
                                 attn(ec)[:, sw * 512:(sw + 1) * 512])
                    V.memset(alpha[:, 7:8], 0.0).then_inc(sSq.h, 1)
                prog.on("vector", _sqv)
                sSq.bump(1)
                prog.wait("tensor", sSq, sSq.n)

                def _stmm(T, ec=ec, p=p, bh=bh, bl=bl):
                    T.matmul(pb[bh][0:1, :], v_shl[:, ec:ec + 1], p2_sq(p),
                             start=(ec == 0), stop=(ec == KC - 1))
                    mm = T.matmul(pb[bl][0:1, :], v_shl[:, KC + ec:KC + ec + 1],
                                  p2_sq(p), start=(ec == 0), stop=(ec == KC - 1))
                    mm.then_inc(sStM.h, 1)
                prog.on("tensor", _stmm)
                sStM.bump(1)
            r = sw % 2
            prog.wait("scalar", sStM, sStM.n)
            if sw >= 2:
                prog.wait("scalar", sRbc2, sw - 1)
            def _rowcp(Sc, r=r, bh=bh):
                Sc.copy(rowA[0:1, r * 512:(r + 1) * 512], pb[bh][0:1, :])
                Sc.copy(arow[0:1, 24:25], c_cv[0:1, 0:1])
                Sc.copy(arow[0:1, 24:25], c_cv[0:1, 0:1]).then_inc(sScR.h, 1)
            prog.on("scalar", _rowcp)
            sScR.bump(1)
            promise(f"bank{bh}", sScR)
            prog.wait("vector", sScR, sScR.n)

            def _rowadd(V, r=r, bl=bl):
                V.tensor_add(rowA[0:1, r * 512:(r + 1) * 512],
                             rowA[0:1, r * 512:(r + 1) * 512], pb[bl][0:1, :])
                V.memset(alpha[:, 7:8], 0.0)
                V.memset(alpha[:, 7:8], 0.0).then_inc(sScR.h, 1)
            prog.on("vector", _rowadd)
            sScR.bump(1)
            promise(f"bank{bl}", sScR)
            prog.wait("scalar", sScR, sScR.n)

            def _rhosq(Sc, r=r):
                Sc.activation(out=rowA[0:1, r * 512:(r + 1) * 512],
                              in_=rowA[0:1, r * 512:(r + 1) * 512],
                              func=AF.Sqrt, bias=c_cv[0:1, 3:4])
                Sc.copy(arow[0:1, 24:25], c_cv[0:1, 0:1])
                Sc.copy(arow[0:1, 24:25], c_cv[0:1, 0:1]).then_inc(sScR.h, 1)
            prog.on("scalar", _rhosq)
            sScR.bump(1)
            prog.wait("vector", sScR, sScR.n)

            def _rhorec(V, r=r):
                V.reciprocal(rowA[0:1, r * 512:(r + 1) * 512],
                             rowA[0:1, r * 512:(r + 1) * 512])
                V.memset(alpha[:, 7:8], 0.0)
                V.memset(alpha[:, 7:8], 0.0).then_inc(sScR.h, 1)
            prog.on("vector", _rhorec)
            sScR.bump(1)
            b = 4 + sw % 2
            te_bank(b)
            prog.wait("tensor", sScR, sScR.n)
            prog.on("tensor", lambda T, r=r, b=b:
                    T.matmul(pb[b][:, :], c_ones_row[0:1, :],
                             rowA[0:1, r * 512:(r + 1) * 512],
                             start=True, stop=True).then_inc(sRbc2.h, 1))
            sRbc2.bump(1)
            prog.wait("vector", sRbc2, sRbc2.n)
            prog.on("vector", lambda V, sw=sw, b=b:
                    V.tensor_copy(rho12_bc[:, sw * 512:(sw + 1) * 512], pb[b][:, :])
                    .then_inc(sRev2.h, 1))
            sRev2.bump(1)
            promise(f"bank{b}", sRev2)

        if debug:
            prog.wait("sync", sRev2, sRev2.n)
            prog.on("sync", lambda E: E.dma_start(out=dbg["rho"],
                                                  in_=rho12_bc)
                    .then_inc(sems["sDbg"].h, 16))
            sems["sDbg"].bump(16)
            prog.wait("sync", sWpSt, WPST_ALL)
            prog.on("sync", lambda E: E.dma_start(out=dbg["wp"],
                                                  in_=wpbf[0:P, 0:512])
                    .then_inc(sems["sDbg"].h, 16))
            sems["sDbg"].bump(16)

        sZ = S("sZ")
        sPm = S("sPm")
        sR1 = S("sR1")
        sR1b = S("sR1b")
        sLnM = S("sLnM")
        sLnR = S("sLnR")
        sX1 = S("sX1")
        sX2 = S("sX2")
        sMS = S("sMS")
        sB2c = S("sB2c")
        sWpL = [S("sWpLa"), S("sWpLb")]
        for sw in range(4):
            half, tl0 = sw // 2, (sw % 2) * 512
            prog.wait("vector", sRev2, sRev2.n)

            def _z(V, sw=sw, half=half, tl0=tl0):
                for ec in range(KC):
                    ins = V.scalar_tensor_tensor(
                        x1_sl(ec, half, tl0, 512),
                        attn(ec)[:, sw * 512:(sw + 1) * 512],
                        v_w12[:, ec:ec + 1],
                        rho12_bc[:, sw * 512:(sw + 1) * 512],
                        ALU.mult, ALU.mult)
                    if ec == KC - 1:
                        ins.then_inc(sZ.h, 1)
            prog.on("vector", _z)
            sZ.bump(1)
            for doc in range(KC):
                sp = doc % 2
                pwait("gpsimd", f"wp{sp}")
                if sw == 0 and doc < 2:
                    prog.wait("gpsimd", sWpSt, WPST_ALL)
                tgtw = sWpL[sp].bump(16)
                prog.on("gpsimd", lambda G, doc=doc, sp=sp:
                        G.dma_start(out=wpsl(sp),
                                    in_=wpbf[doc * P:(doc + 1) * P, :])
                        .then_inc(sWpL[sp].h, 16))
                pwait("sync", f"x2{sp}")
                tgt = sX2.bump(16)
                prog.on("sync", lambda E, doc=doc, sw=sw, sp=sp:
                        E.dma_start(out=p2_xstg(sp),
                                    in_=xT[doc * P:(doc + 1) * P,
                                           sw * 512:(sw + 1) * 512])
                        .then_inc(sX2.h, 16))
                b = doc % 2
                te_bank(b)
                prog.wait("tensor", sZ, sZ.n)
                prog.wait("tensor", sWpL[sp], tgtw)

                def _pmm(T, doc=doc, half=half, tl0=tl0, b=b, sp=sp):
                    for ec in range(KC):
                        mm = T.matmul(pb[b][:, :],
                                      wpsl(sp)[:, ec * 128:(ec + 1) * 128],
                                      x1_sl(ec, half, tl0, 512),
                                      start=(ec == 0), stop=(ec == KC - 1))
                        if ec == KC - 1:
                            mm.then_inc(sPm.h, 1)
                prog.on("tensor", _pmm)
                sPm.bump(1)
                prog.wait("vector", sPm, sPm.n)
                prog.wait("vector", sX2, tgt)
                def _r1ev(V, doc=doc, b=b, sp=sp):
                    V.scalar_tensor_tensor(r1_swp(doc), pb[b][:, :],
                                           alpha[:, 3:4], p2_xstg(sp),
                                           ALU.mult, ALU.add)
                    V.memset(alpha[:, 7:8], 0.0)
                    V.memset(alpha[:, 7:8], 0.0).then_inc(sR1.h, 1)
                prog.on("vector", _r1ev)
                sR1.bump(1)
                promise(f"bank{b}", sR1)
                promise(f"x2{sp}", sR1)
                promise(f"wp{sp}", sR1)
                prog.wait("scalar", sR1, sR1.n)
                prog.wait("scalar", sLnM, max(0, sLnM.n - 1))

                def _r1bf(Sc, doc=doc):
                    Sc.activation(out=p2_r1bf(doc % 2), in_=r1_swp(doc),
                                  func=AF.Copy)
                    Sc.copy(arow[0:1, 24:25], c_cv[0:1, 0:1]) \
                        .then_inc(sR1b.h, 1)
                prog.on("scalar", _r1bf)
                sR1b.bump(1)
                prog.wait("vector", sR1b, sR1b.n)

                def _r1sq(V, doc=doc):
                    V.tensor_mul(p2_r1sq(doc % 2), p2_r1bf(doc % 2),
                                 p2_r1bf(doc % 2))
                    V.memset(alpha[:, 7:8], 0.0)
                    V.memset(alpha[:, 7:8], 0.0).then_inc(sR1b.h, 1)
                prog.on("vector", _r1sq)
                sR1b.bump(1)
                if doc == 0:
                    te_bank(2)
                    te_bank(3)
                prog.wait("tensor", sR1b, sR1b.n)

                def _lnmm(T, doc=doc):
                    T.matmul(pb[2][0:1, :], c_ones_col_bf[:], p2_r1bf(doc % 2),
                             start=(doc == 0), stop=(doc == KC - 1))
                    mm = T.matmul(pb[3][0:1, :], c_ones_col_bf[:], p2_r1sq(doc % 2),
                                  start=(doc == 0), stop=(doc == KC - 1))
                    mm.then_inc(sLnM.h, 1)
                prog.on("tensor", _lnmm)
                sLnM.bump(1)
            prog.wait("vector", sLnM, sLnM.n)

            def _lnr(V):
                V.tensor_scalar(rowB[0:1, 0:512], pb[2][0:1, :], 1.0 / 2048.0, 0.0,
                                ALU.mult, ALU.add)
                V.tensor_mul(rowB[0:1, 512:1024], rowB[0:1, 0:512], rowB[0:1, 0:512])
                V.scalar_tensor_tensor(rowB[0:1, 512:1024], pb[3][0:1, :],
                                       1.0 / 2048.0, rowB[0:1, 512:1024],
                                       ALU.mult, ALU.subtract)
                V.memset(alpha[:, 7:8], 0.0)
                V.memset(alpha[:, 7:8], 0.0).then_inc(sLnR.h, 1)
            prog.on("vector", _lnr)
            sLnR.bump(1)
            promise("bank2", sLnR)
            promise("bank3", sLnR)
            prog.wait("scalar", sLnR, sLnR.n)

            def _lnsq(Sc):
                Sc.activation(out=rowB[0:1, 512:1024], in_=rowB[0:1, 512:1024],
                              func=AF.Sqrt, bias=c_cv[0:1, 4:5])
                Sc.copy(arow[0:1, 24:25], c_cv[0:1, 0:1])
                Sc.copy(arow[0:1, 24:25], c_cv[0:1, 0:1]).then_inc(sLnR.h, 1)
            prog.on("scalar", _lnsq)
            sLnR.bump(1)
            prog.wait("vector", sLnR, sLnR.n)

            def _lnf(V):
                V.reciprocal(rowB[0:1, 512:1024], rowB[0:1, 512:1024])
                V.tensor_mul(rowB[0:1, 0:512], rowB[0:1, 0:512],
                             rowB[0:1, 512:1024])
                V.memset(alpha[:, 7:8], 0.0)
                V.memset(alpha[:, 7:8], 0.0).then_inc(sLnR.h, 1)
            prog.on("vector", _lnf)
            sLnR.bump(1)
            for i, (src0, dstt) in enumerate([(512, p2_sgbc), (0, p2_mubc)]):
                b = 4 + i
                te_bank(b)
                prog.wait("tensor", sLnR, sLnR.n)
                prog.on("tensor", lambda T, src0=src0, b=b:
                        T.matmul(pb[b][:, :], c_ones_row[0:1, :],
                                 rowB[0:1, src0:src0 + 512], start=True, stop=True)
                        .then_inc(sB2c.h, 1))
                sB2c.bump(1)
                prog.wait("vector", sB2c, sB2c.n)
                prog.on("vector", lambda V, dstt=dstt, b=b:
                        V.tensor_copy(dstt, pb[b][:, :])
                        .then_inc(sB2c.h, 1))
                sB2c.bump(1)
                promise(f"bank{b}", sB2c)
            prog.wait("vector", sB2c, sB2c.n)

            def _x1(V, sw=sw, half=half, tl0=tl0):
                for doc in range(KC):
                    V.tensor_mul(r1_swp(doc), r1_swp(doc), p2_sgbc)
                    V.tensor_sub(x1_sl(doc, half, tl0, 512), r1_swp(doc),
                                 p2_mubc)
                V.memset(alpha[:, 7:8], 0.0)
                V.memset(alpha[:, 7:8], 0.0).then_inc(sX1.h, 1)
            prog.on("vector", _x1)
            sX1.bump(1)
        X1_ALL = sX1.n

        if debug:
            prog.wait("sync", sX1, X1_ALL)
            prog.on("sync", lambda E: E.dma_start(out=dbg["x1a"], in_=R3[:, 0:20480])
                    .then_inc(sems["sDbg"].h, 16))
            sems["sDbg"].bump(16)
            prog.on("sync", lambda E: E.dma_start(out=dbg["x1b"],
                                                  in_=R1[:, 31744:44032])
                    .then_inc(sems["sDbg"].h, 16))
            sems["sDbg"].bump(16)

        sSp = S("sSp")
        prog.wait("sync", sX1, X1_ALL)
        prog.on("sync", lambda E: E.dma_start(out=x1h2[:, 0:10240],
                                              in_=R3[:, 10240:20480])
                .then_inc(sSp.h, 16))
        sSp.bump(16)
        prog.on("sync", lambda E: E.dma_start(out=x1h2[:, 10240:16384],
                                              in_=R1[:, 31744 + 6144:44032])
                .then_inc(sSp.h, 16))
        sSp.bump(16)

        if phases == 2:
            return finish_and_emit([("sync", sX1), ("sync", sSp)])

        B2 = S("B2")
        for e in ENGINES:
            prog.on(e, lambda E: E.nop().then_inc(B2.h, 1))
        B2.bump(5)
        for e in ENGINES:
            prog.wait(e, B2, 5)
        prog.wait("gpsimd", sSp, sSp.n)

        sWg = [S("sWgA"), S("sWgB")]
        sWu = [S("sWuA"), S("sWuB")]
        sWd = [S("sWdA"), S("sWdB")]
        sCs = S("sCs")
        sGuM = S("sGuM")
        sGuE = S("sGuE")
        sGu = S("sGu")
        sGsum = S("sGsum")
        sDnM = S("sDnM")
        sR2v = S("sR2v")
        sLn2M = S("sLn2M")
        sLn2R = S("sLn2R")
        sB3 = S("sB3")
        sOut = S("sOut")
        sMSl = S("sMSl")
        sX1r = S("sX1r")

        for sw in range(4):
            half, swl = sw // 2, sw % 2
            if sw == 2:
                prog.wait("sync", sGu, sGu.n)
                prog.wait("sync", sOut, sOut.n)
                prog.wait("sync", sCs, sCs.n)
                prog.wait("sync", sR2v, sR2v.n)
                tgt = sX1r.bump(32)
                prog.on("sync", lambda E:
                        E.dma_start(out=R3[:, 0:10240], in_=x1h2[:, 0:10240])
                        .then_inc(sX1r.h, 16))
                prog.on("sync", lambda E:
                        E.dma_start(out=R1[:, 31744:31744 + 6144],
                                    in_=x1h2[:, 10240:16384])
                        .then_inc(sX1r.h, 16))
                for e in ("tensor", "vector"):
                    prog.wait(e, sX1r, tgt)
            te_bank(4)
            if sw == 0:
                prog.wait("tensor", sX1, X1_ALL)

            def _csmm(T, swl=swl):
                for kc in range(KC):
                    mm = T.matmul(pb[4][0:1, :], c_ones_col_bf[:],
                                  x1_cur(kc, swl * 512, 512),
                                  start=(kc == 0), stop=(kc == KC - 1))
                    if kc == KC - 1:
                        mm.then_inc(sCs.h, 1)
            prog.on("tensor", _csmm)
            sCs.bump(1)
            prog.wait("scalar", sCs, sCs.n)

            def _csrow(Sc):
                Sc.copy(p3_rows[0:1, 0:512], pb[4][0:1, :])
                Sc.copy(arow[0:1, 24:25], c_cv[0:1, 0:1])
                Sc.copy(arow[0:1, 24:25], c_cv[0:1, 0:1]).then_inc(sCs.h, 1)
            prog.on("scalar", _csrow)
            sCs.bump(1)
            promise("bank4", sCs)
            for hc in range(HOC):
                sp = hc % 2
                pwait("gpsimd", f"wg{sp}")
                tgtg = sWg[sp].bump(16)
                prog.on("gpsimd", lambda G, hc=hc, sp=sp:
                        G.dma_start(out=wg_sl(sp),
                                    in_=wgT[hc * P:(hc + 1) * P, :])
                        .then_inc(sWg[sp].h, 16))
                pwait("gpsimd", f"wu{sp}")
                tgtu = sWu[sp].bump(16)
                prog.on("gpsimd", lambda G, hc=hc, sp=sp:
                        G.dma_start(out=wu_sl(sp),
                                    in_=wuT[hc * P:(hc + 1) * P, :])
                        .then_inc(sWu[sp].h, 16))
                te_bank(0)
                prog.wait("tensor", sWg[sp], tgtg)
                if hc == 0:
                    prog.wait("tensor", sCs, sCs.n)

                def _gmm(T, sp=sp, swl=swl):
                    for kc in range(KC):
                        T.matmul(pb[0][:, :], wg_sl(sp)[:, kc * 128:(kc + 1) * 128],
                                 x1_cur(kc, swl * 512, 512),
                                 start=(kc == 0), stop=False)
                    mm = T.matmul(pb[0][:, :], c_neg128_row[0:1, :],
                                  p3_rows[0:1, 0:512], start=False, stop=True)
                    mm.then_inc(sGuM.h, 1)
                prog.on("tensor", _gmm)
                sGuM.bump(1)
                te_bank(1)
                prog.wait("tensor", sWu[sp], tgtu)

                def _umm(T, sp=sp, swl=swl):
                    for kc in range(KC):
                        T.matmul(pb[1][:, :], wu_sl(sp)[:, kc * 128:(kc + 1) * 128],
                                 x1_cur(kc, swl * 512, 512),
                                 start=(kc == 0), stop=False)
                    mm = T.matmul(pb[1][:, :], c_neg128_row[0:1, :],
                                  p3_rows[0:1, 0:512], start=False, stop=True)
                    mm.then_inc(sGuM.h, 1)
                prog.on("tensor", _umm)
                sGuM.bump(1)
                prog.wait("scalar", sGuM, sGuM.n)
                prog.wait("scalar", sGu, max(0, sGu.n - 1))

                def _ge(Sc, hc=hc):
                    Sc.activation(out=gtmp(0), in_=pb[0][:, :], func=AF.Silu,
                                  scale=v_sgate[:, hc:hc + 1])
                    Sc.activation(out=utmp(0), in_=pb[1][:, :], func=AF.Copy,
                                  scale=v_sup[:, hc:hc + 1])
                    Sc.copy(arow[0:1, 24:25], c_cv[0:1, 0:1]) \
                        .then_inc(sGuE.h, 1)
                prog.on("scalar", _ge)
                sGuE.bump(1)
                prog.wait("vector", sGuE, sGuE.n)

                def _gu(V, hc=hc):
                    V.tensor_mul(gu_sl(hc), gtmp(0), utmp(0)).then_inc(sGu.h, 1)
                prog.on("vector", _gu)
                sGu.bump(1)
                promise("bank0", sGu)
                promise("bank1", sGu)
                promise(f"wg{sp}", sGu)
                promise(f"wu{sp}", sGu)
            te_bank(4)
            prog.wait("tensor", sGu, sGu.n)

            def _gsmm(T):
                for hc in range(HOC):
                    mm = T.matmul(pb[4][0:1, :], c_ones_col_bf[:], gu_sl(hc),
                                  start=(hc == 0), stop=(hc == HOC - 1))
                    if hc == HOC - 1:
                        mm.then_inc(sGsum.h, 1)
            prog.on("tensor", _gsmm)
            sGsum.bump(1)
            prog.wait("scalar", sGsum, sGsum.n)

            def _gsrow(Sc):
                Sc.copy(p3_rows[0:1, 512:1024], pb[4][0:1, :])
                Sc.copy(arow[0:1, 24:25], c_cv[0:1, 0:1])
                Sc.copy(arow[0:1, 24:25], c_cv[0:1, 0:1]).then_inc(sGsum.h, 1)
            prog.on("scalar", _gsrow)
            sGsum.bump(1)
            promise("bank4", sGsum)
            for doc in range(KC):
                sp = doc % 2
                pwait("gpsimd", f"wd{sp}")
                tgtd = sWd[sp].bump(16)
                prog.on("gpsimd", lambda G, doc=doc, sp=sp:
                        G.dma_start(out=wd_sl(sp),
                                    in_=wdT[doc * P:(doc + 1) * P, :])
                        .then_inc(sWd[sp].h, 16))
                b = 2 + doc % 2
                te_bank(b)
                prog.wait("tensor", sWd[sp], tgtd)
                if doc == 0:
                    prog.wait("tensor", sGsum, sGsum.n)

                def _dmm(T, doc=doc, sp=sp, b=b):
                    for hc in range(HOC):
                        T.matmul(pb[b][:, :], wd_sl(sp)[:, hc * 128:(hc + 1) * 128],
                                 gu_sl(hc), start=(hc == 0), stop=False)
                    mm = T.matmul(pb[b][:, :], c_neg128_row[0:1, :],
                                  p3_rows[0:1, 512:1024], start=False, stop=True)
                    mm.then_inc(sDnM.h, 1)
                prog.on("tensor", _dmm)
                sDnM.bump(1)
                prog.wait("vector", sDnM, sDnM.n)

                def _r2(V, doc=doc, swl=swl, b=b):
                    V.scalar_tensor_tensor(r2_swp(doc), pb[b][:, :],
                                           v_sdown[:, doc:doc + 1],
                                           x1_cur(doc, swl * 512, 512),
                                           ALU.mult, ALU.add)
                    V.memset(alpha[:, 7:8], 0.0)
                    V.memset(alpha[:, 7:8], 0.0).then_inc(sR2v.h, 1)
                prog.on("vector", _r2)
                sR2v.bump(1)
                promise(f"bank{b}", sR2v)
                promise(f"wd{sp}", sR2v)
                prog.wait("scalar", sR2v, sR2v.n)
                prog.wait("scalar", sLn2M, sLn2M.n)

                def _r2bf(Sc, doc=doc):
                    Sc.activation(out=gtmp(1), in_=r2_swp(doc), func=AF.Copy)
                    Sc.copy(arow[0:1, 24:25], c_cv[0:1, 0:1])
                    Sc.copy(arow[0:1, 24:25], c_cv[0:1, 0:1]) \
                        .then_inc(sR2v.h, 1)
                prog.on("scalar", _r2bf)
                sR2v.bump(1)
                prog.wait("vector", sR2v, sR2v.n)

                def _r2sq(V):
                    V.tensor_mul(utmp(1), gtmp(1), gtmp(1))
                    V.memset(alpha[:, 7:8], 0.0)
                    V.memset(alpha[:, 7:8], 0.0).then_inc(sR2v.h, 1)
                prog.on("vector", _r2sq)
                sR2v.bump(1)
                if doc == 0:
                    te_bank(4)
                    te_bank(5)
                prog.wait("tensor", sR2v, sR2v.n)

                def _ln2mm(T, doc=doc):
                    T.matmul(pb[4][0:1, :], c_ones_col_bf[:], gtmp(1),
                             start=(doc == 0), stop=(doc == KC - 1))
                    mm = T.matmul(pb[5][0:1, :], c_ones_col_bf[:], utmp(1),
                                  start=(doc == 0), stop=(doc == KC - 1))
                    mm.then_inc(sLn2M.h, 1)
                prog.on("tensor", _ln2mm)
                sLn2M.bump(1)
            prog.wait("vector", sLn2M, sLn2M.n)

            def _ln2r(V):
                V.tensor_scalar(rowB[0:1, 0:512], pb[4][0:1, :], 1.0 / 2048.0,
                                0.0, ALU.mult, ALU.add)
                V.tensor_mul(rowB[0:1, 512:1024], rowB[0:1, 0:512],
                             rowB[0:1, 0:512])
                V.scalar_tensor_tensor(rowB[0:1, 512:1024], pb[5][0:1, :],
                                       1.0 / 2048.0, rowB[0:1, 512:1024],
                                       ALU.mult, ALU.subtract)
                V.memset(alpha[:, 7:8], 0.0)
                V.memset(alpha[:, 7:8], 0.0).then_inc(sLn2R.h, 1)
            prog.on("vector", _ln2r)
            sLn2R.bump(1)
            promise("bank4", sLn2R)
            promise("bank5", sLn2R)
            prog.wait("scalar", sLn2R, sLn2R.n)

            def _ln2sq(Sc):
                Sc.activation(out=rowB[0:1, 512:1024], in_=rowB[0:1, 512:1024],
                              func=AF.Sqrt, bias=c_cv[0:1, 4:5])
                Sc.copy(arow[0:1, 24:25], c_cv[0:1, 0:1])
                Sc.copy(arow[0:1, 24:25], c_cv[0:1, 0:1]).then_inc(sLn2R.h, 1)
            prog.on("scalar", _ln2sq)
            sLn2R.bump(1)
            prog.wait("vector", sLn2R, sLn2R.n)

            def _ln2f(V):
                V.reciprocal(rowB[0:1, 512:1024], rowB[0:1, 512:1024])
                V.tensor_mul(rowB[0:1, 0:512], rowB[0:1, 0:512],
                             rowB[0:1, 512:1024])
                V.memset(alpha[:, 7:8], 0.0)
                V.memset(alpha[:, 7:8], 0.0).then_inc(sLn2R.h, 1)
            prog.on("vector", _ln2f)
            sLn2R.bump(1)
            for i, (src0, dstt) in enumerate([(512, p3_s2bc), (0, p3_m2bc)]):
                b = 4 + i
                te_bank(b)
                prog.wait("tensor", sLn2R, sLn2R.n)
                prog.on("tensor", lambda T, src0=src0, b=b:
                        T.matmul(pb[b][:, :], c_ones_row[0:1, :],
                                 rowB[0:1, src0:src0 + 512],
                                 start=True, stop=True).then_inc(sB3.h, 1))
                sB3.bump(1)
                prog.wait("vector", sB3, sB3.n)
                prog.on("vector", lambda V, dstt=dstt, b=b:
                        V.tensor_copy(dstt, pb[b][:, :]).then_inc(sB3.h, 1))
                sB3.bump(1)
                promise(f"bank{b}", sB3)
            for doc in range(KC):
                op = doc % 2
                prog.wait("vector", sB3, sB3.n)
                prog.wait("vector", sOut, max(0, sOut.n - 16))

                def _of(V, doc=doc, op=op):
                    V.tensor_sub(out_sl(op), r2_swp(doc), p3_m2bc)
                    V.tensor_mul(out_sl(op), out_sl(op), p3_s2bc)
                    V.memset(alpha[:, 7:8], 0.0)
                    V.memset(alpha[:, 7:8], 0.0).then_inc(sR2v.h, 1)
                prog.on("vector", _of)
                sR2v.bump(1)
                prog.wait("sync", sR2v, sR2v.n)
                sOut.bump(16)
                prog.on("sync", lambda E, doc=doc, sw=sw, op=op:
                        E.dma_start(out=yout[doc * P:(doc + 1) * P,
                                             sw * 512:(sw + 1) * 512],
                                    in_=out_sl(op)).then_inc(sOut.h, 16))

        prog.wait("sync", sOut, sOut.n)
        if debug:
            prog.wait("sync", sems["sDbg"], sems["sDbg"].n)
            prog.wait("sync", sems["sDbgM"], sems["sDbgM"].n)

        with nc.Block() as block:
            @block.sync
            def _(E):
                for fn in prog.streams["sync"]:
                    fn(E)

            @block.scalar
            def _(E):
                for fn in prog.streams["scalar"]:
                    fn(E)

            @block.vector
            def _(E):
                for fn in prog.streams["vector"]:
                    fn(E)

            @block.tensor
            def _(E):
                for fn in prog.streams["tensor"]:
                    fn(E)

            @block.gpsimd
            def _(E):
                for fn in prog.streams["gpsimd"]:
                    fn(E)

    return nc


def _consts():
    bf = ml_dtypes.bfloat16
    t = np.arange(P, dtype=np.float32)
    c = {}
    c["ident"] = np.eye(P, dtype=bf)
    c["diffT"] = (t[None, :] - t[:, None]).astype(bf)
    c["causal"] = (t[None, :] >= t[:, None]).astype(bf)
    c["iota_row"] = t[None, :].astype(bf)
    c["irev_col"] = (P - t)[:, None].copy()
    c["GIc"] = (t[:, None] + 128.0 * np.arange(H, dtype=np.float32)[None, :]).copy()
    c["J128"] = np.ascontiguousarray(
        np.broadcast_to(128.0 * np.arange(H, dtype=np.float32)[None, :],
                        (P, H)).astype(bf))
    c["ones_row_f"] = np.ones((1, P), np.float32)
    c["neg128_row_f"] = np.full((1, P), -128.0, np.float32)
    c["ones_col_bf"] = np.ones((P, 1), bf)
    c["ones_col_f"] = np.ones((P, 1), np.float32)
    cv = np.zeros((P, 8), np.float32)
    cv[:, 0] = -128.0
    cv[:, 1] = 1.0 / 2048.0
    cv[:, 2] = EPS_RMS
    cv[:, 3] = EPS_RMS * EPS_R1
    cv[:, 4] = EPS_LN
    c["cvals"] = cv
    return c


def _pack_pc(v, ncol):
    return np.ascontiguousarray(np.asarray(v, np.float32).reshape(ncol, P).T)


def kernel(L0_attn_x, L0_attn_qkv_w, L0_attn_qkv_norm_w, L0_attn_proj_w,
           L0_attn_proj_norm_w, L0_attn_norm_w, L0_attn_s,
           L0_ffn_wq_gate, L0_ffn_s_gate, L0_ffn_wq_up, L0_ffn_s_up,
           L0_ffn_wq_down, L0_ffn_s_down):
    x = np.asarray(L0_attn_x, dtype=np.float32)
    wq = np.asarray(L0_attn_qkv_w, dtype=np.float32)
    wp = np.asarray(L0_attn_proj_w, dtype=np.float32)
    wg = np.asarray(L0_ffn_wq_gate, dtype=np.int32)
    wu = np.asarray(L0_ffn_wq_up, dtype=np.int32)
    wd = np.asarray(L0_ffn_wq_down, dtype=np.int32)

    wqT = np.ascontiguousarray(wq.T)
    wpT = np.ascontiguousarray(wp.T)
    wgT = np.ascontiguousarray(
        wg.reshape(HOC, P, KC, P).transpose(0, 3, 2, 1).reshape(HID, D))
    wuT = np.ascontiguousarray(
        wu.reshape(HOC, P, KC, P).transpose(0, 3, 2, 1).reshape(HID, D))
    wdT = np.ascontiguousarray(
        wd.reshape(KC, P, HOC, P).transpose(0, 3, 2, 1).reshape(D, HID))

    shared = dict(
        wqT=wqT, wpT=wpT, wgT=wgT, wuT=wuT, wdT=wdT,
        wqn_pc=_pack_pc(L0_attn_qkv_norm_w, KC),
        w1_pc=_pack_pc(L0_attn_norm_w, KC),
        w2_pc=_pack_pc(L0_attn_proj_norm_w, KC),
        sdown_pc=_pack_pc(L0_ffn_s_down, KC),
        sgate_pc=_pack_pc(L0_ffn_s_gate, HOC),
        sup_pc=_pack_pc(L0_ffn_s_up, HOC),
        s_tiled=np.ascontiguousarray(
            np.broadcast_to(np.asarray(L0_attn_s, np.float32)[None, :], (P, H))),
        **_consts(),
    )

    in_maps = []
    for c in range(8):
        seq, half = c // 2, c % 2
        xs = x[seq, half * TOK:(half + 1) * TOK, :]
        m = dict(shared)
        m["xT"] = np.ascontiguousarray(xs.T)
        m["wq_scan"] = np.ascontiguousarray(wqT[:, c * 768:(c + 1) * 768])
        m["wp_scan"] = np.ascontiguousarray(wpT[:, c * 256:(c + 1) * 256])
        m["corr_gate"] = np.full((P, 1), 1.0 if half == 1 else 0.0, np.float32)
        oh = np.zeros((P, 8), np.float32)
        oh[:, c & ~1] = 1.0
        m["onehot"] = oh
        in_maps.append(m)

    nc = build_program(debug=False)
    res = run_bass_kernel_spmd(nc, in_maps, core_ids=list(range(8)))

    out = np.empty((4, 4096, D), np.float32)
    for c in range(8):
        seq, half = c // 2, c % 2
        yc = res.results[c]["y"]
        out[seq, half * TOK:(half + 1) * TOK, :] = yc.T
    return out
